# version 03
# baseline (speedup 1.0000x reference)
import contextlib
import math
import numpy as np
import ml_dtypes

import concourse.bass as bass
import concourse.mybir as mybir
from concourse.bass_utils import run_bass_kernel_spmd

F32 = mybir.dt.float32
BF16 = mybir.dt.bfloat16
AF = mybir.ActivationFunctionType
ALU = mybir.AluOpType
AX = mybir.AxisListType

NC = 8
D = 1024
L = 16384
TPC = 4096
HALO = 32
W = TPC + 2 * HALO
TILES = [512] * 8 + [64]
C23 = 1.5 * 2.0**23
C52 = 1.5 * 2.0**52
EPS_FLAG = 2.0**-40
IH = 3072


def split_const(x):
    hi = np.float32(x)
    lo = np.float32(np.float64(x) - np.float64(hi))
    return float(hi), float(lo)



class Sched:
    ENGS = ("y", "g", "v", "s", "t")

    def __init__(self, nc):
        self.nc = nc
        self.ops = {e: [] for e in self.ENGS}
        self.n = {}
        self.owner = {}

    def op(self, e, fn, after=(), dma=False, ch=None, auto=None):
        if ch is None:
            assert not dma, "DMA ops must name a channel"
            ch = e
        inc = 16 if dma else 1
        waits = [a for a in after if a is not None]
        if auto is None:
            auto = e in ("v", "s") and not dma
        if auto and self.n.get(e, 0) > 0:
            waits.append((e, self.n[e]))
        self.n[ch] = self.n.get(ch, 0) + inc
        self.owner[ch] = e
        self.ops[e].append((fn, waits, inc, ch))
        return (ch, self.n[ch])

    def build(self):
        nc = self.nc
        chans = list(self.n.keys())
        with contextlib.ExitStack() as es:
            sems = {c: es.enter_context(nc.semaphore(f"sem_{c}"))
                    for c in chans}
            with nc.Block() as block:
                deco = {"y": block.sync, "g": block.gpsimd, "v": block.vector,
                        "s": block.scalar, "t": block.tensor}
                for e in self.ENGS:
                    if not self.ops[e]:
                        continue

                    def body(eng, e=e):
                        for fn, waits, inc, ch in self.ops[e]:
                            for (wc_ch, wc) in waits:
                                if wc_ch in ("cst", "cstg", "cstg2"):
                                    wc = self.n[wc_ch]
                                eng.wait_ge(sems[wc_ch], wc)
                            fn(eng).then_inc(sems[ch], inc)
                        for c in chans:
                            if self.owner[c] == e:
                                eng.wait_ge(sems[c], self.n[c])
                    deco[e](body)


def alloc(nc, es, name, shape, dt=F32):
    return es.enter_context(nc.sbuf_tensor(name, shape, dt))


def emit_flag_g(S, one_sb, g_sb, ev_one):
    ev = S.op("v", lambda e: e.tensor_scalar(
        g_sb[:], one_sb[:], EPS_FLAG, 1.0, ALU.add, ALU.subtract),
        after=[ev_one])
    return S.op("v", lambda e: e.tensor_scalar(
        g_sb[:], g_sb[:], -(2.0**40), 1.0, ALU.mult, ALU.add), after=[ev])


def emit_round(S, out, tin, g_sb, tmpa, tmpb, after=()):
    eva = S.op("v", lambda e: e.tensor_scalar(
        tmpa, tin, C23, C23, ALU.add, ALU.subtract), after=after)
    evb = S.op("v", lambda e: e.tensor_scalar(
        tmpb, tin, C52, C52, ALU.add, ALU.subtract), after=after)
    return S.op("v", lambda e: e.scalar_tensor_tensor(
        out, tmpa, g_sb[:], tmpb, ALU.mult, ALU.add), after=[eva, evb])


def ap3(handle, pitch, part, d1, d2, off=0):
    (d1s, d1n), (d2s, d2n) = d1, d2
    return bass.AP(handle if not hasattr(handle, "tensor") else handle.tensor,
                   off, [[pitch, part], [d1s, d1n], [d2s, d2n]])



def build_A():
    nc = bass.Bass("TRN2", target_bir_lowering=False)
    xt = nc.dram_tensor("xt", [D, W], F32, kind="ExternalInput")
    winv = nc.dram_tensor("winv", [D, D], F32, kind="ExternalInput")
    winx2 = nc.dram_tensor("winx2", [D, D], F32, kind="ExternalInput")
    bv = nc.dram_tensor("bv", [128, 8], F32, kind="ExternalInput")
    bx2 = nc.dram_tensor("bx2", [128, 8], F32, kind="ExternalInput")
    swc = nc.dram_tensor("swc", [128, 24], F32, kind="ExternalInput")
    sbc = nc.dram_tensor("sbc", [128, 8], F32, kind="ExternalInput")
    edges = nc.dram_tensor("edges", [128, 16], F32, kind="ExternalInput")
    one128 = nc.dram_tensor("one128", [128, 1], F32, kind="ExternalInput")
    wg_s = nc.dram_tensor("wg_s", [IH // NC, D], F32, kind="ExternalInput")
    wu_s = nc.dram_tensor("wu_s", [IH // NC, D], F32, kind="ExternalInput")
    wd_s = nc.dram_tensor("wd_s", [D // NC, IH], F32, kind="ExternalInput")

    v_out = nc.dram_tensor("v_out", [D, TPC], BF16, kind="ExternalOutput")
    x2_out = nc.dram_tensor("x2_out", [D, W], BF16, kind="ExternalOutput")
    wgq = nc.dram_tensor("wgq", [IH // NC, D], BF16, kind="ExternalOutput")
    wuq = nc.dram_tensor("wuq", [IH // NC, D], BF16, kind="ExternalOutput")
    wdq = nc.dram_tensor("wdq", [D // NC, IH], BF16, kind="ExternalOutput")
    vraw_h = nc.dram_tensor("vraw_h", [D, W], BF16)

    es = contextlib.ExitStack()
    with es:
        S = Sched(nc)
        wv_sb = alloc(nc, es, "wv", [128, 8 * 8 * 128], BF16)
        wx_sb = alloc(nc, es, "wx", [128, 8 * 8 * 128], BF16)
        ones_sb = alloc(nc, es, "ones", [128, 128], BF16)
        bv_sb = alloc(nc, es, "bvs", [128, 8], F32)
        bx_sb = alloc(nc, es, "bxs", [128, 8], F32)
        swc_sb = alloc(nc, es, "swcs", [128, 8 * 3], F32)
        sbc_sb = alloc(nc, es, "sbcs", [128, 8], F32)
        edges_sb = alloc(nc, es, "edg", [128, 8 * 2], F32)
        one_sb = alloc(nc, es, "one", [128, 1], F32)
        g_sb = alloc(nc, es, "gflag", [128, 1], F32)

        xf = [alloc(nc, es, f"xf{i}", [128, 8 * 512], F32) for i in range(2)]
        xbf = [alloc(nc, es, f"xbf{i}", [128, 8 * 512], BF16) for i in range(2)]
        xsq_t = alloc(nc, es, "xsq", [128, 8 * 512], BF16)
        mu = [alloc(nc, es, f"mu{i}", [128, 512], F32) for i in range(2)]
        rb = [alloc(nc, es, f"rb{i}", [128, 512], F32) for i in range(2)]
        tstat = alloc(nc, es, "tstat", [128, 512], F32)
        tstat2 = alloc(nc, es, "tstat2", [128, 512], F32)
        xln = [alloc(nc, es, f"xln{i}", [128, 8 * 512], BF16) for i in range(2)]
        x2sb = [alloc(nc, es, f"x2sb{i}", [128, 8 * 512], BF16) for i in range(2)]
        vsb = [alloc(nc, es, f"vsb{i}", [128, 8 * 512], BF16) for i in range(2)]
        vin = [alloc(nc, es, f"vin{i}", [128, 8 * 514], BF16) for i in range(2)]
        cv1 = alloc(nc, es, "cv1", [128, 512], F32)
        cv2 = alloc(nc, es, "cv2", [128, 512], F32)
        vch = alloc(nc, es, "vch", [128, 8 * 512], BF16)
        sred = alloc(nc, es, "sred", [128, 32], F32)
        rs_ = alloc(nc, es, "rs_", [128, 32], F32)
        wqf = lambda: xf[0][:, 0:1024]
        qa = lambda: xf[0][:, 1024:2048]
        qb = lambda: xf[0][:, 2048:3072]
        qt = lambda: xf[1][:, 0:1024]
        qo = lambda: xln[0][:, 0:1024]

        with (nc.psum_tensor("ps_sum", [128, 512], F32) as ps_sum,
              nc.psum_tensor("ps_sq", [128, 512], F32) as ps_sq,
              nc.psum_tensor("pm0", [128, 512], F32) as pm0,
              nc.psum_tensor("pm1", [128, 512], F32) as pm1):

            ev_wv = S.op("g", lambda e: e.dma_start(
                out=ap3(wv_sb, 8192, 128, (1024, 8), (1, 1024)),
                in_=bass.AP(winv, 0, [[D, 128], [128 * D, 8], [1, 1024]])),
                dma=True, ch="cstg")
            ev_wx = S.op("g", lambda e: e.dma_start(
                out=ap3(wx_sb, 8192, 128, (1024, 8), (1, 1024)),
                in_=bass.AP(winx2, 0, [[D, 128], [128 * D, 8], [1, 1024]])),
                dma=True, ch="cstg")
            ev_on = S.op("g", lambda e: e.memset(ones_sb[:], 1.0))
            ev_bv = S.op("y", lambda e: e.dma_start(
                out=bv_sb[:], in_=bv.ap()), dma=True, ch="cst")
            ev_bx = S.op("y", lambda e: e.dma_start(
                out=bx_sb[:], in_=bx2.ap()), dma=True, ch="cst")
            ev_sw = S.op("y", lambda e: e.dma_start(
                out=swc_sb[:], in_=swc.ap()), dma=True, ch="cst")
            ev_sb = S.op("y", lambda e: e.dma_start(
                out=sbc_sb[:], in_=sbc.ap()), dma=True, ch="cst")
            ev_ed = S.op("y", lambda e: e.dma_start(
                out=edges_sb[:], in_=edges.ap()), dma=True, ch="cst")
            ev_1 = S.op("y", lambda e: e.dma_start(out=one_sb[:],
                                                   in_=one128.ap()), dma=True, ch="cst")
            ev_cst = ("cst", S.n["cst"])
            ev_cstg = ("cstg", S.n["cstg"])
            ev_wv = ev_wx = ev_cstg
            ev_bv = ev_bx = ev_sw = ev_sb = ev_ed = ev_cst
            ev_g = emit_flag_g(S, one_sb, g_sb, ev_cst)

            ev_inp = {0: None, 1: None}
            ev_evac = {0: None, 1: None}
            ev_x2st = {0: None, 1: None}
            ev_vst = {0: None, 1: None}
            ev_stats_rd = None
            ev_stats_sqrt = None
            ev_sqmm_prev = None
            col = 0
            for ti, cw in enumerate(TILES):
                bi = ti % 2
                c0 = col
                col += cw
                ev_ld = S.op("y", lambda e, c0=c0, cw=cw, bi=bi: e.dma_start(
                    out=ap3(xf[bi], 8 * 512, 128, (cw, 8), (1, cw)),
                    in_=bass.AP(xt, c0, [[W, 128], [128 * W, 8], [1, cw]])),
                    after=[ev_inp[bi]], dma=True, ch=f"ld{bi}")
                ev_cast = S.op("v", lambda e, cw=cw, bi=bi: e.tensor_copy(
                    xbf[bi][:, 0:8 * cw], xf[bi][:, 0:8 * cw]), after=[ev_ld])
                ev_sq = S.op("s", lambda e, cw=cw, bi=bi: e.activation(
                    xsq_t[:, 0:8 * cw], xbf[bi][:, 0:8 * cw], AF.Square),
                    after=[ev_cast, ev_sqmm_prev])
                ev_mm = None
                for k in range(8):
                    aft = [ev_cast, ev_on, ev_stats_rd] if k == 0 else []
                    ev_mm = S.op("t", lambda e, k=k, cw=cw, bi=bi: e.matmul(
                        ps_sum[:, 0:cw], ones_sb[:],
                        xbf[bi][:, k * cw:(k + 1) * cw],
                        start=(k == 0), stop=(k == 7)), after=aft)
                for k in range(8):
                    aft = [ev_sq] if k == 0 else []
                    ev_mm = S.op("t", lambda e, k=k, cw=cw: e.matmul(
                        ps_sq[:, 0:cw], ones_sb[:],
                        xsq_t[:, k * cw:(k + 1) * cw],
                        start=(k == 0), stop=(k == 7)), after=aft)
                ev_sqmm_prev = ev_mm
                ev1 = S.op("v", lambda e, cw=cw, bi=bi: e.tensor_scalar(
                    mu[bi][:, 0:cw], ps_sum[:, 0:cw], 1.0 / D, None, ALU.mult),
                    after=[ev_mm])
                ev2 = S.op("v", lambda e, cw=cw: e.tensor_scalar(
                    tstat[:, 0:cw], ps_sq[:, 0:cw], 1.0 / D, 1e-5,
                    ALU.mult, ALU.add), after=[ev_stats_sqrt])
                ev3 = S.op("v", lambda e, cw=cw, bi=bi: e.tensor_tensor(
                    tstat2[:, 0:cw], mu[bi][:, 0:cw], mu[bi][:, 0:cw],
                    ALU.mult))
                ev4 = S.op("v", lambda e, cw=cw: e.tensor_tensor(
                    tstat[:, 0:cw], tstat[:, 0:cw], tstat2[:, 0:cw],
                    ALU.subtract))
                ev_stats_rd = ev2
                ev_stats_sqrt = S.op("s", lambda e, cw=cw: e.activation(
                    tstat2[:, 0:cw], tstat[:, 0:cw], AF.Sqrt), after=[ev4])
                ev6 = S.op("v", lambda e, cw=cw, bi=bi: e.reciprocal(
                    rb[bi][:, 0:cw], tstat2[:, 0:cw]), after=[ev_stats_sqrt])
                ev7 = S.op("v", lambda e, cw=cw, bi=bi: e.tensor_tensor(
                    xf[bi][:, 0:8 * cw], xf[bi][:, 0:8 * cw],
                    bass.AP(mu[bi], 0, [[512, 128], [0, 8], [1, cw]]),
                    ALU.subtract))
                ev8 = S.op("v", lambda e, cw=cw, bi=bi: e.tensor_tensor(
                    xln[bi][:, 0:8 * cw], xf[bi][:, 0:8 * cw],
                    bass.AP(rb[bi], 0, [[512, 128], [0, 8], [1, cw]]),
                    ALU.mult))
                last_inp = None
                ev_x2 = []
                ev_v = []
                for m in range(16):
                    isv = m < 8
                    wsb = wv_sb if isv else wx_sb
                    mm_ = m if isv else m - 8
                    pbank = pm0 if (m % 2 == 0) else pm1
                    evk = None
                    for k in range(8):
                        aft = ([ev8, ev_wv if isv else ev_wx, ev_evac[m % 2]]
                               if k == 0 else [])
                        evk = S.op("t", lambda e, k=k, mm_=mm_, cw=cw, bi=bi,
                                   wsb=wsb, pbank=pbank: e.matmul(
                            pbank[:, 0:cw],
                            wsb[:, (k * 8 + mm_) * 128:(k * 8 + mm_ + 1) * 128],
                            xln[bi][:, k * cw:(k + 1) * cw],
                            start=(k == 0), stop=(k == 7)), after=aft)
                    if isv:
                        def dst(mm_=mm_, cw=cw, bi=bi):
                            return vsb[bi][:, mm_ * cw:(mm_ + 1) * cw]
                        bias_sb, ev_b, ev_st = bv_sb, ev_bv, ev_vst[bi]
                    else:
                        def dst(mm_=mm_, cw=cw, bi=bi):
                            return x2sb[bi][:, mm_ * cw:(mm_ + 1) * cw]
                        bias_sb, ev_b, ev_st = bx_sb, ev_bx, ev_x2st[bi]
                    ev_evac[m % 2] = S.op("s", lambda e, dst=dst, mm_=mm_,
                                          cw=cw, pbank=pbank,
                                          bias_sb=bias_sb: e.activation(
                        dst(), pbank[:, 0:cw], AF.Identity,
                        bias=bias_sb[:, mm_:mm_ + 1]), after=[evk, ev_b, ev_st])
                    last_inp = ev_evac[m % 2]
                    (ev_v if isv else ev_x2).append(ev_evac[m % 2])
                ev_inp[bi] = last_inp
                ev_x2st[bi] = S.op("g", lambda e, c0=c0, cw=cw, bi=bi:
                                   e.dma_start(
                    out=bass.AP(x2_out, c0, [[W, 128], [128 * W, 8], [1, cw]]),
                    in_=ap3(x2sb[bi], 8 * 512, 128, (cw, 8), (1, cw))),
                    after=ev_x2, dma=True, ch=f"x2{bi}")
                ev_vst[bi] = S.op("g", lambda e, c0=c0, cw=cw, bi=bi:
                                  e.dma_start(
                    out=bass.AP(vraw_h, c0, [[W, 128], [128 * W, 8], [1, cw]]),
                    in_=ap3(vsb[bi], 8 * 512, 128, (cw, 8), (1, cw))),
                    after=ev_v, dma=True, ch=f"vs{bi}")
            ev_pass1 = [ev_vst[0], ev_vst[1], ev_x2st[0], ev_x2st[1],
                        ev_inp[0], ev_inp[1]]

            ev_vld = {0: None, 1: None}
            ev_conv_rd = {0: None, 1: None}
            ev_cv_rd = None
            ev_vch_st = None
            for ti in range(8):
                bi = ti % 2
                o = HALO + ti * 512
                ev_vld[bi] = S.op("y", lambda e, o=o, bi=bi: e.dma_start(
                    out=ap3(vin[bi], 8 * 514, 128, (514, 8), (1, 514)),
                    in_=bass.AP(vraw_h, o - 1, [[W, 128], [128 * W, 8],
                                                [1, 514]])),
                    after=ev_pass1 + [ev_conv_rd[bi]], dma=True, ch=f"cl{bi}")
                evm = [ev_vld[bi], ev_ed, ev_sw]
                if ti == 0:
                    evm0 = S.op("v", lambda e: e.tensor_scalar(
                        bass.AP(vin[0], 0, [[8 * 514, 128], [514, 8], [1, 1]]),
                        bass.AP(vin[0], 0, [[8 * 514, 128], [514, 8], [1, 1]]),
                        edges_sb[:, 0:1], None, ALU.mult), after=evm)
                    evm = [evm0, ev_sw]
                if ti == 7:
                    evm0 = S.op("v", lambda e: e.tensor_scalar(
                        bass.AP(vin[1], 513, [[8 * 514, 128], [514, 8],
                                              [1, 1]]),
                        bass.AP(vin[1], 513, [[8 * 514, 128], [514, 8],
                                              [1, 1]]),
                        edges_sb[:, 1:2], None, ALU.mult), after=evm)
                    evm = [evm0, ev_sw]
                last_e4 = None
                for ch in range(8):
                    co = ch * 514
                    e1 = S.op("v", lambda e, co=co, bi=bi, ch=ch: e.tensor_scalar(
                        cv1[:], vin[bi][:, co + 1:co + 513],
                        swc_sb[:, ch * 3 + 1:ch * 3 + 2], None, ALU.mult),
                        after=evm + [ev_cv_rd])
                    e2 = S.op("v", lambda e, co=co, bi=bi, ch=ch:
                              e.scalar_tensor_tensor(
                        cv2[:], vin[bi][:, co:co + 512],
                        swc_sb[:, ch * 3 + 0:ch * 3 + 1], cv1[:],
                        ALU.mult, ALU.add))
                    e3 = S.op("v", lambda e, co=co, bi=bi, ch=ch:
                              e.scalar_tensor_tensor(
                        cv1[:], vin[bi][:, co + 2:co + 514],
                        swc_sb[:, ch * 3 + 2:ch * 3 + 3], cv2[:],
                        ALU.mult, ALU.add))
                    e4 = S.op("s", lambda e, ch=ch: e.activation(
                        vch[:, ch * 512:(ch + 1) * 512], cv1[:], AF.Identity,
                        bias=sbc_sb[:, ch:ch + 1]),
                        after=[e3, ev_sb, ev_vch_st])
                    ev_cv_rd = e4
                    last_e4 = e4
                ev_conv_rd[bi] = last_e4
                ev_vch_st = S.op("g", lambda e, ti=ti: e.dma_start(
                    out=bass.AP(v_out, ti * 512, [[TPC, 128], [128 * TPC, 8],
                                                  [1, 512]]),
                    in_=ap3(vch, 8 * 512, 128, (512, 8), (1, 512))),
                    after=[last_e4], dma=True, ch="vch")

            SEG = 1024

            def lpbq(src_t, dst_t, nrow, ncol, after):
                evs = list(after)
                for chunk in range(nrow // 128):
                    for s0 in range(0, ncol, SEG):
                        off = chunk * 128 * ncol + s0
                        nb = SEG // 32
                        evl = S.op("g", lambda e, off=off, ncol=ncol:
                                   e.dma_start(
                            out=wqf(),
                            in_=bass.AP(src_t, off, [[ncol, 128], [1, SEG]])),
                            after=evs, dma=True, ch="lp")
                        ev = S.op("v", lambda e: e.tensor_reduce(
                            sred[:], bass.AP(xf[0], 0, [[8 * 512, 128],
                                                        [32, 32], [1, 32]]),
                            AX.X, ALU.max, apply_absolute_value=True),
                            after=[evl])
                        ev = S.op("v", lambda e: e.tensor_scalar(
                            sred[:], sred[:], 1.0 / 7.0, 1e-12,
                            ALU.mult, ALU.max))
                        ev = S.op("v", lambda e: e.reciprocal(rs_[:], sred[:]))
                        ev = S.op("v", lambda e: e.tensor_tensor(
                            qt(), wqf(),
                            bass.AP(rs_, 0, [[32, 128], [1, 32], [0, 32]]),
                            ALU.mult))
                        ev = emit_round(S, qt(), qt(), g_sb, qa(), qb(),
                                        after=[ev, ev_g])
                        ev = S.op("v", lambda e: e.tensor_tensor(
                            qo(), qt(),
                            bass.AP(sred, 0, [[32, 128], [1, 32], [0, 32]]),
                            ALU.mult), after=[ev])
                        evst = S.op("g", lambda e, off=off, ncol=ncol:
                                    e.dma_start(
                            out=bass.AP(dst_t, off, [[ncol, 128], [1, SEG]]),
                            in_=qo()), after=[ev], dma=True, ch="lp")
                        evs = [evst]
                return evs[0]

            ev_all = ev_pass1 + [ev_vch_st, ev_cv_rd, ev_g]
            e_g = lpbq(wg_s, wgq, IH // NC, D, ev_all)
            e_u = lpbq(wu_s, wuq, IH // NC, D, [e_g])
            e_d = lpbq(wd_s, wdq, D // NC, IH, [e_u])

            S.build()
    return nc



def build_B():
    nc = bass.Bass("TRN2", target_bir_lowering=False)
    v_in = nc.dram_tensor("v_in", [128, 2 * L], BF16, kind="ExternalInput")
    fw1 = nc.dram_tensor("fw1", [1, 64], F32, kind="ExternalInput")
    fb1 = nc.dram_tensor("fb1", [64, 1], F32, kind="ExternalInput")
    fw2 = nc.dram_tensor("fw2", [64, 128], F32, kind="ExternalInput")
    fb2 = nc.dram_tensor("fb2", [128, 1], F32, kind="ExternalInput")
    fw3c = nc.dram_tensor("fw3c", [128, 128], F32, kind="ExternalInput")
    fb3c = nc.dram_tensor("fb3c", [128, 1], F32, kind="ExternalInput")
    posn = nc.dram_tensor("posn", [1, L], F32, kind="ExternalInput")
    E = {}
    for nm, cols in [("ec256", 256), ("es256", 256), ("ec128", 128),
                     ("es128", 128), ("ectw", 256), ("estw", 256),
                     ("ecitw", 256), ("esitw", 256), ("ecc256", 256),
                     ("esc256", 256)]:
        E[nm] = nc.dram_tensor(nm, [128, cols], F32, kind="ExternalInput")
    conv_out = nc.dram_tensor("conv_out", [128, 2 * L], BF16,
                              kind="ExternalOutput")
    h_hbm = nc.dram_tensor("h_hbm", [128, L], BF16)

    TWO_PI = 2.0 * math.pi
    es = contextlib.ExitStack()
    with es:
        S = Sched(nc)
        T = {nm: alloc(nc, es, "t_" + nm, [128, cols], F32)
             for nm, cols in [("cA512", 512), ("cB512", 512),
                              ("f128re", 128), ("f128im", 128),
                              ("nf128im", 128),
                              ("cA256", 256), ("cB256", 256),
                              ("twre", 256), ("twim", 256),
                              ("itwre", 256), ("itwim", 256),
                              ("c256re", 256), ("c256im", 256),
                              ("nc256im", 256)]}
        tgen = alloc(nc, es, "tgen", [128, 256], F32)
        tgen2 = alloc(nc, es, "tgen2", [128, 256], F32)
        fw1_sb = alloc(nc, es, "fw1s", [1, 64], F32)
        fb1_sb = alloc(nc, es, "fb1s", [64, 1], F32)
        fw2_sb = alloc(nc, es, "fw2s", [64, 128], F32)
        fb2_sb = alloc(nc, es, "fb2s", [128, 1], F32)
        fw3_sb = alloc(nc, es, "fw3s", [128, 128], F32)
        fb3_sb = alloc(nc, es, "fb3s", [128, 1], F32)
        pk = [alloc(nc, es, f"pk{i}", [1, 512], F32) for i in range(2)]
        pb = [alloc(nc, es, f"pb{i}", [1, 512], F32) for i in range(2)]
        h1t = alloc(nc, es, "h1t", [64, 512], F32)
        h2t = alloc(nc, es, "h2t", [128, 512], F32)
        hot = [alloc(nc, es, f"hot{i}", [128, 512], BF16) for i in range(2)]
        hz = [alloc(nc, es, f"hz{i}", [128, 128], F32) for i in range(4)]
        zre = [alloc(nc, es, f"zre{i}", [128, 128], F32) for i in range(4)]
        zim = [alloc(nc, es, f"zim{i}", [128, 128], F32) for i in range(4)]
        s2re = alloc(nc, es, "s2re", [128, 1024], F32)
        s2im = alloc(nc, es, "s2im", [128, 1024], F32)
        hre = alloc(nc, es, "hre", [128, 1024], F32)
        him = alloc(nc, es, "him", [128, 1024], F32)
        xre = alloc(nc, es, "xre", [128, 1024], F32)
        xim = alloc(nc, es, "xim", [128, 1024], F32)
        pre = alloc(nc, es, "pre", [128, 1024], F32)
        pim = alloc(nc, es, "pim", [128, 1024], F32)
        t2 = {(pl, r): alloc(nc, es, f"t2{pl}{r}", [128, 512], F32)
              for pl in "ri" for r in range(2)}
        t2p = {(pl, r): alloc(nc, es, f"t2p{pl}{r}", [128, 512], F32)
               for pl in "ri" for r in range(2)}
        zo = [alloc(nc, es, f"zo{i}", [128, 512], BF16) for i in range(2)]
        tmp1 = alloc(nc, es, "tmp1", [128, 1024], F32)
        tmp2 = alloc(nc, es, "tmp2", [128, 1024], F32)
        tmp3 = alloc(nc, es, "tmp3", [128, 1024], F32)
        tmp4 = alloc(nc, es, "tmp4", [128, 1024], F32)

        with (nc.psum_tensor("pA0", [128, 512], F32) as pA0,
              nc.psum_tensor("pA1", [128, 512], F32) as pA1,
              nc.psum_tensor("pA2", [128, 512], F32) as pA2,
              nc.psum_tensor("pA3", [128, 512], F32) as pA3,
              nc.psum_tensor("pB0", [128, 512], F32) as pB0,
              nc.psum_tensor("pB1", [128, 512], F32) as pB1,
              nc.psum_tensor("pB2", [128, 512], F32) as pB2,
              nc.psum_tensor("pB3", [128, 512], F32) as pB3):
            pA = [pA0, pA1, pA2, pA3]
            pB = [pB0, pB1, pB2, pB3]

            ev_e = {}
            for nm in E:
                ev_e[nm] = S.op("y", lambda e, nm=nm: e.dma_start(
                    out=tgen[:, 0:E[nm].shape[1]] if False else None,
                    in_=None), dma=True, ch="cst") if False else None
            def gen_table(enm, dst, off, period_scale, extra_scale, after):
                cols = E[enm].shape[1]
                chi, clo = split_const(TWO_PI * period_scale)
                ev = S.op("y", lambda e: e.dma_start(
                    out=tgen[:, 0:cols], in_=E[enm].ap()),
                    dma=True, ch="tg", after=after)
                ev = S.op("v", lambda e: e.tensor_scalar(
                    tgen2[:, 0:cols], tgen[:, 0:cols], chi, None, ALU.mult),
                    after=[ev])
                ev = S.op("v", lambda e: e.scalar_tensor_tensor(
                    tgen2[:, 0:cols], tgen[:, 0:cols], clo, tgen2[:, 0:cols],
                    ALU.mult, ALU.add))
                ev = S.op("s", lambda e: e.activation(
                    tgen2[:, 0:cols], tgen2[:, 0:cols], AF.Sin), after=[ev])
                ev = S.op("v", lambda e: e.tensor_scalar(
                    dst[:, off:off + cols], tgen2[:, 0:cols], extra_scale,
                    None, ALU.mult), after=[ev])
                return ev

            ev = None
            for enm, dnm, off, psc, xsc in [
                    ("ec256", "cA512", 0, 1.0 / 256, 1.0),
                    ("es256", "cA512", 256, 1.0 / 256, 1.0),
                    ("es256", "cB512", 0, 1.0 / 256, -1.0),
                    ("ec256", "cB512", 256, 1.0 / 256, 1.0),
                    ("ec128", "f128re", 0, 1.0 / 128, 1.0),
                    ("es128", "f128im", 0, 1.0 / 128, 1.0),
                    ("es128", "nf128im", 0, 1.0 / 128, -1.0),
                    ("ec128", "cA256", 0, 1.0 / 128, 1.0),
                    ("es128", "cA256", 128, 1.0 / 128, -1.0),
                    ("es128", "cB256", 0, 1.0 / 128, 1.0),
                    ("ec128", "cB256", 128, 1.0 / 128, 1.0),
                    ("ectw", "twre", 0, 1.0 / 32768, 1.0),
                    ("estw", "twim", 0, 1.0 / 32768, 1.0),
                    ("ecitw", "itwre", 0, 1.0 / 32768, 2.0**-15),
                    ("esitw", "itwim", 0, 1.0 / 32768, -(2.0**-15)),
                    ("ecc256", "c256re", 0, 1.0 / 256, 1.0),
                    ("esc256", "c256im", 0, 1.0 / 256, -1.0),
                    ("esc256", "nc256im", 0, 1.0 / 256, 1.0)]:
                ev = gen_table(enm, T[dnm], off, psc, xsc, [ev])
            ev_tables = ev

            chi, clo = split_const(1.0 / (L - 1))
            ev_w = []
            for src_t, dst_t in [(fw1, fw1_sb), (fw2, fw2_sb), (fw3c, fw3_sb)]:
                ev_w.append(S.op("g", lambda e, s=src_t, d=dst_t: e.dma_start(
                    out=d[:], in_=s.ap()), dma=True, ch="cstg"))
            for src_t, dst_t in [(fb1, fb1_sb), (fb2, fb2_sb), (fb3c, fb3_sb)]:
                ev_w.append(S.op("y", lambda e, s=src_t, d=dst_t: e.dma_start(
                    out=d[:], in_=s.ap()), dma=True, ch="cst"))
            ev_cst = ("cst", S.n["cst"])
            ev_cstg = ("cstg", S.n["cstg"])
            ev_hst = {0: None, 1: None}
            ev_pbu = {0: None, 1: None}
            ev_pkrd = {0: None, 1: None}
            for ti in range(32):
                c0 = ti * 512
                bi = ti % 2
                evk = S.op("y", lambda e, c0=c0, bi=bi: e.dma_start(
                    out=pk[bi][:], in_=bass.AP(posn, c0, [[L, 1], [1, 512]])),
                    dma=True, ch=f"pk{bi}", after=[ev_pkrd[bi]])
                eva = S.op("v", lambda e, bi=bi: e.tensor_scalar(
                    pb[bi][:], pk[bi][:], chi, None, ALU.mult),
                    after=[evk, ev_pbu[bi]])
                evb = S.op("v", lambda e, bi=bi: e.scalar_tensor_tensor(
                    pb[bi][:], pk[bi][:], clo, pb[bi][:], ALU.mult, ALU.add))
                ev_pkrd[bi] = evb
                evm = S.op("t", lambda e, bi=bi: e.matmul(
                    pA[0][:, 0:512][0:64, :], fw1_sb[:],
                    pb[bi][:], start=True, stop=True),
                    after=[evb, ev_cstg, ev_cst, ev_h2rd, ev_hmmrd]
                    if ti > 0 else [evb, ev_cstg, ev_cst])
                ev_pbu[bi] = evm
                evg = S.op("s", lambda e: e.activation(
                    h1t[:], pA[0][0:64, 0:512], AF.Gelu, bias=fb1_sb[:]),
                    after=[evm, ev_cst])
                evm2 = S.op("t", lambda e: e.matmul(
                    pA[1][:, 0:512], fw2_sb[:], h1t[:], start=True,
                    stop=True), after=[evg])
                evg2 = S.op("s", lambda e: e.activation(
                    h2t[:], pA[1][:, 0:512], AF.Gelu, bias=fb2_sb[:]),
                    after=[evm2])
                ev_h2rd = evg2
                evm3 = S.op("t", lambda e: e.matmul(
                    pA[2][:, 0:512], fw3_sb[:], h2t[:], start=True,
                    stop=True), after=[evg2])
                ev_hmmrd = evm3
                evg3 = S.op("s", lambda e, bi=bi: e.activation(
                    hot[bi][:], pA[2][:, 0:512], AF.Identity,
                    bias=fb3_sb[:]), after=[evm3, ev_hst[bi]])
                ev_hst[bi] = S.op("g", lambda e, c0=c0, bi=bi: e.dma_start(
                    out=bass.AP(h_hbm, c0, [[L, 128], [1, 512]]),
                    in_=hot[bi][:]), after=[evg3], dma=True, ch=f"hs{bi}")
            ev_hdone = [ev_hst[0], ev_hst[1]]
            ev_mlp_last_read = evg3

            ev_cplx_prev = [None, None]

            def cplx6(outr, outi, ar, ai, br_ap, bi_ap, after, n=1024):
                t1 = tmp1[:, 0:n]
                t2_ = tmp2[:, 0:n]
                t3 = tmp3[:, 0:n]
                t4 = tmp4[:, 0:n]
                pv, pg = ev_cplx_prev
                e1 = S.op("v", lambda e: e.tensor_tensor(
                    t1, ar, br_ap, ALU.mult), after=after)
                e2 = S.op("g", lambda e: e.tensor_tensor(
                    t2_, ai, bi_ap, ALU.mult), after=after + [pv])
                e3 = S.op("v", lambda e: e.tensor_tensor(
                    t3, ar, bi_ap, ALU.mult), after=[pg])
                e4 = S.op("g", lambda e: e.tensor_tensor(
                    t4, ai, br_ap, ALU.mult), after=after + [pg])
                e5 = S.op("v", lambda e: e.tensor_tensor(
                    outr, t1, t2_, ALU.subtract), after=[e2])
                e6 = S.op("g", lambda e: e.tensor_tensor(
                    outi, t3, t4, ALU.add), after=[e2, e3, e4])
                ev_cplx_prev[0] = e5
                ev_cplx_prev[1] = e6
                return [e5, e6]

            def bc(tile, cols):
                return bass.AP(tile, 0, [[tile.shape[1], 128], [0, 4],
                                         [1, cols]])

            ev_zl = {}
            ev_zuse = {}
            ev_evA = ev_mlp_last_read
            ev_evB = None
            ev_prod = []
            ev_t2rd = []
            ev_t2prd = None
            ev_twrd = []
            ev_s2rd = None
            ev_pird = None
            ev_zodma = {0: None, 1: None}
            for g in range(32):
                cbase = g * 4
                for c in range(4):
                    ev_zl[("h", c)] = S.op("g", lambda e, c=c, cbase=cbase: e.dma_start(
                        out=hz[c][:],
                        in_=bass.AP(h_hbm, (cbase + c) * L,
                                    [[128, 128], [1, 128]])),
                        after=ev_hdone + [ev_zuse.get(("h", c))],
                        dma=True, ch=f"zh{c}")
                    ev_zl[("re", c)] = S.op("g", lambda e, c=c, cbase=cbase: e.dma_start(
                        out=zre[c][:],
                        in_=bass.AP(v_in, (cbase + c) * 2 * L,
                                    [[128, 128], [1, 128]])),
                        after=[ev_zuse.get(("re", c))], dma=True,
                        ch=f"zr{c}")
                    ev_zl[("im", c)] = S.op("g", lambda e, c=c, cbase=cbase: e.dma_start(
                        out=zim[c][:],
                        in_=bass.AP(v_in, (cbase + c) * 2 * L + L,
                                    [[128, 128], [1, 128]])),
                        after=[ev_zuse.get(("im", c))], dma=True,
                        ch=f"zi{c}")
                ev_s1 = []
                for c in range(4):
                    ev2 = S.op("t", lambda e, c=c: e.matmul(
                        pA[c][:, 0:512], hz[c][:], T["cA512"][:],
                        start=True, stop=True),
                        after=[ev_zl[("h", c)], ev_tables, ev_evA])
                    ev_zuse[("h", c)] = ev2
                    ev_s1.append(ev2)
                ev_ev = []
                for c in range(4):
                    e1 = S.op("s", lambda e, c=c: e.copy(
                        s2re[:, c * 256:(c + 1) * 256], pA[c][:, 0:256]),
                        after=[ev_s1[c], ev_s2rd] + ev_twrd)
                    e2 = S.op("s", lambda e, c=c: e.copy(
                        s2im[:, c * 256:(c + 1) * 256], pA[c][:, 256:512]))
                    ev_ev += [e1, e2]
                ev_evA = ev_ev[-1]
                ev_tw = cplx6(s2re[:], s2im[:], s2re[:], s2im[:],
                              bc(T["twre"], 256), bc(T["twim"], 256),
                              ev_ev + [ev_s2rd])
                ev_twrd = ev_tw
                ev_hs2 = []
                for half in range(2):
                    sl = slice(half * 512, (half + 1) * 512)
                    e1 = S.op("t", lambda e, sl=sl, half=half: e.matmul(
                        pB[half][:, 0:512], T["f128re"][:], s2re[:, sl],
                        start=True, stop=False), after=ev_tw + [ev_evB])
                    e1 = S.op("t", lambda e, sl=sl, half=half: e.matmul(
                        pB[half][:, 0:512], T["nf128im"][:], s2im[:, sl],
                        start=False, stop=True))
                    e2 = S.op("t", lambda e, sl=sl, half=half: e.matmul(
                        pB[2 + half][:, 0:512], T["f128re"][:], s2im[:, sl],
                        start=True, stop=False))
                    e2 = S.op("t", lambda e, sl=sl, half=half: e.matmul(
                        pB[2 + half][:, 0:512], T["f128im"][:], s2re[:, sl],
                        start=False, stop=True))
                    ev_hs2 += [e1, e2]
                ev_s2rd = ev_hs2[-1]
                ev_ev = []
                for half in range(2):
                    sl = slice(half * 512, (half + 1) * 512)
                    ev_ev.append(S.op("s", lambda e, sl=sl, half=half:
                                      e.copy(hre[:, sl],
                                                    pB[half][:, 0:512]),
                                      after=[ev_hs2[2 * half + 1]] + ev_prod))
                    ev_ev.append(S.op("s", lambda e, sl=sl, half=half:
                                      e.copy(him[:, sl],
                                                    pB[2 + half][:, 0:512])))
                ev_evB = ev_ev[-1]
                ev_hev = list(ev_ev)
                ev_s1 = []
                for c in range(4):
                    e2 = S.op("t", lambda e, c=c: e.matmul(
                        pA[c][:, 0:512], zre[c][:], T["cA512"][:],
                        start=True, stop=False),
                        after=[ev_zl[("re", c)], ev_zl[("im", c)],
                               ev_evA])
                    e2 = S.op("t", lambda e, c=c: e.matmul(
                        pA[c][:, 0:512], zim[c][:], T["cB512"][:],
                        start=False, stop=True))
                    ev_zuse[("re", c)] = e2
                    ev_zuse[("im", c)] = e2
                    ev_s1.append(e2)
                ev_ev = []
                for c in range(4):
                    e1 = S.op("s", lambda e, c=c: e.copy(
                        s2re[:, c * 256:(c + 1) * 256], pA[c][:, 0:256]),
                        after=[ev_s1[c], ev_s2rd] + ev_twrd)
                    e2 = S.op("s", lambda e, c=c: e.copy(
                        s2im[:, c * 256:(c + 1) * 256], pA[c][:, 256:512]))
                    ev_ev += [e1, e2]
                ev_evA = ev_ev[-1]
                ev_tw = cplx6(s2re[:], s2im[:], s2re[:], s2im[:],
                              bc(T["twre"], 256), bc(T["twim"], 256),
                              ev_ev + [ev_s2rd])
                ev_twrd = ev_tw
                ev_vs2 = []
                for half in range(2):
                    sl = slice(half * 512, (half + 1) * 512)
                    e1 = S.op("t", lambda e, sl=sl, half=half: e.matmul(
                        pB[half][:, 0:512], T["f128re"][:], s2re[:, sl],
                        start=True, stop=False),
                        after=ev_tw + [ev_evB])
                    e1 = S.op("t", lambda e, sl=sl, half=half: e.matmul(
                        pB[half][:, 0:512], T["nf128im"][:], s2im[:, sl],
                        start=False, stop=True))
                    e2 = S.op("t", lambda e, sl=sl, half=half: e.matmul(
                        pB[2 + half][:, 0:512], T["f128re"][:], s2im[:, sl],
                        start=True, stop=False))
                    e2 = S.op("t", lambda e, sl=sl, half=half: e.matmul(
                        pB[2 + half][:, 0:512], T["f128im"][:], s2re[:, sl],
                        start=False, stop=True))
                    ev_vs2 += [e1, e2]
                ev_s2rd = ev_vs2[-1]
                ev_ev = []
                for half in range(2):
                    sl = slice(half * 512, (half + 1) * 512)
                    ev_ev.append(S.op("s", lambda e, sl=sl, half=half:
                                      e.copy(xre[:, sl],
                                                    pB[half][:, 0:512]),
                                      after=[ev_vs2[2 * half + 1]]
                                      + ev_prod))
                    ev_ev.append(S.op("s", lambda e, sl=sl, half=half:
                                      e.copy(xim[:, sl],
                                                    pB[2 + half][:, 0:512])))
                ev_evB = ev_ev[-1]
                ev_prod = cplx6(pre[:], pim[:], xre[:], xim[:], hre[:],
                                him[:], ev_ev + ev_hev + [ev_pird])
                ev_t1 = []
                for c in range(4):
                    evs = []
                    for r in range(2):
                        lre = pre[:, c * 256 + r * 128:c * 256 + (r + 1) * 128]
                        lim = pim[:, c * 256 + r * 128:c * 256 + (r + 1) * 128]
                        e2 = S.op("t", lambda e, c=c, r=r, lre=lre: e.matmul(
                            pA[c][:, r * 256:r * 256 + 256], lre,
                            T["cA256"][:], start=True, stop=False),
                            after=ev_prod + [ev_evA])
                        e2 = S.op("t", lambda e, c=c, r=r, lim=lim: e.matmul(
                            pA[c][:, r * 256:r * 256 + 256], lim,
                            T["cB256"][:], start=False, stop=True))
                        evs.append(e2)
                    ev_t1.append(evs)
                ev_pird = ev_t1[-1][-1]
                ev_ev = []
                for c in range(4):
                    for r in range(2):
                        ev_ev.append(S.op("s", lambda e, c=c, r=r:
                                          e.copy(
                            t2[("r", r)][:, c * 128:(c + 1) * 128],
                            pA[c][:, r * 256:r * 256 + 128]),
                            after=[ev_t1[c][1]] + ev_t2rd))
                        ev_ev.append(S.op("s", lambda e, c=c, r=r:
                                          e.copy(
                            t2[("i", r)][:, c * 128:(c + 1) * 128],
                            pA[c][:, r * 256 + 128:(r + 1) * 256])))
                ev_evA = ev_ev[-1]
                ev_itw = []
                for r in range(2):
                    def bcr(tile, r=r):
                        return bass.AP(tile, r * 128,
                                       [[256, 128], [0, 4], [1, 128]])
                    e6 = cplx6(t2p[("r", r)][:], t2p[("i", r)][:],
                               t2[("r", r)][:], t2[("i", r)][:],
                               bcr(T["itwre"]), bcr(T["itwim"]),
                               ev_ev + [ev_t2prd], n=512)
                    ev_itw.append(e6)
                ev_t2rd = ev_itw[-1]
                el = []
                for (bank, pl1, tb1, pl2, tb2) in [
                        (0, "r", "c256re", "i", "nc256im"),
                        (1, "r", "c256im", "i", "c256re")]:
                    e1 = None
                    for r in range(2):
                        e1 = S.op("t", lambda e, bank=bank, pl1=pl1,
                                  tb1=tb1, r=r: e.matmul(
                            pB[bank][:, 0:512],
                            bass.AP(T[tb1], r * 128, [[256, 128], [1, 128]]),
                            t2p[(pl1, r)][:],
                            start=(r == 0), stop=False),
                            after=ev_itw[r] + [ev_evB] if r == 0
                            else ev_itw[r])
                    for r in range(2):
                        e1 = S.op("t", lambda e, bank=bank, pl2=pl2,
                                  tb2=tb2, r=r: e.matmul(
                            pB[bank][:, 0:512],
                            bass.AP(T[tb2], r * 128, [[256, 128], [1, 128]]),
                            t2p[(pl2, r)][:],
                            start=False, stop=(r == 1)))
                    el.append(e1)
                ev_t2prd = el[-1]
                ev_o = []
                for pl in range(2):
                    eo = S.op("s", lambda e, pl=pl: e.copy(
                        zo[pl][:], pB[pl][:, 0:512]),
                        after=[el[pl], ev_zodma[pl]])
                    ev_o.append(eo)
                ev_evB = ev_o[-1]
                for pl in range(2):
                    last = None
                    for cc in range(4):
                        last = S.op("g", lambda e, pl=pl, cbase=cbase, cc=cc:
                                    e.dma_start(
                            out=bass.AP(conv_out,
                                        (cbase + cc) * 2 * L + pl * L,
                                        [[128, 128], [1, 128]]),
                            in_=zo[pl][:, cc * 128:(cc + 1) * 128]),
                            after=[ev_o[pl]], dma=True, ch=f"co{pl}")
                    ev_zodma[pl] = last

            S.build()
    return nc


def in_maps_B(inputs, v_relay):
    import ml_dtypes
    maps = []
    fw1 = np.asarray(inputs["L0_attn_fw1"], np.float32)
    fb1 = np.asarray(inputs["L0_attn_fb1"], np.float32).reshape(64, 1)
    fw2 = np.asarray(inputs["L0_attn_fw2"], np.float32)
    fb2 = np.asarray(inputs["L0_attn_fb2"], np.float32).reshape(128, 1)
    fw3 = np.asarray(inputs["L0_attn_fw3"], np.float32)
    fb3 = np.asarray(inputs["L0_attn_fb3"], np.float32)
    posn = np.arange(L, dtype=np.float32).reshape(1, L)
    j = np.arange(128)
    def wrap(x, n):
        return ((x + n // 2) % n) - n // 2
    k2g = np.arange(256)
    ec256 = wrap(64 - (j[:, None] * k2g[None, :]) % 256, 256).astype(np.float32)
    es256 = wrap(-(j[:, None] * k2g[None, :]) % 256, 256).astype(np.float32)
    k1g = np.arange(128)
    ec128 = wrap(32 - (j[:, None] * k1g[None, :]) % 128, 128).astype(np.float32)
    es128 = wrap(-(j[:, None] * k1g[None, :]) % 128, 128).astype(np.float32)
    n_ = 32768
    ectw = wrap(8192 - (j[:, None] * k2g[None, :]) % n_, n_).astype(np.float32)
    estw = wrap(-(j[:, None] * k2g[None, :]) % n_, n_).astype(np.float32)
    k2f = np.arange(256)
    itw_c = wrap(8192 - (k2f[:, None] * j[None, :]) % n_, n_)
    itw_s = wrap(+(k2f[:, None] * j[None, :]) % n_, n_)
    ecitw = np.concatenate([itw_c[0:128], itw_c[128:256]], axis=1).astype(np.float32)
    esitw = np.concatenate([itw_s[0:128] * 0 + wrap(-(k2f[0:128, None] * j[None, :]) % n_, n_),
                            wrap(-(k2f[128:256, None] * j[None, :]) % n_, n_)],
                           axis=1).astype(np.float32)
    j2g = np.arange(128)
    c_c = wrap(64 - (k2f[:, None] * j2g[None, :]) % 256, 256)
    c_s = wrap(-(k2f[:, None] * j2g[None, :]) % 256, 256)
    ecc256 = np.concatenate([c_c[0:128], c_c[128:256]], axis=1).astype(np.float32)
    esc256 = np.concatenate([c_s[0:128], c_s[128:256]], axis=1).astype(np.float32)
    for c in range(NC):
        vloc = np.empty((128, 2 * L), dtype=ml_dtypes.bfloat16)
        for s in range(NC):
            b, t0 = s // 4, (s % 4) * TPC
            vloc[:, b * L + t0:b * L + t0 + TPC] =                 v_relay[s][c * 128:(c + 1) * 128]
        ci = np.arange(c * 128, (c + 1) * 128)
        maps.append({
            "v_in": vloc, "fw1": fw1, "fb1": fb1, "fw2": fw2, "fb2": fb2,
            "fw3c": np.ascontiguousarray(fw3[:, 2 * ci + 1]),
            "fb3c": np.ascontiguousarray(fb3[2 * ci + 1]).reshape(128, 1),
            "posn": posn,
            "ec256": ec256, "es256": es256, "ec128": ec128, "es128": es128,
            "ectw": ectw, "estw": estw, "ecitw": ecitw, "esitw": esitw,
            "ecc256": ecc256, "esc256": esc256,
        })
    return maps



def build_C():
    nc = bass.Bass("TRN2", target_bir_lowering=False)
    xt = nc.dram_tensor("xt", [D, W], F32, kind="ExternalInput")
    x2t = nc.dram_tensor("x2t", [D, W], BF16, kind="ExternalInput")
    cvt = nc.dram_tensor("cvt", [D, W], BF16, kind="ExternalInput")
    wout = nc.dram_tensor("wout", [D, D], F32, kind="ExternalInput")
    bout = nc.dram_tensor("bout", [128, 8], F32, kind="ExternalInput")
    id_h = nc.dram_tensor("id_h", [128, 128], F32, kind="ExternalInput")
    x1_out = nc.dram_tensor("x1_out", [D, W], F32, kind="ExternalOutput")
    xln2_out = nc.dram_tensor("xln2_out", [D, W], BF16, kind="ExternalOutput")
    m1_out = nc.dram_tensor("m1_out", [1, 1], F32, kind="ExternalOutput")

    es = contextlib.ExitStack()
    with es:
        S = Sched(nc)
        wo_sb = alloc(nc, es, "wo", [128, 8 * 8 * 128], BF16)
        ones_sb = alloc(nc, es, "ones", [128, 128], BF16)
        bo_sb = alloc(nc, es, "bo", [128, 8], F32)
        id_sb = alloc(nc, es, "idm", [128, 128], BF16)
        xf = [alloc(nc, es, f"xf{i}", [128, 8 * 512], F32) for i in range(2)]
        x2b = [alloc(nc, es, f"x2b{i}", [128, 8 * 512], BF16) for i in range(2)]
        cvb = [alloc(nc, es, f"cvb{i}", [128, 8 * 512], BF16) for i in range(2)]
        ub = alloc(nc, es, "ub", [128, 8 * 512], BF16)
        yb = alloc(nc, es, "yb", [128, 8 * 512], F32)
        ybf = alloc(nc, es, "ybf", [128, 8 * 512], BF16)
        ysq = alloc(nc, es, "ysq", [128, 8 * 512], BF16)
        mu_ = alloc(nc, es, "mu_", [128, 512], F32)
        rb_ = alloc(nc, es, "rb_", [128, 512], F32)
        ts1 = alloc(nc, es, "ts1", [128, 512], F32)
        ts2 = alloc(nc, es, "ts2", [128, 512], F32)
        xl2 = [alloc(nc, es, f"xl2{i}", [128, 8 * 512], BF16) for i in range(2)]
        mxc = alloc(nc, es, "mxc", [128, 16], F32)
        mxt = alloc(nc, es, "mxt", [128, 128], BF16)
        m1sb = alloc(nc, es, "m1sb", [1, 1], F32)

        with (nc.psum_tensor("psA", [128, 512], F32) as psA,
              nc.psum_tensor("psB", [128, 512], F32) as psB,
              nc.psum_tensor("pss1", [128, 512], F32) as pss1,
              nc.psum_tensor("pss2", [128, 512], F32) as pss2,
              nc.psum_tensor("ptp", [128, 128], BF16) as ptp):

            ev_w = S.op("g", lambda e: e.dma_start(
                out=ap3(wo_sb, 8192, 128, (1024, 8), (1, 1024)),
                in_=bass.AP(wout, 0, [[D, 128], [128 * D, 8], [1, 1024]])),
                dma=True, ch="cstg")
            ev_id = S.op("g", lambda e: e.dma_start(
                out=id_sb[:], in_=id_h.ap()), dma=True, ch="cstg")
            S.op("g", lambda e: e.memset(ones_sb[:], 1.0), ch="cstg2")
            S.op("g", lambda e: e.memset(mxc[:], 0.0), ch="cstg2")
            ev_on = S.op("g", lambda e: e.memset(mxt[:], 0.0), ch="cstg2")
            ev_bo = S.op("y", lambda e: e.dma_start(
                out=bo_sb[:], in_=bout.ap()), dma=True, ch="cst")
            ev_cst = ("cst", S.n["cst"])
            ev_cstg = ("cstg", S.n["cstg"])
            ev_cstg2 = ("cstg2", S.n["cstg2"])

            ev_use = {0: None, 1: None}
            ev_evac = {0: None, 1: None}
            ev_x1st = {0: None, 1: None}
            ev_xl2st = {0: None, 1: None}
            ev_srd = None
            ev_ssq = None
            ev_mmu = None
            ev_mxw = ev_cstg2
            col = 0
            for ti, cw in enumerate(TILES):
                bi = ti % 2
                c0 = col
                col += cw
                el1 = S.op("y", lambda e, c0=c0, cw=cw, bi=bi: e.dma_start(
                    out=ap3(xf[bi], 8 * 512, 128, (cw, 8), (1, cw)),
                    in_=bass.AP(xt, c0, [[W, 128], [128 * W, 8], [1, cw]])),
                    after=[ev_use[bi]], dma=True, ch=f"lx{bi}")
                el2 = S.op("y", lambda e, c0=c0, cw=cw, bi=bi: e.dma_start(
                    out=ap3(x2b[bi], 8 * 512, 128, (cw, 8), (1, cw)),
                    in_=bass.AP(x2t, c0, [[W, 128], [128 * W, 8], [1, cw]])),
                    after=[ev_use[bi]], dma=True, ch=f"l2{bi}")
                el3 = S.op("y", lambda e, c0=c0, cw=cw, bi=bi: e.dma_start(
                    out=ap3(cvb[bi], 8 * 512, 128, (cw, 8), (1, cw)),
                    in_=bass.AP(cvt, c0, [[W, 128], [128 * W, 8], [1, cw]])),
                    after=[ev_use[bi]], dma=True, ch=f"lc{bi}")
                evu = S.op("v", lambda e, cw=cw, bi=bi: e.tensor_tensor(
                    ub[:, 0:8 * cw], x2b[bi][:, 0:8 * cw],
                    cvb[bi][:, 0:8 * cw], ALU.mult), after=[el2, el3, ev_mmu])
                ev_ev = []
                for m in range(8):
                    pbank = psA if m % 2 == 0 else psB
                    evk = None
                    for k in range(8):
                        aft = ([evu, ev_w, ev_evac[m % 2]] if k == 0 else [])
                        evk = S.op("t", lambda e, k=k, m=m, cw=cw,
                                   pbank=pbank: e.matmul(
                            pbank[:, 0:cw],
                            wo_sb[:, (k * 8 + m) * 128:(k * 8 + m + 1) * 128],
                            ub[:, k * cw:(k + 1) * cw],
                            start=(k == 0), stop=(k == 7)), after=aft)
                    ev_evac[m % 2] = S.op("s", lambda e, m=m, cw=cw,
                                          pbank=pbank: e.activation(
                        yb[:, m * cw:(m + 1) * cw], pbank[:, 0:cw],
                        AF.Identity, bias=bo_sb[:, m:m + 1]),
                        after=[evk, ev_cst])
                    ev_ev.append(ev_evac[m % 2])
                ev_mmu = evk
                evc = S.op("v", lambda e, cw=cw: e.tensor_copy(
                    ybf[:, 0:8 * cw], yb[:, 0:8 * cw]), after=ev_ev)
                evq = S.op("s", lambda e, cw=cw: e.activation(
                    ysq[:, 0:8 * cw], ybf[:, 0:8 * cw], AF.Square),
                    after=[evc, ev_ssq])
                evm = None
                for k in range(8):
                    aft = [evc, ev_cstg2, ev_srd] if k == 0 else []
                    evm = S.op("t", lambda e, k=k, cw=cw: e.matmul(
                        pss1[:, 0:cw], ones_sb[:], ybf[:, k * cw:(k + 1) * cw],
                        start=(k == 0), stop=(k == 7)), after=aft)
                for k in range(8):
                    aft = [evq] if k == 0 else []
                    evm = S.op("t", lambda e, k=k, cw=cw: e.matmul(
                        pss2[:, 0:cw], ones_sb[:], ysq[:, k * cw:(k + 1) * cw],
                        start=(k == 0), stop=(k == 7)), after=aft)
                ev_ssq = evm

                def stat_chain(evm, cw):
                    e1 = S.op("v", lambda e: e.tensor_scalar(
                        mu_[:, 0:cw], pss1[:, 0:cw], 1.0 / D, None, ALU.mult),
                        after=[evm])
                    e2 = S.op("v", lambda e: e.tensor_scalar(
                        ts1[:, 0:cw], pss2[:, 0:cw], 1.0 / D, 1e-5,
                        ALU.mult, ALU.add))
                    e3 = S.op("v", lambda e: e.tensor_tensor(
                        ts2[:, 0:cw], mu_[:, 0:cw], mu_[:, 0:cw], ALU.mult))
                    e4 = S.op("v", lambda e: e.tensor_tensor(
                        ts1[:, 0:cw], ts1[:, 0:cw], ts2[:, 0:cw],
                        ALU.subtract))
                    e5 = S.op("s", lambda e: e.activation(
                        ts2[:, 0:cw], ts1[:, 0:cw], AF.Sqrt), after=[e4])
                    e6 = S.op("v", lambda e: e.reciprocal(
                        rb_[:, 0:cw], ts2[:, 0:cw]), after=[e5])
                    return e2, e6

                ev_srd, e6 = stat_chain(evm, cw)
                e7 = S.op("v", lambda e, cw=cw: e.tensor_tensor(
                    yb[:, 0:8 * cw], yb[:, 0:8 * cw],
                    bass.AP(mu_, 0, [[512, 128], [0, 8], [1, cw]]),
                    ALU.subtract), after=[e6])
                e8 = S.op("v", lambda e, cw=cw: e.tensor_tensor(
                    yb[:, 0:8 * cw], yb[:, 0:8 * cw],
                    bass.AP(rb_, 0, [[512, 128], [0, 8], [1, cw]]), ALU.mult))
                e9 = S.op("v", lambda e, cw=cw, bi=bi: e.tensor_tensor(
                    xf[bi][:, 0:8 * cw], xf[bi][:, 0:8 * cw],
                    yb[:, 0:8 * cw], ALU.add), after=[e8, el1])
                ev_x1st[bi] = S.op("g", lambda e, c0=c0, cw=cw, bi=bi:
                                   e.dma_start(
                    out=bass.AP(x1_out, c0, [[W, 128], [128 * W, 8], [1, cw]]),
                    in_=ap3(xf[bi], 8 * 512, 128, (cw, 8), (1, cw))),
                    after=[e9], dma=True, ch=f"sx{bi}")
                evc2 = S.op("v", lambda e, cw=cw, bi=bi: e.tensor_copy(
                    ybf[:, 0:8 * cw], xf[bi][:, 0:8 * cw]), after=[e9])
                evq2 = S.op("s", lambda e, cw=cw: e.activation(
                    ysq[:, 0:8 * cw], ybf[:, 0:8 * cw], AF.Square),
                    after=[evc2])
                evm2 = None
                for k in range(8):
                    aft = [evc2, ev_srd] if k == 0 else []
                    evm2 = S.op("t", lambda e, k=k, cw=cw: e.matmul(
                        pss1[:, 0:cw], ones_sb[:], ybf[:, k * cw:(k + 1) * cw],
                        start=(k == 0), stop=(k == 7)), after=aft)
                for k in range(8):
                    aft = [evq2] if k == 0 else []
                    evm2 = S.op("t", lambda e, k=k, cw=cw: e.matmul(
                        pss2[:, 0:cw], ones_sb[:], ysq[:, k * cw:(k + 1) * cw],
                        start=(k == 0), stop=(k == 7)), after=aft)
                ev_ssq = evm2
                ev_srd, e6b = stat_chain(evm2, cw)
                ea = S.op("v", lambda e, cw=cw, bi=bi: e.tensor_tensor(
                    xf[bi][:, 0:8 * cw], xf[bi][:, 0:8 * cw],
                    bass.AP(mu_, 0, [[512, 128], [0, 8], [1, cw]]),
                    ALU.subtract), after=[e6b, ev_x1st[bi]])
                eb = S.op("v", lambda e, cw=cw, bi=bi: e.tensor_tensor(
                    xl2[bi][:, 0:8 * cw], xf[bi][:, 0:8 * cw],
                    bass.AP(rb_, 0, [[512, 128], [0, 8], [1, cw]]), ALU.mult),
                    after=[ev_xl2st[bi]])
                ev_use[bi] = eb
                ev_xl2st[bi] = S.op("g", lambda e, c0=c0, cw=cw, bi=bi:
                                    e.dma_start(
                    out=bass.AP(xln2_out, c0,
                                [[W, 128], [128 * W, 8], [1, cw]]),
                    in_=ap3(xl2[bi], 8 * 512, 128, (cw, 8), (1, cw))),
                    after=[eb], dma=True, ch=f"s2{bi}")
                lo = HALO - c0 if c0 < HALO else 0
                hi = cw if c0 + cw <= W - HALO else max(0, W - HALO - c0)
                if hi > lo:
                    ev_mxw = S.op("v", lambda e, cw=cw, bi=bi, ti=ti, lo=lo,
                                  hi=hi: e.tensor_reduce(
                        mxc[:, ti:ti + 1],
                        bass.AP(xl2[bi], lo, [[8 * 512, 128], [cw, 8],
                                              [1, hi - lo]]),
                        AX.XY, ALU.max, apply_absolute_value=True),
                        after=[eb, ev_mxw])

            ev1 = S.op("v", lambda e: e.tensor_reduce(
                bass.AP(mxt, 0, [[128, 128], [1, 1]]), mxc[:], AX.X, ALU.max,
                apply_absolute_value=True), after=[ev_mxw, ev_on])
            ev2 = S.op("t", lambda e: e.transpose(
                ptp[:], mxt[:], id_sb[:]), after=[ev1, ev_id])
            ev3 = S.op("v", lambda e: e.tensor_reduce(
                m1sb[:], ptp[0:1, :], AX.X, ALU.max,
                apply_absolute_value=True), after=[ev2])
            ev4 = S.op("g", lambda e: e.dma_start(
                out=m1_out.ap(), in_=m1sb[:]), after=[ev3], dma=True, ch="m1")
            S.build()
    return nc



def build_D():
    nc = bass.Bass("TRN2", target_bir_lowering=False)
    xlt = nc.dram_tensor("xlt", [D, W], BF16, kind="ExternalInput")
    m1r = nc.dram_tensor("m1r", [128, 8], F32, kind="ExternalInput")
    wgT = nc.dram_tensor("wgT", [D, IH], BF16, kind="ExternalInput")
    wuT = nc.dram_tensor("wuT", [D, IH], BF16, kind="ExternalInput")
    one128 = nc.dram_tensor("one128", [128, 1], F32, kind="ExternalInput")
    id_h = nc.dram_tensor("id_h", [128, 128], F32, kind="ExternalInput")
    g_out = nc.dram_tensor("g_out", [IH, W], BF16, kind="ExternalOutput")
    u_out = nc.dram_tensor("u_out", [IH, W], BF16, kind="ExternalOutput")
    mg_out = nc.dram_tensor("mg_out", [1, 1], F32, kind="ExternalOutput")
    mu_out = nc.dram_tensor("mu_out", [1, 1], F32, kind="ExternalOutput")

    es = contextlib.ExitStack()
    with es:
        S = Sched(nc)
        wg_sb = alloc(nc, es, "wg", [128, 24 * 8 * 128], BF16)
        wu_sb = alloc(nc, es, "wu", [128, 24 * 8 * 128], BF16)
        id_sb = alloc(nc, es, "idm", [128, 128], BF16)
        one_sb = alloc(nc, es, "one", [128, 1], F32)
        g_sb = alloc(nc, es, "gflag", [128, 1], F32)
        m1_sb = alloc(nc, es, "m1s", [128, 8], F32)
        s1_sb = alloc(nc, es, "s1s", [128, 1], F32)
        r1_sb = alloc(nc, es, "r1s", [128, 1], F32)
        xlb = [alloc(nc, es, f"xlb{i}", [128, 8 * 512], BF16) for i in range(2)]
        tq = alloc(nc, es, "tq", [128, 8 * 512], F32)
        ta = alloc(nc, es, "ta", [128, 8 * 512], F32)
        tb = alloc(nc, es, "tb", [128, 8 * 512], F32)
        xq = [alloc(nc, es, f"xq{i}", [128, 8 * 512], BF16) for i in range(2)]
        gch = [alloc(nc, es, f"gch{i}", [128, 512], BF16) for i in range(4)]
        mxg = alloc(nc, es, "mxg", [128, 1], F32)
        mxu = alloc(nc, es, "mxu", [128, 1], F32)
        mtmp = alloc(nc, es, "mtmp", [128, 1], F32)
        mxt = alloc(nc, es, "mxt", [128, 128], BF16)
        msc = alloc(nc, es, "msc", [1, 1], F32)

        with (nc.psum_tensor("pg0", [128, 512], F32) as pg0,
              nc.psum_tensor("pg1", [128, 512], F32) as pg1,
              nc.psum_tensor("pu0", [128, 512], F32) as pu0,
              nc.psum_tensor("pu1", [128, 512], F32) as pu1,
              nc.psum_tensor("ptp", [128, 128], BF16) as ptp):
            ev_wg = S.op("g", lambda e: e.dma_start(
                out=ap3(wg_sb, 24 * 1024, 128, (3072, 8), (1, 3072)),
                in_=bass.AP(wgT, 0, [[IH, 128], [128 * IH, 8], [1, 3072]])),
                dma=True, ch="cstg")
            ev_wu = S.op("g", lambda e: e.dma_start(
                out=ap3(wu_sb, 24 * 1024, 128, (3072, 8), (1, 3072)),
                in_=bass.AP(wuT, 0, [[IH, 128], [128 * IH, 8], [1, 3072]])),
                dma=True, ch="cstg")
            ev_id = S.op("g", lambda e: e.dma_start(
                out=id_sb[:], in_=id_h.ap()), dma=True, ch="cstg")
            ev_m = S.op("y", lambda e: e.dma_start(
                out=m1_sb[:], in_=m1r.ap()), dma=True, ch="cst")
            ev_1 = S.op("y", lambda e: e.dma_start(
                out=one_sb[:], in_=one128.ap()), dma=True, ch="cst")
            ev_mz = S.op("g", lambda e: e.memset(mxg[:], 0.0), ch="cstg2")
            ev_mz = S.op("g", lambda e: e.memset(mxu[:], 0.0), ch="cstg2")
            ev_mz2 = S.op("g", lambda e: e.memset(mxt[:], 0.0), ch="cstg2")
            ev_cst = ("cst", S.n["cst"])
            ev_cstg = ("cstg", S.n["cstg"])
            ev_cstg2 = ("cstg2", S.n["cstg2"])
            ev_g = emit_flag_g(S, one_sb, g_sb, ev_cst)
            ev = S.op("v", lambda e: e.tensor_reduce(
                s1_sb[:], m1_sb[:], AX.X, ALU.max), after=[ev_m])
            ev = S.op("v", lambda e: e.tensor_scalar(
                s1_sb[:], s1_sb[:], 1.0 / 32767.0, 1e-12, ALU.mult, ALU.max))
            ev_s1 = S.op("v", lambda e: e.reciprocal(r1_sb[:], s1_sb[:]))

            ev_use = {0: None, 1: None}
            ev_gst = {i: None for i in range(4)}
            ev_evac = {i: None for i in range(4)}
            ev_mx = ev_cstg2
            col = 0
            for ti, cw in enumerate(TILES):
                bi = ti % 2
                c0 = col
                col += cw
                el = S.op("y", lambda e, c0=c0, cw=cw, bi=bi: e.dma_start(
                    out=ap3(xlb[bi], 8 * 512, 128, (cw, 8), (1, cw)),
                    in_=bass.AP(xlt, c0, [[W, 128], [128 * W, 8], [1, cw]])),
                    after=[ev_use[bi]], dma=True, ch=f"lx{bi}")
                e1 = S.op("v", lambda e, cw=cw, bi=bi: e.tensor_scalar(
                    tq[:, 0:8 * cw], xlb[bi][:, 0:8 * cw], r1_sb[:], None,
                    ALU.mult), after=[el, ev_s1])
                e2 = emit_round(S, tq[:, 0:8 * cw], tq[:, 0:8 * cw], g_sb,
                                ta[:, 0:8 * cw], tb[:, 0:8 * cw],
                                after=[e1, ev_g])
                e3 = S.op("v", lambda e, cw=cw, bi=bi: e.tensor_scalar(
                    xq[bi][:, 0:8 * cw], tq[:, 0:8 * cw], s1_sb[:], None,
                    ALU.mult), after=[e2])
                ev_use[bi] = e3
                last_mm = None
                for which, wsb, evw, outt, mxacc in [
                        (0, wg_sb, ev_wg, g_out, mxg),
                        (1, wu_sb, ev_wu, u_out, mxu)]:
                    for m in range(24):
                        slot = which * 2 + (m % 2)
                        pbank = [pg0, pg1, pu0, pu1][slot]
                        evk = None
                        for k in range(8):
                            aft = ([e3, evw, ev_evac[slot]] if k == 0 else [])
                            evk = S.op("t", lambda e, k=k, m=m, cw=cw, bi=bi,
                                       wsb=wsb, pbank=pbank: e.matmul(
                                pbank[:, 0:cw],
                                wsb[:, (k * 24 + m) * 128:
                                       (k * 24 + m + 1) * 128],
                                xq[bi][:, k * cw:(k + 1) * cw],
                                start=(k == 0), stop=(k == 7)), after=aft)
                        ecp = S.op("s", lambda e, slot=slot, cw=cw,
                                   pbank=pbank: e.copy(
                            gch[slot][:, 0:cw], pbank[:, 0:cw]),
                            after=[evk, ev_gst[slot]])
                        ev_evac[slot] = ecp
                        lo = HALO - c0 if c0 < HALO else 0
                        hi = cw if c0 + cw <= W - HALO else max(0, W - HALO - c0)
                        if hi > lo:
                            ev_mx = S.op("v", lambda e, slot=slot, lo=lo,
                                         hi=hi: e.tensor_reduce(
                                mtmp[:], gch[slot][:, lo:hi], AX.X, ALU.max,
                                apply_absolute_value=True),
                                after=[ecp, ev_mx])
                            ev_mx = S.op("v", lambda e, mxacc=mxacc:
                                         e.tensor_tensor(
                                mxacc[:], mxacc[:], mtmp[:], ALU.max),
                                after=[ev_mx])
                        ev_gst[slot] = S.op("g", lambda e, m=m, c0=c0, cw=cw,
                                            slot=slot, outt=outt: e.dma_start(
                            out=bass.AP(outt, m * 128 * W + c0,
                                        [[W, 128], [1, cw]]),
                            in_=gch[slot][:, 0:cw]), after=[ecp], dma=True,
                            ch=f"gs{slot}")
                        last_mm = evk
            for mxacc, outm in [(mxg, mg_out), (mxu, mu_out)]:
                ev1 = S.op("v", lambda e, mxacc=mxacc: e.tensor_copy(
                    bass.AP(mxt, 0, [[128, 128], [1, 1]]), mxacc[:]),
                    after=[ev_mx, ev_mz2])
                ev2 = S.op("t", lambda e: e.transpose(
                    ptp[:], mxt[:], id_sb[:]), after=[ev1, ev_id])
                ev3 = S.op("v", lambda e: e.tensor_reduce(
                    msc[:], ptp[0:1, :], AX.X, ALU.max,
                    apply_absolute_value=True), after=[ev2])
                ev4 = S.op("g", lambda e, outm=outm: e.dma_start(
                    out=outm.ap(), in_=msc[:]), after=[ev3], dma=True,
                    ch="mo")
                ev_mz2 = ev4
                ev_id = ev4
            S.build()
    return nc



def build_E():
    nc = bass.Bass("TRN2", target_bir_lowering=False)
    g_t = nc.dram_tensor("g_t", [IH, W], BF16, kind="ExternalInput")
    u_t = nc.dram_tensor("u_t", [IH, W], BF16, kind="ExternalInput")
    mgr = nc.dram_tensor("mgr", [128, 8], F32, kind="ExternalInput")
    mur = nc.dram_tensor("mur", [128, 8], F32, kind="ExternalInput")
    one128 = nc.dram_tensor("one128", [128, 1], F32, kind="ExternalInput")
    id_h = nc.dram_tensor("id_h", [128, 128], F32, kind="ExternalInput")
    t0_out = nc.dram_tensor("t0_out", [IH, W], BF16, kind="ExternalOutput")
    uq_out = nc.dram_tensor("uq_out", [IH, W], BF16, kind="ExternalOutput")
    m4_out = nc.dram_tensor("m4_out", [1, 1], F32, kind="ExternalOutput")

    es = contextlib.ExitStack()
    with es:
        S = Sched(nc)
        id_sb = alloc(nc, es, "idm", [128, 128], BF16)
        one_sb = alloc(nc, es, "one", [128, 1], F32)
        g_sb = alloc(nc, es, "gflag", [128, 1], F32)
        mg_sb = alloc(nc, es, "mgs", [128, 8], F32)
        mu_sb = alloc(nc, es, "mus", [128, 8], F32)
        s3_sb = alloc(nc, es, "s3s", [128, 1], F32)
        r3_sb = alloc(nc, es, "r3s", [128, 1], F32)
        s2_sb = alloc(nc, es, "s2s", [128, 1], F32)
        r2_sb = alloc(nc, es, "r2s", [128, 1], F32)
        gb = [alloc(nc, es, f"gb{i}", [128, W], BF16) for i in range(2)]
        ub = [alloc(nc, es, f"ub{i}", [128, W], BF16) for i in range(2)]
        tg = [alloc(nc, es, f"tg{i}", [128, W], F32) for i in range(2)]
        sg = [alloc(nc, es, f"sg{i}", [128, W], F32) for i in range(2)]
        ta = [alloc(nc, es, f"ta{i}", [128, W], F32) for i in range(2)]
        tb = [alloc(nc, es, f"tb{i}", [128, W], F32) for i in range(2)]
        t0b = [alloc(nc, es, f"t0b{i}", [128, W], BF16) for i in range(2)]
        uqb = [alloc(nc, es, f"uqb{i}", [128, W], BF16) for i in range(2)]
        mx4 = alloc(nc, es, "mx4", [128, 1], F32)
        mtmp = alloc(nc, es, "mtmp", [128, 1], F32)
        mxt = alloc(nc, es, "mxt", [128, 128], BF16)
        msc = alloc(nc, es, "msc", [1, 1], F32)
        with nc.psum_tensor("ptp", [128, 128], BF16) as ptp:
            ev_id = S.op("g", lambda e: e.dma_start(
                out=id_sb[:], in_=id_h.ap()), dma=True, ch="cstg")
            ev_mg = S.op("y", lambda e: e.dma_start(
                out=mg_sb[:], in_=mgr.ap()), dma=True, ch="cst")
            ev_mu = S.op("y", lambda e: e.dma_start(
                out=mu_sb[:], in_=mur.ap()), dma=True, ch="cst")
            ev_1 = S.op("y", lambda e: e.dma_start(
                out=one_sb[:], in_=one128.ap()), dma=True, ch="cst")
            S.op("g", lambda e: e.memset(mx4[:], 0.0), ch="cstg2")
            ev_mz = S.op("g", lambda e: e.memset(mxt[:], 0.0), ch="cstg2")
            ev_cst = ("cst", S.n["cst"])
            ev_cstg2 = ("cstg2", S.n["cstg2"])
            ev_g = emit_flag_g(S, one_sb, g_sb, ev_cst)
            for src_sb, ssb, rsb in [(mg_sb, s3_sb, r3_sb),
                                     (mu_sb, s2_sb, r2_sb)]:
                ev = S.op("v", lambda e, a=src_sb, b=ssb: e.tensor_reduce(
                    b[:], a[:], AX.X, ALU.max), after=[ev_cst])
                ev = S.op("v", lambda e, b=ssb: e.tensor_scalar(
                    b[:], b[:], 1.0 / 32767.0, 1e-12, ALU.mult, ALU.max))
                ev_sc = S.op("v", lambda e, b=ssb, r=rsb: e.reciprocal(
                    r[:], b[:]))

            ev_use = {0: None, 1: None}
            ev_st = {0: None, 1: None}
            ev_ust = {0: None, 1: None}
            ev_mx = ev_cstg2
            for hc in range(24):
                bi = hc % 2
                el = S.op("y", lambda e, hc=hc, bi=bi: e.dma_start(
                    out=gb[bi][:], in_=bass.AP(g_t, hc * 128 * W,
                                               [[W, 128], [1, W]])),
                    after=[ev_use[bi]], dma=True, ch=f"lg{bi}")
                elu = S.op("y", lambda e, hc=hc, bi=bi: e.dma_start(
                    out=ub[bi][:], in_=bass.AP(u_t, hc * 128 * W,
                                               [[W, 128], [1, W]])),
                    after=[ev_use[bi]], dma=True, ch=f"lu{bi}")
                e1 = S.op("v", lambda e, bi=bi: e.tensor_scalar(
                    tg[bi][:], gb[bi][:], r3_sb[:], None, ALU.mult),
                    after=[el, ev_sc, ev_g])
                e2 = emit_round(S, tg[bi][:], tg[bi][:], g_sb, ta[bi][:],
                                tb[bi][:], after=[e1])
                e3 = S.op("s", lambda e, bi=bi: e.activation(
                    sg[bi][:], tg[bi][:], AF.Sigmoid, scale=s3_sb[:]),
                    after=[e2])
                e4 = S.op("v", lambda e, bi=bi: e.tensor_scalar(
                    sg[bi][:], sg[bi][:], 65536.0, None, ALU.mult),
                    after=[e3])
                e5 = emit_round(S, sg[bi][:], sg[bi][:], g_sb, ta[bi][:],
                                tb[bi][:], after=[e4])
                e6 = S.op("v", lambda e, bi=bi: e.tensor_scalar(
                    sg[bi][:], sg[bi][:], 65535.0, None, ALU.min))
                e7 = S.op("v", lambda e, bi=bi: e.tensor_tensor(
                    t0b[bi][:], tg[bi][:], sg[bi][:], ALU.mult),
                    after=[e6, ev_st[bi]])
                e8 = S.op("v", lambda e, bi=bi: e.tensor_reduce(
                    mtmp[:], t0b[bi][:, HALO:W - HALO], AX.X, ALU.max,
                    apply_absolute_value=True), after=[e7, ev_mx])
                ev_mx = S.op("v", lambda e: e.tensor_tensor(
                    mx4[:], mx4[:], mtmp[:], ALU.max))
                ev_st[bi] = S.op("g", lambda e, hc=hc, bi=bi: e.dma_start(
                    out=bass.AP(t0_out, hc * 128 * W, [[W, 128], [1, W]]),
                    in_=t0b[bi][:]), after=[e7], dma=True, ch=f"st{bi}")
                f1 = S.op("v", lambda e, bi=bi: e.tensor_scalar(
                    tg[bi][:], ub[bi][:], r2_sb[:], None, ALU.mult),
                    after=[elu, e7])
                f2 = emit_round(S, tg[bi][:], tg[bi][:], g_sb, ta[bi][:],
                                tb[bi][:], after=[f1])
                f3 = S.op("v", lambda e, bi=bi: e.tensor_scalar(
                    uqb[bi][:], tg[bi][:], s2_sb[:], None, ALU.mult),
                    after=[f2, ev_ust[bi]])
                ev_use[bi] = f3
                ev_ust[bi] = S.op("g", lambda e, hc=hc, bi=bi: e.dma_start(
                    out=bass.AP(uq_out, hc * 128 * W, [[W, 128], [1, W]]),
                    in_=uqb[bi][:]), after=[f3], dma=True, ch=f"su{bi}")
            ev1 = S.op("v", lambda e: e.tensor_copy(
                bass.AP(mxt, 0, [[128, 128], [1, 1]]), mx4[:]),
                after=[ev_mx, ev_mz])
            ev2 = S.op("t", lambda e: e.transpose(
                ptp[:], mxt[:], id_sb[:]), after=[ev1, ev_id])
            ev3 = S.op("v", lambda e: e.tensor_reduce(
                msc[:], ptp[0:1, :], AX.X, ALU.max,
                apply_absolute_value=True), after=[ev2])
            S.op("g", lambda e: e.dma_start(
                out=m4_out.ap(), in_=msc[:]), after=[ev3], dma=True, ch="mo")
            S.build()
    return nc



def build_F():
    nc = bass.Bass("TRN2", target_bir_lowering=False)
    t0_t = nc.dram_tensor("t0_t", [IH, W], BF16, kind="ExternalInput")
    uq_t = nc.dram_tensor("uq_t", [IH, W], BF16, kind="ExternalInput")
    m4r = nc.dram_tensor("m4r", [128, 8], F32, kind="ExternalInput")
    mgr = nc.dram_tensor("mgr", [128, 8], F32, kind="ExternalInput")
    one128 = nc.dram_tensor("one128", [128, 1], F32, kind="ExternalInput")
    id_h = nc.dram_tensor("id_h", [128, 128], F32, kind="ExternalInput")
    u0_out = nc.dram_tensor("u0_out", [IH, W], BF16, kind="ExternalOutput")
    m5_out = nc.dram_tensor("m5_out", [1, 1], F32, kind="ExternalOutput")

    es = contextlib.ExitStack()
    with es:
        S = Sched(nc)
        id_sb = alloc(nc, es, "idm", [128, 128], BF16)
        one_sb = alloc(nc, es, "one", [128, 1], F32)
        g_sb = alloc(nc, es, "gflag", [128, 1], F32)
        m4_sb = alloc(nc, es, "m4s", [128, 8], F32)
        mg_sb = alloc(nc, es, "mgs", [128, 8], F32)
        s3_sb = alloc(nc, es, "s3s", [128, 1], F32)
        s4_sb = alloc(nc, es, "s4s", [128, 1], F32)
        rf_sb = alloc(nc, es, "rfs", [128, 1], F32)
        t0b = [alloc(nc, es, f"t0b{i}", [128, W], BF16) for i in range(2)]
        uqb = [alloc(nc, es, f"uqb{i}", [128, W], BF16) for i in range(2)]
        tg = [alloc(nc, es, f"tg{i}", [128, W], F32) for i in range(2)]
        ta = [alloc(nc, es, f"ta{i}", [128, W], F32) for i in range(2)]
        tb = [alloc(nc, es, f"tb{i}", [128, W], F32) for i in range(2)]
        u0b = [alloc(nc, es, f"u0b{i}", [128, W], BF16) for i in range(2)]
        mx5 = alloc(nc, es, "mx5", [128, 1], F32)
        mtmp = alloc(nc, es, "mtmp", [128, 1], F32)
        mxt = alloc(nc, es, "mxt", [128, 128], BF16)
        msc = alloc(nc, es, "msc", [1, 1], F32)
        with nc.psum_tensor("ptp", [128, 128], BF16) as ptp:
            ev_id = S.op("g", lambda e: e.dma_start(
                out=id_sb[:], in_=id_h.ap()), dma=True, ch="cstg")
            ev_m4 = S.op("y", lambda e: e.dma_start(
                out=m4_sb[:], in_=m4r.ap()), dma=True, ch="cst")
            ev_mg = S.op("y", lambda e: e.dma_start(
                out=mg_sb[:], in_=mgr.ap()), dma=True, ch="cst")
            ev_1 = S.op("y", lambda e: e.dma_start(
                out=one_sb[:], in_=one128.ap()), dma=True, ch="cst")
            S.op("g", lambda e: e.memset(mx5[:], 0.0), ch="cstg2")
            ev_mz = S.op("g", lambda e: e.memset(mxt[:], 0.0), ch="cstg2")
            ev_cst = ("cst", S.n["cst"])
            ev_cstg2 = ("cstg2", S.n["cstg2"])
            ev_g = emit_flag_g(S, one_sb, g_sb, ev_cst)
            ev = S.op("v", lambda e: e.tensor_reduce(
                s3_sb[:], mg_sb[:], AX.X, ALU.max), after=[ev_cst])
            ev = S.op("v", lambda e: e.tensor_scalar(
                s3_sb[:], s3_sb[:], 1.0 / 32767.0, 1e-12, ALU.mult, ALU.max))
            ev = S.op("v", lambda e: e.tensor_reduce(
                s4_sb[:], m4_sb[:], AX.X, ALU.max))
            ev = S.op("v", lambda e: e.tensor_tensor(
                s4_sb[:], s4_sb[:], s3_sb[:], ALU.mult))
            ev = S.op("v", lambda e: e.tensor_scalar(
                s4_sb[:], s4_sb[:], 2.0**-16 / 32767.0, 1e-12,
                ALU.mult, ALU.max))
            ev = S.op("v", lambda e: e.reciprocal(rf_sb[:], s4_sb[:]))
            ev = S.op("v", lambda e: e.tensor_tensor(
                rf_sb[:], rf_sb[:], s3_sb[:], ALU.mult))
            ev_sc = S.op("v", lambda e: e.tensor_scalar(
                rf_sb[:], rf_sb[:], 2.0**-16, None, ALU.mult))

            ev_use = {0: None, 1: None}
            ev_st = {0: None, 1: None}
            ev_mx = ev_cstg2
            for hc in range(24):
                bi = hc % 2
                el = S.op("y", lambda e, hc=hc, bi=bi: e.dma_start(
                    out=t0b[bi][:], in_=bass.AP(t0_t, hc * 128 * W,
                                                [[W, 128], [1, W]])),
                    after=[ev_use[bi]], dma=True, ch=f"lt{bi}")
                elu = S.op("y", lambda e, hc=hc, bi=bi: e.dma_start(
                    out=uqb[bi][:], in_=bass.AP(uq_t, hc * 128 * W,
                                                [[W, 128], [1, W]])),
                    after=[ev_use[bi]], dma=True, ch=f"lu{bi}")
                e1 = S.op("v", lambda e, bi=bi: e.tensor_scalar(
                    tg[bi][:], t0b[bi][:], rf_sb[:], None, ALU.mult),
                    after=[el, ev_sc, ev_g])
                e2 = emit_round(S, tg[bi][:], tg[bi][:], g_sb, ta[bi][:],
                                tb[bi][:], after=[e1])
                e3 = S.op("v", lambda e, bi=bi: e.tensor_scalar(
                    tg[bi][:], tg[bi][:], s4_sb[:], None, ALU.mult))
                e4 = S.op("v", lambda e, bi=bi: e.tensor_tensor(
                    u0b[bi][:], tg[bi][:], uqb[bi][:], ALU.mult),
                    after=[elu, ev_st[bi]])
                ev_use[bi] = e4
                e5 = S.op("v", lambda e, bi=bi: e.tensor_reduce(
                    mtmp[:], u0b[bi][:, HALO:W - HALO], AX.X, ALU.max,
                    apply_absolute_value=True), after=[e4, ev_mx])
                ev_mx = S.op("v", lambda e: e.tensor_tensor(
                    mx5[:], mx5[:], mtmp[:], ALU.max))
                ev_st[bi] = S.op("g", lambda e, hc=hc, bi=bi: e.dma_start(
                    out=bass.AP(u0_out, hc * 128 * W, [[W, 128], [1, W]]),
                    in_=u0b[bi][:]), after=[e4], dma=True, ch=f"st{bi}")
            ev1 = S.op("v", lambda e: e.tensor_copy(
                bass.AP(mxt, 0, [[128, 128], [1, 1]]), mx5[:]),
                after=[ev_mx, ev_mz])
            ev2 = S.op("t", lambda e: e.transpose(
                ptp[:], mxt[:], id_sb[:]), after=[ev1, ev_id])
            ev3 = S.op("v", lambda e: e.tensor_reduce(
                msc[:], ptp[0:1, :], AX.X, ALU.max,
                apply_absolute_value=True), after=[ev2])
            S.op("g", lambda e: e.dma_start(
                out=m5_out.ap(), in_=msc[:]), after=[ev3], dma=True, ch="mo")
            S.build()
    return nc



def build_G():
    nc = bass.Bass("TRN2", target_bir_lowering=False)
    u0_t = nc.dram_tensor("u0_t", [IH, W], BF16, kind="ExternalInput")
    x1t = nc.dram_tensor("x1t", [D, W], F32, kind="ExternalInput")
    m5r = nc.dram_tensor("m5r", [128, 8], F32, kind="ExternalInput")
    wdT = nc.dram_tensor("wdT", [IH, D], BF16, kind="ExternalInput")
    wqkv = nc.dram_tensor("wqkv", [D, 3 * D], F32, kind="ExternalInput")
    wo2 = nc.dram_tensor("wo2", [D, D], F32, kind="ExternalInput")
    bo2 = nc.dram_tensor("bo2", [128, 8], F32, kind="ExternalInput")
    selh = nc.dram_tensor("selh", [128, 128], F32, kind="ExternalInput")
    bselh = nc.dram_tensor("bselh", [16, 1024], F32, kind="ExternalInput")
    maskA = nc.dram_tensor("maskA", [16, W], BF16, kind="ExternalInput")
    maskB = nc.dram_tensor("maskB", [16, W], BF16, kind="ExternalInput")
    edgea = nc.dram_tensor("edgea", [128, 2], F32, kind="ExternalInput")
    one128 = nc.dram_tensor("one128", [128, 1], F32, kind="ExternalInput")
    id_h = nc.dram_tensor("id_h", [128, 128], F32, kind="ExternalInput")
    x3_out = nc.dram_tensor("x3_out", [D, TPC], F32, kind="ExternalOutput")
    xln5_out = nc.dram_tensor("xln5_out", [D, TPC], BF16,
                              kind="ExternalOutput")
    mi_out = nc.dram_tensor("mi_out", [1, 1], F32, kind="ExternalOutput")
    x2r_h = nc.dram_tensor("x2r_h", [D, W], F32)
    xln4_h = nc.dram_tensor("xln4_h", [D, W], BF16)

    es = contextlib.ExitStack()
    with es:
        S = Sched(nc)
        wpool = alloc(nc, es, "wpool", [128, 24 * 8 * 128], BF16)
        wo_sb = alloc(nc, es, "wo2s", [128, 8 * 8 * 128], BF16)
        ones_sb = alloc(nc, es, "ones", [128, 128], BF16)
        id_sb = alloc(nc, es, "idm", [128, 128], BF16)
        sel_sb = alloc(nc, es, "sel", [128, 128], BF16)
        bsel_sb = alloc(nc, es, "bsel", [16, 1024], BF16)
        mA_sb = alloc(nc, es, "mA", [16, 512], BF16)
        mB_sb = alloc(nc, es, "mB", [16, 512], BF16)
        bo_sb = alloc(nc, es, "bo2s", [128, 8], F32)
        bo2x = alloc(nc, es, "bo2x", [128, 8], F32)
        bL_sb = alloc(nc, es, "bL", [128, 8], F32)
        bR_sb = alloc(nc, es, "bR", [128, 8], F32)
        edg_sb = alloc(nc, es, "edga", [128, 2], F32)
        one_sb = alloc(nc, es, "one", [128, 1], F32)
        g_sb = alloc(nc, es, "gflag", [128, 1], F32)
        m5_sb = alloc(nc, es, "m5s", [128, 8], F32)
        s5_sb = alloc(nc, es, "s5s", [128, 1], F32)
        r5_sb = alloc(nc, es, "r5s", [128, 1], F32)
        u0pool = alloc(nc, es, "u0pool", [128, 24 * 512], BF16)
        tqpool = alloc(nc, es, "tqpool", [128, 3584], F32)
        ta = alloc(nc, es, "ta", [128, 1536], F32)
        tb = alloc(nc, es, "tb", [128, 1536], F32)
        xf = alloc(nc, es, "xf", [128, 8 * 512], F32)
        yb = alloc(nc, es, "yb", [128, 8 * 512], F32)
        ybf = alloc(nc, es, "ybf", [128, 8 * 512], BF16)
        ysq = alloc(nc, es, "ysq", [128, 8 * 512], BF16)
        mu_ = alloc(nc, es, "mu_", [128, 512], F32)
        rb_ = alloc(nc, es, "rb_", [128, 512], F32)
        ts1 = alloc(nc, es, "ts1", [128, 512], F32)
        ts2 = alloc(nc, es, "ts2", [128, 512], F32)
        mxc = alloc(nc, es, "mxi", [128, 1], F32)
        mtmp = alloc(nc, es, "mtmp", [128, 1], F32)
        mxt = alloc(nc, es, "mxt", [128, 128], BF16)
        msc = alloc(nc, es, "msc", [1, 1], F32)
        qkv_sb = alloc(nc, es, "qkv", [128, 24 * 518], BF16)
        cf_ = alloc(nc, es, "cf_", [16, 7 * 512], BF16)
        tmpv = ts2
        u0b = [u0pool]
        oq = u0pool
        xl4 = ybf
        acc = yb
        accb = u0pool[:, 0:4096]
        xl4b = u0pool[:, 4096:4096 + 8 * 518]
        prod = ysq
        with (nc.psum_tensor("pA", [128, 512], F32) as pA,
              nc.psum_tensor("pB", [128, 512], F32) as pB,
              nc.psum_tensor("pS1", [128, 512], F32) as pS1,
              nc.psum_tensor("pS2", [128, 512], F32) as pS2,
              nc.psum_tensor("pD0", [128, 512], F32) as pD0,
              nc.psum_tensor("pDT", [128, 128], BF16) as pDT,
              nc.psum_tensor("pC0", [128, 512], F32) as pC0,
              nc.psum_tensor("pC1", [128, 512], F32) as pC1):

            ev_wd = S.op("g", lambda e: e.dma_start(
                out=ap3(wpool, 24 * 1024, 128, (1024, 24), (1, 1024)),
                in_=bass.AP(wdT, 0, [[D, 128], [128 * D, 24], [1, 1024]])),
                dma=True, ch="wp")
            ev_wo = S.op("g", lambda e: e.dma_start(
                out=ap3(wo_sb, 8192, 128, (1024, 8), (1, 1024)),
                in_=bass.AP(wo2, 0, [[D, 128], [128 * D, 8], [1, 1024]])),
                dma=True, ch="cstg")
            ev_id = S.op("g", lambda e: e.dma_start(
                out=id_sb[:], in_=id_h.ap()), dma=True, ch="cstg")
            ev_sel = S.op("g", lambda e: e.dma_start(
                out=sel_sb[:], in_=selh.ap()), dma=True, ch="cstg")
            ev_bsel = S.op("g", lambda e: e.dma_start(
                out=bsel_sb[:], in_=bselh.ap()), dma=True, ch="cstg")
            ev_bo = S.op("y", lambda e: e.dma_start(
                out=bo_sb[:], in_=bo2.ap()), dma=True, ch="cst")
            ev_ea = S.op("y", lambda e: e.dma_start(
                out=edg_sb[:], in_=edgea.ap()), dma=True, ch="cst")
            ev_m5 = S.op("y", lambda e: e.dma_start(
                out=m5_sb[:], in_=m5r.ap()), dma=True, ch="cst")
            ev_1 = S.op("y", lambda e: e.dma_start(
                out=one_sb[:], in_=one128.ap()), dma=True, ch="cst")
            S.op("g", lambda e: e.memset(ones_sb[:], 1.0), ch="cstg2")
            S.op("g", lambda e: e.memset(mxc[:], 0.0), ch="cstg2")
            ev_mz = S.op("g", lambda e: e.memset(mxt[:], 0.0), ch="cstg2")
            ev_cst = ("cst", S.n["cst"])
            ev_cstg = ("cstg", S.n["cstg"])
            ev_cstg2 = ("cstg2", S.n["cstg2"])
            ev_wo = ev_id = ev_sel = ev_bsel = ev_cstg
            ev_g = emit_flag_g(S, one_sb, g_sb, ev_cst)
            ev = S.op("v", lambda e: e.tensor_reduce(
                s5_sb[:], m5_sb[:], AX.X, ALU.max), after=[ev_cst])
            ev = S.op("v", lambda e: e.tensor_scalar(
                s5_sb[:], s5_sb[:], 1.0 / 32767.0, 1e-12, ALU.mult, ALU.max))
            ev_s5 = S.op("v", lambda e: e.reciprocal(r5_sb[:], s5_sb[:]))
            ev_b2 = S.op("v", lambda e: e.tensor_scalar(
                bo2x[:], bo_sb[:], 2.0, None, ALU.mult), after=[ev_cst])
            ev_bL = S.op("v", lambda e: e.tensor_scalar(
                bL_sb[:], bo_sb[:], edg_sb[:, 0:1], None, ALU.mult))
            ev_bR = S.op("v", lambda e: e.tensor_scalar(
                bR_sb[:], bo_sb[:], edg_sb[:, 1:2], None, ALU.mult))

            def stat_chain(evm, cw):
                e1 = S.op("v", lambda e: e.tensor_scalar(
                    mu_[:, 0:cw], pS1[:, 0:cw], 1.0 / D, None, ALU.mult),
                    after=[evm])
                e2 = S.op("v", lambda e: e.tensor_scalar(
                    ts1[:, 0:cw], pS2[:, 0:cw], 1.0 / D, 1e-5,
                    ALU.mult, ALU.add))
                e3 = S.op("v", lambda e: e.tensor_tensor(
                    ts2[:, 0:cw], mu_[:, 0:cw], mu_[:, 0:cw], ALU.mult))
                e4 = S.op("v", lambda e: e.tensor_tensor(
                    ts1[:, 0:cw], ts1[:, 0:cw], ts2[:, 0:cw], ALU.subtract))
                e5 = S.op("s", lambda e: e.activation(
                    ts2[:, 0:cw], ts1[:, 0:cw], AF.Sqrt), after=[e4])
                e6 = S.op("v", lambda e: e.reciprocal(
                    rb_[:, 0:cw], ts2[:, 0:cw]), after=[e5])
                return e2, e6

            def ln_stats(evc_src, ybf_t, ysq_t, cw, aft_sq, aft_mm):
                evq = S.op("s", lambda e: e.activation(
                    ysq_t[:, 0:8 * cw], ybf_t[:, 0:8 * cw], AF.Square),
                    after=aft_sq)
                evm = None
                for k in range(8):
                    aft = aft_mm if k == 0 else []
                    evm = S.op("t", lambda e, k=k: e.matmul(
                        pS1[:, 0:cw], ones_sb[:],
                        ybf_t[:, k * cw:(k + 1) * cw],
                        start=(k == 0), stop=(k == 7)), after=aft)
                for k in range(8):
                    aft = [evq] if k == 0 else []
                    evm = S.op("t", lambda e, k=k: e.matmul(
                        pS2[:, 0:cw], ones_sb[:],
                        ysq_t[:, k * cw:(k + 1) * cw],
                        start=(k == 0), stop=(k == 7)), after=aft)
                return evm

            ev_ssq = None
            ev_srd = None
            ev_evac = {0: None, 1: None}
            ev_oq = None
            ev_x2st = None
            ev_x4st = None
            ev_x4rd = None
            ev_dn_last = None
            ev_prev_ld = None
            col = 0
            for ti, cw in enumerate(TILES):
                c0 = col
                col += cw
                el = S.op("y", lambda e, c0=c0, cw=cw: e.dma_start(
                    out=ap3(u0pool, 24 * 512, 128, (cw, 24), (1, cw)),
                    in_=bass.AP(u0_t, c0, [[W, 128], [128 * W, 24], [1, cw]])),
                    after=[ev_oq, ev_dn_last], dma=True, ch="lu")
                elx = S.op("y", lambda e, c0=c0, cw=cw: e.dma_start(
                    out=ap3(xf, 8 * 512, 128, (cw, 8), (1, cw)),
                    in_=bass.AP(x1t, c0, [[W, 128], [128 * W, 8], [1, cw]])),
                    after=[ev_prev_ld], dma=True, ch="lx")
                nseg = (24 * cw + 2047) // 2048
                ev_oq = None
                for sgi in range(nseg):
                    s0 = sgi * 2048
                    s1 = min(24 * cw, s0 + 2048)
                    n = s1 - s0
                    e1 = S.op("v", lambda e, s0=s0, n=n: e.tensor_scalar(
                        tqpool[:, 0:n], u0pool[:, s0:s0 + n], r5_sb[:], None,
                        ALU.mult), after=[el, ev_s5, ev_g, ev_oq])
                    h = (n + 1) // 2
                    for qq in range(2):
                        q0, q1 = qq * h, min(n, (qq + 1) * h)
                        if q1 <= q0:
                            continue
                        m = q1 - q0
                        eva = S.op("v", lambda e, q0=q0, m=m: e.tensor_scalar(
                            ta[:, 0:m], tqpool[:, q0:q0 + m], C23, C23,
                            ALU.add, ALU.subtract))
                        evb = S.op("v", lambda e, q0=q0, m=m: e.tensor_scalar(
                            tb[:, 0:m], tqpool[:, q0:q0 + m], C52, C52,
                            ALU.add, ALU.subtract))
                        evc_ = S.op("v", lambda e, q0=q0, m=m:
                                    e.scalar_tensor_tensor(
                            tqpool[:, q0:q0 + m], ta[:, 0:m], g_sb[:],
                            tb[:, 0:m], ALU.mult, ALU.add))
                    ev_oq = S.op("v", lambda e, s0=s0, n=n: e.tensor_scalar(
                        u0pool[:, s0:s0 + n], tqpool[:, 0:n], s5_sb[:], None,
                        ALU.mult))
                ev_ev = []
                for m in range(8):
                    pbank = pA if m % 2 == 0 else pB
                    evk = None
                    for k in range(24):
                        aft = ([ev_oq, ev_wd, ev_evac[m % 2]] if k == 0
                               else [])
                        evk = S.op("t", lambda e, k=k, m=m, cw=cw,
                                   pbank=pbank: e.matmul(
                            pbank[:, 0:cw],
                            wpool[:, (k * 8 + m) * 128:(k * 8 + m + 1) * 128],
                            u0pool[:, k * cw:(k + 1) * cw],
                            start=(k == 0), stop=(k == 23)), after=aft)
                    ev_evac[m % 2] = S.op("s", lambda e, m=m, cw=cw,
                                          pbank=pbank: e.copy(
                        yb[:, m * cw:(m + 1) * cw], pbank[:, 0:cw]),
                        after=[evk])
                    ev_ev.append(ev_evac[m % 2])
                    ev_dn_last = evk
                evc = S.op("v", lambda e, cw=cw: e.tensor_copy(
                    ybf[:, 0:8 * cw], yb[:, 0:8 * cw]),
                    after=ev_ev + [ev_x4st])
                evm = ln_stats(evc, ybf, ysq, cw, [evc, ev_ssq],
                               [evc, ev_cstg2, ev_srd])
                ev_ssq = evm
                ev_srd, e6 = stat_chain(evm, cw)
                e7 = S.op("v", lambda e, cw=cw: e.tensor_tensor(
                    yb[:, 0:8 * cw], yb[:, 0:8 * cw],
                    bass.AP(mu_, 0, [[512, 128], [0, 8], [1, cw]]),
                    ALU.subtract), after=[e6])
                e8 = S.op("v", lambda e, cw=cw: e.tensor_tensor(
                    yb[:, 0:8 * cw], yb[:, 0:8 * cw],
                    bass.AP(rb_, 0, [[512, 128], [0, 8], [1, cw]]), ALU.mult))
                e9 = S.op("v", lambda e, cw=cw: e.tensor_tensor(
                    xf[:, 0:8 * cw], xf[:, 0:8 * cw], yb[:, 0:8 * cw],
                    ALU.add), after=[e8, elx])
                ev_x2st = S.op("g", lambda e, c0=c0, cw=cw: e.dma_start(
                    out=bass.AP(x2r_h, c0, [[W, 128], [128 * W, 8], [1, cw]]),
                    in_=ap3(xf, 8 * 512, 128, (cw, 8), (1, cw))),
                    after=[e9], dma=True, ch="sx2")
                evc2 = S.op("v", lambda e, cw=cw: e.tensor_copy(
                    ybf[:, 0:8 * cw], xf[:, 0:8 * cw]), after=[e9])
                evm2 = ln_stats(evc2, ybf, ysq, cw, [evc2, ev_ssq],
                                [evc2, ev_srd])
                ev_ssq = evm2
                ev_srd, e6b = stat_chain(evm2, cw)
                ea = S.op("v", lambda e, cw=cw: e.tensor_tensor(
                    xf[:, 0:8 * cw], xf[:, 0:8 * cw],
                    bass.AP(mu_, 0, [[512, 128], [0, 8], [1, cw]]),
                    ALU.subtract), after=[e6b, ev_x2st])
                eb = S.op("v", lambda e, cw=cw: e.tensor_tensor(
                    xl4[:, 0:8 * cw], xf[:, 0:8 * cw],
                    bass.AP(rb_, 0, [[512, 128], [0, 8], [1, cw]]), ALU.mult),
                    after=[ev_x4st])
                ev_prev_ld = eb
                ev_x4st = S.op("g", lambda e, c0=c0, cw=cw: e.dma_start(
                    out=bass.AP(xln4_h, c0, [[W, 128], [128 * W, 8],
                                             [1, cw]]),
                    in_=ap3(xl4, 8 * 512, 128, (cw, 8), (1, cw))),
                    after=[eb], dma=True, ch="sx4")
            ev_phase1 = [ev_x2st, ev_x4st, ev_oq, ev_prev_ld, ev_ssq, ev_srd]

            ev_wq = S.op("g", lambda e: e.dma_start(
                out=ap3(wpool, 24 * 1024, 128, (3072, 8), (1, 3072)),
                in_=bass.AP(wqkv, 0, [[3 * D, 128], [128 * 3 * D, 8],
                                      [1, 3072]])),
                after=ev_phase1 + [ev_evac[0], ev_evac[1]], dma=True,
                ch="wp")

            PIT = 3584
            PITA = 1536

            def dotsv(off, dims):
                return bass.AP(tqpool, off, [[PIT, 16]] + dims)

            ev_qkv_rd = None
            ev_accb_rd = None
            ev_x3st = None
            ev_x5st = None
            ev_mx = ev_cstg2
            ev_dots_rd = None
            ev_cf_rd = None
            ev_prod_rd = None
            ev_dlast = None
            ev_xf_rd = None
            ev_att_evac = {0: None, 1: None}
            ev_mload = {0: None, 1: None}
            for ti in range(8):
                c0 = HALO + ti * 512 - 3
                elq = S.op("y", lambda e, c0=c0: e.dma_start(
                    out=ap3(xl4b, 24 * 512, 128, (518, 8), (1, 518), off=4096),
                    in_=bass.AP(xln4_h, c0, [[W, 128], [128 * W, 8],
                                             [1, 518]])),
                    after=[ev_qkv_rd, ev_x4st, ev_oq], dma=True, ch="lq")
                evmA = S.op("y", lambda e, ti=ti: e.dma_start(
                    out=mA_sb[:], in_=bass.AP(maskA, HALO + ti * 512,
                                              [[W, 16], [1, 512]])),
                    after=[ev_mload[0]], dma=True, ch="lmA")
                evmB = S.op("y", lambda e, ti=ti: e.dma_start(
                    out=mB_sb[:], in_=bass.AP(maskB, HALO + ti * 512,
                                              [[W, 16], [1, 512]])),
                    after=[ev_mload[1]], dma=True, ch="lmB")
                ev_qmm = []
                for m in range(24):
                    pbank = pA if m % 2 == 0 else pB
                    evk = None
                    for k in range(8):
                        aft = ([elq, ev_wq, ev_att_evac[m % 2]] if k == 0
                               else [])
                        evk = S.op("t", lambda e, k=k, m=m, pbank=pbank:
                                   e.matmul(
                            pbank[:, 0:512],
                            wpool[:, (k * 24 + m) * 128:
                                     (k * 24 + m + 1) * 128],
                            bass.AP(u0pool, 4096 + k * 518,
                                    [[24 * 512, 128], [1, 512]]),
                            start=(k == 0), stop=(k == 7)), after=aft)
                    evk2 = None
                    for k in range(8):
                        aft = [evk] if k == 0 else []
                        evk2 = S.op("t", lambda e, k=k, m=m: e.matmul(
                            (pC0 if m % 2 == 0 else pC1)[:, 0:6],
                            wpool[:, (k * 24 + m) * 128:
                                     (k * 24 + m + 1) * 128],
                            bass.AP(u0pool, 4096 + k * 518 + 512,
                                    [[24 * 512, 128], [1, 6]]),
                            start=(k == 0), stop=(k == 7)), after=aft)
                    e_a = S.op("s", lambda e, m=m, pbank=pbank: e.copy(
                        qkv_sb[:, m * 518:m * 518 + 512], pbank[:, 0:512]),
                        after=[evk, ev_qkv_rd])
                    e_b = S.op("s", lambda e, m=m: e.copy(
                        qkv_sb[:, m * 518 + 512:(m + 1) * 518],
                        (pC0 if m % 2 == 0 else pC1)[:, 0:6]),
                        after=[evk2])
                    ev_att_evac[m % 2] = e_b
                    ev_qmm.append(e_b)
                ev_d7 = []
                for dlt in range(-3, 4):
                    di = dlt + 3
                    dbank = pD0
                    aftp = ev_qmm + ([ev_prod_rd] if ev_prod_rd else [])
                    ep = S.op("v", lambda e, dlt=dlt: e.tensor_tensor(
                        prod[:, 0:4096],
                        bass.AP(qkv_sb, 3, [[24 * 518, 128], [518, 8],
                                            [1, 512]]),
                        bass.AP(qkv_sb, 8 * 518 + 3 + dlt,
                                [[24 * 518, 128], [518, 8], [1, 512]]),
                        ALU.mult), after=aftp)
                    evr = None
                    for ci in range(8):
                        aft = ([ep, ev_sel, ev_dots_rd, ev_dlast]
                               if ci == 0 else [])
                        evr = S.op("t", lambda e, ci=ci, dbank=dbank:
                                   e.matmul(
                            dbank[0:16, 0:512],
                            sel_sb[:, 16 * ci:16 * (ci + 1)],
                            prod[:, ci * 512:(ci + 1) * 512],
                            start=(ci == 0), stop=(ci == 7)), after=aft)
                    ev_prod_rd = evr
                    edv = S.op("s", lambda e, di=di, dbank=dbank:
                               e.activation(
                        dotsv(di * 512, [[1, 512]]), dbank[0:16, 0:512],
                        AF.Identity, scale=0.03125), after=[evr, ev_cf_rd])
                    ev_dlast = edv
                    ev_d7.append(edv)
                ev_dots_rd = ev_d7[-1]
                ez = S.op("v", lambda e: e.memset(cf_[:], 0.0),
                          after=ev_d7 + ([ev_cf_rd] if ev_cf_rd else []))
                ev_cfw = ez
                for par in range(2):
                    for slot in range(2):
                        start = (3 - par) if slot == 0 else (1 - par)
                        msk = mA_sb if slot == 0 else mB_sb
                        evmsk = evmA if slot == 0 else evmB
                        doff = start * 512 + par
                        e1 = S.op("v", lambda e, doff=doff: e.tensor_reduce(
                            bass.AP(ta, 1024, [[PITA, 16], [1, 256]]),
                            dotsv(doff, [[2, 256], [512, 4]]),
                            AX.X, ALU.max), after=[ev_cfw, ev_dots_rd])
                        e2 = S.op("v", lambda e, doff=doff: e.tensor_tensor(
                            bass.AP(ta, 0, [[PITA, 16], [1, 1024]]),
                            dotsv(doff, [[2, 256], [512, 4]]),
                            bass.AP(ta, 1024, [[PITA, 16], [1, 256], [0, 4]]),
                            ALU.subtract))
                        e3 = S.op("s", lambda e: e.activation(
                            bass.AP(ta, 0, [[PITA, 16], [1, 1024]]),
                            bass.AP(ta, 0, [[PITA, 16], [1, 1024]]),
                            AF.Exp), after=[e2])
                        e4 = S.op("v", lambda e: e.tensor_reduce(
                            bass.AP(tb, 0, [[PITA, 16], [1, 256]]),
                            bass.AP(ta, 0, [[PITA, 16], [4, 256], [1, 4]]),
                            AX.X, ALU.add), after=[e3])
                        e5 = S.op("v", lambda e: e.reciprocal(
                            bass.AP(tb, 256, [[PITA, 16], [1, 256]]),
                            bass.AP(tb, 0, [[PITA, 16], [1, 256]])))
                        e6 = S.op("v", lambda e, msk=msk, par=par:
                                  e.tensor_tensor(
                            bass.AP(tb, 256, [[PITA, 16], [1, 256]]),
                            bass.AP(tb, 256, [[PITA, 16], [1, 256]]),
                            bass.AP(msk, par, [[512, 16], [2, 256]]),
                            ALU.mult), after=[e5, evmsk])
                        e7 = S.op("v", lambda e: e.tensor_tensor(
                            bass.AP(ta, 0, [[PITA, 16], [4, 256], [1, 4]]),
                            bass.AP(ta, 0, [[PITA, 16], [4, 256], [1, 4]]),
                            bass.AP(tb, 256, [[PITA, 16], [1, 256], [0, 4]]),
                            ALU.mult), after=[e6])
                        ev_cfw = S.op("v", lambda e, doff=doff:
                                      e.tensor_tensor(
                            bass.AP(cf_, doff, [[7 * 512, 16], [2, 256],
                                                [512, 4]]),
                            bass.AP(cf_, doff, [[7 * 512, 16], [2, 256],
                                                [512, 4]]),
                            bass.AP(ta, 0, [[PITA, 16], [4, 256], [1, 4]]),
                            ALU.add), after=[e7])
                ev_cf_rd = ev_cfw
                ezz = S.op("v", lambda e: e.memset(acc[:], 0.0),
                           after=[ev_accb_rd, ev_cfw])
                ev_ap = ezz
                for di in range(7):
                    for ci in range(8):
                        cbank = pC0 if (di * 8 + ci) % 2 == 0 else pC1
                        eb1 = S.op("t", lambda e, di=di, ci=ci, cbank=cbank:
                                   e.matmul(
                            cbank[:, 0:512],
                            bsel_sb[:, 128 * ci:128 * (ci + 1)],
                            cf_[:, di * 512:(di + 1) * 512],
                            start=True, stop=True),
                            after=[ev_cfw, ev_bsel, ev_ap])
                        eb2 = S.op("v", lambda e, di=di, ci=ci, cbank=cbank:
                                   e.tensor_tensor(
                            tmpv[:], cbank[:, 0:512],
                            bass.AP(qkv_sb, 16 * 518 + ci * 518 + di,
                                    [[24 * 518, 128], [1, 512]]),
                            ALU.mult), after=[eb1])
                        ev_ap = S.op("v", lambda e, ci=ci: e.tensor_tensor(
                            acc[:, ci * 512:(ci + 1) * 512],
                            acc[:, ci * 512:(ci + 1) * 512], tmpv[:],
                            ALU.add))
                ev_qkv_rd = ev_ap
                ecb = S.op("v", lambda e: e.tensor_copy(accb[:], acc[:]),
                           after=[ev_ap, ev_oq])
                ev_oq = ecb
                ev_accb_rd = None
                ev_ev2 = []
                for m in range(8):
                    pbank = pA if m % 2 == 0 else pB
                    evk = None
                    for k in range(8):
                        aft = ([ecb, ev_wo, ev_att_evac[m % 2]] if k == 0
                               else [])
                        evk = S.op("t", lambda e, k=k, m=m, pbank=pbank:
                                   e.matmul(
                            pbank[:, 0:512],
                            wo_sb[:, (k * 8 + m) * 128:(k * 8 + m + 1) * 128],
                            accb[:, k * 512:(k + 1) * 512],
                            start=(k == 0), stop=(k == 7)), after=aft)
                    ev_att_evac[m % 2] = S.op("s", lambda e, m=m,
                                              pbank=pbank: e.activation(
                        yb[:, m * 512:(m + 1) * 512], pbank[:, 0:512],
                        AF.Identity, bias=bo2x[:, m:m + 1]),
                        after=[evk, ev_b2, ev_qkv_rd])
                    ev_ev2.append(ev_att_evac[m % 2])
                ev_accb_rd = evk
                ev_fix = list(ev_ev2)
                if ti == 0:
                    f1 = S.op("v", lambda e: e.tensor_tensor(
                        bass.AP(yb, 0, [[4096, 128], [512, 8], [1, 1]]),
                        bass.AP(yb, 0, [[4096, 128], [512, 8], [1, 1]]),
                        bass.AP(bL_sb, 0, [[8, 128], [1, 8], [0, 1]]),
                        ALU.subtract), after=ev_ev2 + [ev_bL])
                    f2 = S.op("v", lambda e: e.tensor_tensor(
                        bass.AP(yb, 1, [[4096, 128], [512, 8], [1, 1]]),
                        bass.AP(yb, 1, [[4096, 128], [512, 8], [1, 1]]),
                        bass.AP(bL_sb, 0, [[8, 128], [1, 8], [0, 1]]),
                        ALU.subtract))
                    ev_fix = [f2]
                if ti == 7:
                    f1 = S.op("v", lambda e: e.tensor_tensor(
                        bass.AP(yb, 510, [[4096, 128], [512, 8], [1, 1]]),
                        bass.AP(yb, 510, [[4096, 128], [512, 8], [1, 1]]),
                        bass.AP(bR_sb, 0, [[8, 128], [1, 8], [0, 1]]),
                        ALU.subtract), after=ev_ev2 + [ev_bR])
                    f2 = S.op("v", lambda e: e.tensor_tensor(
                        bass.AP(yb, 511, [[4096, 128], [512, 8], [1, 1]]),
                        bass.AP(yb, 511, [[4096, 128], [512, 8], [1, 1]]),
                        bass.AP(bR_sb, 0, [[8, 128], [1, 8], [0, 1]]),
                        ALU.subtract))
                    ev_fix = [f2]
                elx2 = S.op("y", lambda e, ti=ti: e.dma_start(
                    out=ap3(xf, 8 * 512, 128, (512, 8), (1, 512)),
                    in_=bass.AP(x2r_h, HALO + ti * 512,
                                [[W, 128], [128 * W, 8], [1, 512]])),
                    after=[ev_x3st, ev_x2st, ev_prev_ld, ev_xf_rd],
                    dma=True, ch="lx")
                evc = S.op("v", lambda e: e.tensor_copy(
                    ybf[:], yb[:]), after=ev_fix + [ev_x4st, ev_x5st])
                evm = ln_stats(evc, ybf, ysq, 512, [evc, ev_ssq],
                               [evc, ev_srd])
                ev_ssq = evm
                ev_srd, e6 = stat_chain(evm, 512)
                e7 = S.op("v", lambda e: e.tensor_tensor(
                    yb[:], yb[:], bass.AP(mu_, 0, [[512, 128], [0, 8],
                                                   [1, 512]]),
                    ALU.subtract), after=[e6])
                e8 = S.op("v", lambda e: e.tensor_tensor(
                    yb[:], yb[:], bass.AP(rb_, 0, [[512, 128], [0, 8],
                                                   [1, 512]]), ALU.mult))
                e9 = S.op("v", lambda e: e.tensor_tensor(
                    xf[:], xf[:], yb[:], ALU.add), after=[e8, elx2])
                ev_x3st = S.op("g", lambda e, ti=ti: e.dma_start(
                    out=bass.AP(x3_out, ti * 512, [[TPC, 128], [128 * TPC, 8],
                                                   [1, 512]]),
                    in_=ap3(xf, 8 * 512, 128, (512, 8), (1, 512))),
                    after=[e9], dma=True, ch="sx3")
                evc2 = S.op("v", lambda e: e.tensor_copy(ybf[:], xf[:]),
                            after=[e9])
                evm2 = ln_stats(evc2, ybf, ysq, 512, [evc2, ev_ssq],
                                [evc2, ev_srd])
                ev_ssq = evm2
                ev_srd, e6b = stat_chain(evm2, 512)
                ea2 = S.op("v", lambda e: e.tensor_tensor(
                    xf[:], xf[:], bass.AP(mu_, 0, [[512, 128], [0, 8],
                                                   [1, 512]]),
                    ALU.subtract), after=[e6b, ev_x3st])
                eb2 = S.op("v", lambda e: e.tensor_tensor(
                    xl4[:], xf[:], bass.AP(rb_, 0, [[512, 128], [0, 8],
                                                    [1, 512]]), ALU.mult),
                    after=[ev_x5st])
                ev_xf_rd = eb2
                em1 = S.op("v", lambda e: e.tensor_reduce(
                    mtmp[:], bass.AP(xl4, 0, [[8 * 512, 128], [1, 8 * 512]]),
                    AX.X, ALU.max, apply_absolute_value=True),
                    after=[eb2, ev_mx])
                ev_mx = S.op("v", lambda e: e.tensor_tensor(
                    mxc[:], mxc[:], mtmp[:], ALU.max))
                ev_x5st = S.op("g", lambda e, ti=ti: e.dma_start(
                    out=bass.AP(xln5_out, ti * 512,
                                [[TPC, 128], [128 * TPC, 8], [1, 512]]),
                    in_=ap3(xl4, 8 * 512, 128, (512, 8), (1, 512))),
                    after=[eb2], dma=True, ch="sx5")
            ev1 = S.op("v", lambda e: e.tensor_copy(
                bass.AP(mxt, 0, [[128, 128], [1, 1]]), mxc[:]),
                after=[ev_mx, ev_mz])
            ev2 = S.op("t", lambda e: e.transpose(
                pDT[:], mxt[:], id_sb[:]), after=[ev1, ev_id, ev_dots_rd,
                                                  ev_prod_rd])
            ev3 = S.op("v", lambda e: e.tensor_reduce(
                msc[:], pDT[0:1, :], AX.X, ALU.max,
                apply_absolute_value=True), after=[ev2])
            S.op("g", lambda e: e.dma_start(
                out=mi_out.ap(), in_=msc[:]), after=[ev3], dma=True, ch="mo")
            S.build()
    return nc



def build_H():
    nc = bass.Bass("TRN2", target_bir_lowering=False)
    xlt = nc.dram_tensor("xlt", [D, TPC], BF16, kind="ExternalInput")
    x3t = nc.dram_tensor("x3t", [D, TPC], F32, kind="ExternalInput")
    mir = nc.dram_tensor("mir", [128, 8], F32, kind="ExternalInput")
    l1w = nc.dram_tensor("l1w", [D, D], F32, kind="ExternalInput")
    l1b = nc.dram_tensor("l1b", [128, 8], F32, kind="ExternalInput")
    one128 = nc.dram_tensor("one128", [128, 1], F32, kind="ExternalInput")
    out_t = nc.dram_tensor("out_t", [D, TPC], F32, kind="ExternalOutput")
    scr = nc.dram_tensor("scr", [128, 1], F32)
    scr2 = nc.dram_tensor("scr2", [1, 1], F32)

    es = contextlib.ExitStack()
    with es:
        S = Sched(nc)
        wf = alloc(nc, es, "wf", [128, 8 * 8 * 128], F32)
        wt = alloc(nc, es, "wt", [128, 8 * 8 * 128], BF16)
        ones_sb = alloc(nc, es, "ones", [128, 128], BF16)
        one_sb = alloc(nc, es, "one", [128, 1], F32)
        g_sb = alloc(nc, es, "gflag", [128, 1], F32)
        mi_sb = alloc(nc, es, "mis", [128, 8], F32)
        si_sb = alloc(nc, es, "sis", [128, 1], F32)
        ri_sb = alloc(nc, es, "ris", [128, 1], F32)
        bias_sb = alloc(nc, es, "bias", [128, 8], F32)
        wsum = alloc(nc, es, "wsum", [128, 1], F32)
        wrow = alloc(nc, es, "wrow", [1, 128], F32)
        wsc1 = alloc(nc, es, "wsc1", [1, 1], F32)
        wscb = alloc(nc, es, "wscb", [128, 1], F32)
        thr = alloc(nc, es, "thr", [128, 1], F32)
        sc_ = alloc(nc, es, "sc_", [128, 1], F32)
        absw = alloc(nc, es, "absw", [128, 2048], F32)
        sgnw = alloc(nc, es, "sgnw", [128, 2048], F32)
        xlb = [alloc(nc, es, f"xlb{i}", [128, 8 * 512], BF16) for i in range(2)]
        xf = [alloc(nc, es, f"xf{i}", [128, 8 * 512], F32) for i in range(2)]
        tq = alloc(nc, es, "tq", [128, 8 * 512], F32)
        ta = alloc(nc, es, "ta", [128, 2048], F32)
        tb = alloc(nc, es, "tb", [128, 2048], F32)
        xq = alloc(nc, es, "xq", [128, 8 * 512], BF16)
        yb = alloc(nc, es, "yb", [128, 8 * 512], F32)
        ybf = alloc(nc, es, "ybf", [128, 8 * 512], BF16)
        ysq = alloc(nc, es, "ysq", [128, 8 * 512], BF16)
        mu_ = alloc(nc, es, "mu_", [128, 512], F32)
        rb_ = alloc(nc, es, "rb_", [128, 512], F32)
        ts1 = alloc(nc, es, "ts1", [128, 512], F32)
        ts2 = alloc(nc, es, "ts2", [128, 512], F32)

        with (nc.psum_tensor("pA", [128, 512], F32) as pA,
              nc.psum_tensor("pB", [128, 512], F32) as pB,
              nc.psum_tensor("pS1", [128, 512], F32) as pS1,
              nc.psum_tensor("pS2", [128, 512], F32) as pS2):
            ev_w = S.op("g", lambda e: e.dma_start(
                out=ap3(wf, 8192, 128, (1024, 8), (1, 1024)),
                in_=bass.AP(l1w, 0, [[D, 128], [128 * D, 8], [1, 1024]])),
                dma=True, ch="cstg")
            ev_on = S.op("g", lambda e: e.memset(ones_sb[:], 1.0),
                         ch="cstg2")
            ev_mi = S.op("y", lambda e: e.dma_start(
                out=mi_sb[:], in_=mir.ap()), dma=True, ch="cst")
            ev_b = S.op("y", lambda e: e.dma_start(
                out=bias_sb[:], in_=l1b.ap()), dma=True, ch="cst")
            ev_1 = S.op("y", lambda e: e.dma_start(
                out=one_sb[:], in_=one128.ap()), dma=True, ch="cst")
            ev_cst = ("cst", S.n["cst"])
            ev_cstg = ("cstg", S.n["cstg"])
            ev_cstg2 = ("cstg2", S.n["cstg2"])
            ev_g = emit_flag_g(S, one_sb, g_sb, ev_cst)
            ev = S.op("v", lambda e: e.tensor_reduce(
                wsum[:], wf[:], AX.X, ALU.add, apply_absolute_value=True),
                after=[ev_w])
            ev = S.op("g", lambda e: e.dma_start(
                out=scr.ap(), in_=wsum[:]), after=[ev], dma=True, ch="ws")
            ev = S.op("y", lambda e: e.dma_start(
                out=wrow[:], in_=bass.AP(scr, 0, [[128, 1], [1, 128]])),
                after=[ev], dma=True, ch="ws2")
            ev = S.op("v", lambda e: e.tensor_reduce(
                wsc1[:], wrow[:], AX.X, ALU.add), after=[ev])
            ev = S.op("v", lambda e: e.tensor_scalar(
                wsc1[:], wsc1[:], 1.0 / (1024.0 * 1024.0), None, ALU.mult))
            ev = S.op("g", lambda e: e.dma_start(
                out=scr2.ap(), in_=wsc1[:]), after=[ev], dma=True, ch="ws")
            ev_wsc = S.op("y", lambda e: e.dma_start(
                out=wscb[:], in_=bass.AP(scr2, 0, [[0, 128], [1, 1]])),
                after=[ev], dma=True, ch="ws2")
            ev_thr = S.op("v", lambda e: e.tensor_scalar(
                thr[:], wscb[:], 0.5, None, ALU.mult), after=[ev_wsc])
            ev_tq = None
            for hh in range(4):
                sl = slice(hh * 2048, (hh + 1) * 2048)
                e1 = S.op("s", lambda e, sl=sl: e.activation(
                    absw[:], wf[:, sl], AF.Abs), after=[ev_w, ev_tq])
                e2 = S.op("s", lambda e, sl=sl: e.activation(
                    sgnw[:], wf[:, sl], AF.Sign), after=[e1])
                e3 = S.op("v", lambda e: e.tensor_scalar(
                    absw[:], absw[:], thr[:], None, ALU.is_gt),
                    after=[e2, ev_thr])
                ev_tq = S.op("v", lambda e, sl=sl: e.tensor_tensor(
                    wt[:, sl], absw[:], sgnw[:], ALU.mult))
            ev = S.op("v", lambda e: e.tensor_reduce(
                si_sb[:], mi_sb[:], AX.X, ALU.max), after=[ev_cst])
            ev = S.op("v", lambda e: e.tensor_scalar(
                si_sb[:], si_sb[:], 1.0 / 127.0, None, ALU.mult))
            ev_si = S.op("v", lambda e: e.reciprocal(ri_sb[:], si_sb[:]))
            ev_sc = S.op("v", lambda e: e.tensor_tensor(
                sc_[:], wscb[:], si_sb[:], ALU.mult), after=[ev_wsc, ev_si])

            ev_use = {0: None, 1: None}
            ev_evac = {0: None, 1: None}
            ev_ost = {0: None, 1: None}
            ev_ssq = None
            ev_srd = None
            for ti in range(8):
                bi = ti % 2
                c0 = ti * 512
                el = S.op("y", lambda e, c0=c0, bi=bi: e.dma_start(
                    out=ap3(xlb[bi], 8 * 512, 128, (512, 8), (1, 512)),
                    in_=bass.AP(xlt, c0, [[TPC, 128], [128 * TPC, 8],
                                          [1, 512]])),
                    after=[ev_use[bi]], dma=True, ch=f"lx{bi}")
                elx = S.op("y", lambda e, c0=c0, bi=bi: e.dma_start(
                    out=ap3(xf[bi], 8 * 512, 128, (512, 8), (1, 512)),
                    in_=bass.AP(x3t, c0, [[TPC, 128], [128 * TPC, 8],
                                          [1, 512]])),
                    after=[ev_ost[bi]], dma=True, ch=f"l3{bi}")
                e1 = S.op("v", lambda e, bi=bi: e.tensor_scalar(
                    tq[:], xlb[bi][:], ri_sb[:], None, ALU.mult),
                    after=[el, ev_si, ev_g])
                e2 = e1
                for hh in range(2):
                    sl = slice(hh * 2048, (hh + 1) * 2048)
                    e2 = emit_round(S, tq[:, sl], tq[:, sl], g_sb, ta[:],
                                    tb[:], after=[e2])
                e3 = S.op("v", lambda e: e.tensor_scalar(
                    xq[:], tq[:], 127.0, -128.0, ALU.min, ALU.max),
                    after=[e2])
                ev_use[bi] = e3
                ev_ev = []
                for m in range(8):
                    pbank = pA if m % 2 == 0 else pB
                    evk = None
                    for k in range(8):
                        aft = ([e3, ev_tq, ev_evac[m % 2]] if k == 0 else [])
                        evk = S.op("t", lambda e, k=k, m=m, pbank=pbank:
                                   e.matmul(
                            pbank[:, 0:512],
                            wt[:, (k * 8 + m) * 128:(k * 8 + m + 1) * 128],
                            xq[:, k * 512:(k + 1) * 512],
                            start=(k == 0), stop=(k == 7)), after=aft)
                    ev_evac[m % 2] = S.op("s", lambda e, m=m, pbank=pbank:
                                          e.activation(
                        yb[:, m * 512:(m + 1) * 512], pbank[:, 0:512],
                        AF.Identity, bias=bias_sb[:, m:m + 1],
                        scale=sc_[:]), after=[evk, ev_sc, ev_b])
                    ev_ev.append(ev_evac[m % 2])
                evc = S.op("v", lambda e: e.tensor_copy(ybf[:], yb[:]),
                           after=ev_ev)
                evq = S.op("s", lambda e: e.activation(
                    ysq[:], ybf[:], AF.Square), after=[evc, ev_ssq])
                evm = None
                for k in range(8):
                    aft = [evc, ev_cstg2, ev_srd] if k == 0 else []
                    evm = S.op("t", lambda e, k=k: e.matmul(
                        pS1[:, 0:512], ones_sb[:],
                        ybf[:, k * 512:(k + 1) * 512],
                        start=(k == 0), stop=(k == 7)), after=aft)
                for k in range(8):
                    aft = [evq] if k == 0 else []
                    evm = S.op("t", lambda e, k=k: e.matmul(
                        pS2[:, 0:512], ones_sb[:],
                        ysq[:, k * 512:(k + 1) * 512],
                        start=(k == 0), stop=(k == 7)), after=aft)
                ev_ssq = evm
                e1 = S.op("v", lambda e: e.tensor_scalar(
                    mu_[:], pS1[:, 0:512], 1.0 / D, None, ALU.mult),
                    after=[evm])
                e2 = S.op("v", lambda e: e.tensor_scalar(
                    ts1[:], pS2[:, 0:512], 1.0 / D, 1e-5, ALU.mult, ALU.add))
                e3 = S.op("v", lambda e: e.tensor_tensor(
                    ts2[:], mu_[:], mu_[:], ALU.mult))
                e4 = S.op("v", lambda e: e.tensor_tensor(
                    ts1[:], ts1[:], ts2[:], ALU.subtract))
                ev_srd = e2
                e5 = S.op("s", lambda e: e.activation(
                    ts2[:], ts1[:], AF.Sqrt), after=[e4])
                e6 = S.op("v", lambda e: e.reciprocal(rb_[:], ts2[:]),
                          after=[e5])
                e7 = S.op("v", lambda e: e.tensor_tensor(
                    yb[:], yb[:], bass.AP(mu_, 0, [[512, 128], [0, 8],
                                                   [1, 512]]),
                    ALU.subtract), after=[e6])
                e8 = S.op("v", lambda e: e.tensor_tensor(
                    yb[:], yb[:], bass.AP(rb_, 0, [[512, 128], [0, 8],
                                                   [1, 512]]), ALU.mult))
                e9 = S.op("v", lambda e, bi=bi: e.tensor_tensor(
                    xf[bi][:], xf[bi][:], yb[:], ALU.add), after=[e8, elx])
                ev_ost[bi] = S.op("g", lambda e, c0=c0, bi=bi: e.dma_start(
                    out=bass.AP(out_t, c0, [[TPC, 128], [128 * TPC, 8],
                                            [1, 512]]),
                    in_=ap3(xf[bi], 8 * 512, 128, (512, 8), (1, 512))),
                    after=[e9], dma=True, ch=f"so{bi}")
            S.build()
    return nc



def stage_x(x):
    xs = []
    xT = np.ascontiguousarray(np.transpose(x, (0, 2, 1)))
    for c in range(NC):
        b, t0 = c // 4, (c % 4) * TPC
        buf = np.zeros((D, W), np.float32)
        lo, hi = t0 - HALO, t0 + TPC + HALO
        slo, shi = max(lo, 0), min(hi, L)
        buf[:, slo - lo:W - (hi - shi)] = xT[b, :, slo:shi]
        xs.append(buf)
    return xs


def in_maps_A(inputs):
    x = np.asarray(inputs["L0_attn_x"], np.float32)
    win = np.asarray(inputs["L0_attn_in_proj_w"], np.float32)
    binp = np.asarray(inputs["L0_attn_in_proj_b"], np.float32)
    cidx = np.arange(D)
    winv = np.ascontiguousarray(win[:, 3 * cidx])
    winx2 = np.ascontiguousarray(win[:, 3 * cidx + 2])
    def perm8(a):
        a = a.reshape(8, 128, -1)
        return np.ascontiguousarray(a.transpose(1, 0, 2).reshape(128, -1))
    bv = perm8(binp[3 * cidx].reshape(D, 1))
    bx2 = perm8(binp[3 * cidx + 2].reshape(D, 1))
    swc = perm8(np.ascontiguousarray(
        np.asarray(inputs["L0_attn_short_w"], np.float32)[:, 0, :]))
    sbc = perm8(np.asarray(inputs["L0_attn_short_b"], np.float32).reshape(D, 1))
    wg = np.ascontiguousarray(np.asarray(inputs["L0_ffn_w_gate"], np.float32))
    wu = np.ascontiguousarray(np.asarray(inputs["L0_ffn_w_up"], np.float32))
    wd = np.ascontiguousarray(np.asarray(inputs["L0_ffn_w_down"], np.float32))
    one = np.ones((128, 1), np.float32)
    xs = stage_x(x)
    maps = []
    for c in range(NC):
        t0 = (c % 4) * TPC
        edges = np.zeros((128, 16), np.float32)
        edges[:, 0::2] = 1.0 if t0 > 0 else 0.0
        edges[:, 1::2] = 1.0 if t0 + TPC < L else 0.0
        maps.append({
            "xt": xs[c], "winv": winv, "winx2": winx2, "bv": bv, "bx2": bx2,
            "swc": swc, "sbc": sbc, "edges": edges, "one128": one,
            "wg_s": np.ascontiguousarray(wg[c * (IH // NC):(c + 1) * (IH // NC)]),
            "wu_s": np.ascontiguousarray(wu[c * (IH // NC):(c + 1) * (IH // NC)]),
            "wd_s": np.ascontiguousarray(wd[c * (D // NC):(c + 1) * (D // NC)]),
        })
    return maps


def in_maps_C(inputs, a_out, b_out):
    x = np.asarray(inputs["L0_attn_x"], np.float32)
    xs = stage_x(x)
    wo = np.ascontiguousarray(np.asarray(inputs["L0_attn_out_w"], np.float32))
    def perm8(a):
        a = a.reshape(8, 128, -1)
        return np.ascontiguousarray(a.transpose(1, 0, 2).reshape(128, -1))
    bo = perm8(np.asarray(inputs["L0_attn_out_b"], np.float32).reshape(D, 1))
    idm = np.eye(128, dtype=np.float32)
    maps = []
    for c in range(NC):
        b_, t0 = c // 4, (c % 4) * TPC
        cvt = np.zeros((D, W), ml_dtypes.bfloat16)
        lo, hi = t0 - HALO, t0 + TPC + HALO
        slo, shi = max(lo, 0), min(hi, L)
        for s in range(NC):
            cvt[s * 128:(s + 1) * 128, slo - lo:W - (hi - shi)] = \
                b_out[s]["conv_out"][:, b_ * L + slo:b_ * L + shi]
        maps.append({"xt": xs[c], "x2t": a_out[c]["x2_out"], "cvt": cvt,
                     "wout": wo, "bout": bo, "id_h": idm})
    return maps


def _repl(vals):
    v = np.asarray(vals, np.float32).reshape(1, 8)
    return np.ascontiguousarray(np.broadcast_to(v, (128, 8)))


def in_maps_D(inputs, a_out, c_out):
    wgq = np.concatenate([a_out[c]["wgq"] for c in range(NC)], axis=0)
    wuq = np.concatenate([a_out[c]["wuq"] for c in range(NC)], axis=0)
    wgT = np.ascontiguousarray(wgq.T)
    wuT = np.ascontiguousarray(wuq.T)
    m1 = _repl([c_out[c]["m1_out"][0, 0] for c in range(NC)])
    one = np.ones((128, 1), np.float32)
    idm = np.eye(128, dtype=np.float32)
    return [{"xlt": c_out[c]["xln2_out"], "m1r": m1, "wgT": wgT, "wuT": wuT,
             "one128": one, "id_h": idm} for c in range(NC)]


def in_maps_E(inputs, d_out):
    mg = _repl([d_out[c]["mg_out"][0, 0] for c in range(NC)])
    mu = _repl([d_out[c]["mu_out"][0, 0] for c in range(NC)])
    one = np.ones((128, 1), np.float32)
    idm = np.eye(128, dtype=np.float32)
    return [{"g_t": d_out[c]["g_out"], "u_t": d_out[c]["u_out"], "mgr": mg,
             "mur": mu, "one128": one, "id_h": idm} for c in range(NC)]


def in_maps_F(inputs, d_out, e_out):
    m4 = _repl([e_out[c]["m4_out"][0, 0] for c in range(NC)])
    mg = _repl([d_out[c]["mg_out"][0, 0] for c in range(NC)])
    one = np.ones((128, 1), np.float32)
    idm = np.eye(128, dtype=np.float32)
    return [{"t0_t": e_out[c]["t0_out"], "uq_t": e_out[c]["uq_out"],
             "m4r": m4, "mgr": mg, "one128": one, "id_h": idm}
            for c in range(NC)]


def in_maps_G(inputs, a_out, c_out, e_out, f_out):
    wdq = np.concatenate([a_out[c]["wdq"] for c in range(NC)], axis=0)
    wdT = np.ascontiguousarray(wdq.T)
    m5 = _repl([f_out[c]["m5_out"][0, 0] for c in range(NC)])
    wqkv = np.ascontiguousarray(np.asarray(inputs["L1_attn_w_qkv"],
                                           np.float32))
    wo2 = np.ascontiguousarray(np.asarray(inputs["L1_attn_w_out"],
                                          np.float32))
    def perm8(a):
        a = a.reshape(8, 128, -1)
        return np.ascontiguousarray(a.transpose(1, 0, 2).reshape(128, -1))
    bo2 = perm8(np.asarray(inputs["L1_attn_b_out"], np.float32).reshape(D, 1))
    one = np.ones((128, 1), np.float32)
    idm = np.eye(128, dtype=np.float32)
    selh = np.zeros((128, 128), np.float32)
    for ci in range(8):
        for p in range(128):
            selh[p, 16 * ci + 2 * ci + p // 64] = 1.0
    bselh = np.zeros((16, 1024), np.float32)
    for ci in range(8):
        for p in range(128):
            bselh[2 * ci + p // 64, 128 * ci + p] = 1.0
    maps = []
    for c in range(NC):
        b_, t0 = c // 4, (c % 4) * TPC
        mA = np.ones((16, W), np.float32)
        mB = np.ones((16, W), np.float32)
        for j in range(W):
            t = t0 + j - HALO
            if t >= L - 2:
                mA[:, j] = 0.0
            if t < 2:
                mB[:, j] = 0.0
        edgea = np.zeros((128, 2), np.float32)
        edgea[:, 0] = 1.0 if t0 == 0 else 0.0
        edgea[:, 1] = 1.0 if t0 + TPC == L else 0.0
        maps.append({
            "u0_t": f_out[c]["u0_out"], "x1t": c_out[c]["x1_out"],
            "m5r": m5, "wdT": wdT, "wqkv": wqkv, "wo2": wo2, "bo2": bo2,
            "selh": selh, "bselh": bselh,
            "maskA": mA.astype(ml_dtypes.bfloat16),
            "maskB": mB.astype(ml_dtypes.bfloat16),
            "edgea": edgea, "one128": one, "id_h": idm})
    return maps


def in_maps_H(inputs, g_out):
    mi = _repl([g_out[c]["mi_out"][0, 0] for c in range(NC)])
    l1w = np.ascontiguousarray(np.asarray(inputs["L1_ffn_weight"],
                                          np.float32).T)
    def perm8(a):
        a = a.reshape(8, 128, -1)
        return np.ascontiguousarray(a.transpose(1, 0, 2).reshape(128, -1))
    l1b = perm8(np.asarray(inputs["L1_ffn_bias"], np.float32).reshape(D, 1))
    one = np.ones((128, 1), np.float32)
    return [{"xlt": g_out[c]["xln5_out"], "x3t": g_out[c]["x3_out"],
             "mir": mi, "l1w": l1w, "l1b": l1b, "one128": one}
            for c in range(NC)]


_NCS = {}


def _get(name, builder):
    if name not in _NCS:
        _NCS[name] = builder()
    return _NCS[name]


def kernel(**inputs):
    import ml_dtypes
    cores = list(range(NC))
    ncA = _get("A", build_A)
    a_out = run_bass_kernel_spmd(ncA, in_maps_A(inputs), core_ids=cores).results
    ncB = _get("B", build_B)
    v_relay = [a_out[c]["v_out"] for c in range(NC)]
    b_out = run_bass_kernel_spmd(ncB, in_maps_B(inputs, v_relay),
                                 core_ids=cores).results
    ncC = _get("C", build_C)
    c_out = run_bass_kernel_spmd(ncC, in_maps_C(inputs, a_out, b_out),
                                 core_ids=cores).results
    ncD = _get("D", build_D)
    d_out = run_bass_kernel_spmd(ncD, in_maps_D(inputs, a_out, c_out),
                                 core_ids=cores).results
    ncE = _get("E", build_E)
    e_out = run_bass_kernel_spmd(ncE, in_maps_E(inputs, d_out),
                                 core_ids=cores).results
    ncF = _get("F", build_F)
    f_out = run_bass_kernel_spmd(ncF, in_maps_F(inputs, d_out, e_out),
                                 core_ids=cores).results
    ncG = _get("G", build_G)
    g_out = run_bass_kernel_spmd(ncG, in_maps_G(inputs, a_out, c_out, e_out,
                                                f_out),
                                 core_ids=cores).results
    ncH = _get("H", build_H)
    h_out = run_bass_kernel_spmd(ncH, in_maps_H(inputs, g_out),
                                 core_ids=cores).results
    out = np.empty((2, L, D), np.float32)
    for c in range(NC):
        b_, t0 = c // 4, (c % 4) * TPC
        out[b_, t0:t0 + TPC, :] = np.asarray(h_out[c]["out_t"],
                                             np.float32).T
    return out


# version 04 (final)
# speedup vs baseline: 1.0029x; 1.0029x over previous
import contextlib
import math
import numpy as np
import ml_dtypes

import concourse.bass as bass
import concourse.mybir as mybir
from concourse.bass_utils import run_bass_kernel_spmd

F32 = mybir.dt.float32
BF16 = mybir.dt.bfloat16
AF = mybir.ActivationFunctionType
ALU = mybir.AluOpType
AX = mybir.AxisListType

NC = 8
D = 1024
L = 16384
TPC = 4096
HALO = 32
W = TPC + 2 * HALO
TILES = [512] * 8 + [64]
C23 = 1.5 * 2.0**23
C52 = 1.5 * 2.0**52
EPS_FLAG = 2.0**-40
IH = 3072


def split_const(x):
    hi = np.float32(x)
    lo = np.float32(np.float64(x) - np.float64(hi))
    return float(hi), float(lo)



class Sched:
    ENGS = ("y", "g", "v", "s", "t")

    def __init__(self, nc):
        self.nc = nc
        self.ops = {e: [] for e in self.ENGS}
        self.n = {}
        self.owner = {}

    def op(self, e, fn, after=(), dma=False, ch=None, auto=None):
        if ch is None:
            assert not dma, "DMA ops must name a channel"
            ch = e
        inc = 16 if dma else 1
        waits = [a for a in after if a is not None]
        if auto is None:
            auto = e in ("v", "s") and not dma
        if auto and self.n.get(e, 0) > 0:
            waits.append((e, self.n[e]))
        self.n[ch] = self.n.get(ch, 0) + inc
        self.owner[ch] = e
        self.ops[e].append((fn, waits, inc, ch))
        return (ch, self.n[ch])

    def build(self):
        nc = self.nc
        chans = list(self.n.keys())
        with contextlib.ExitStack() as es:
            sems = {c: es.enter_context(nc.semaphore(f"sem_{c}"))
                    for c in chans}
            with nc.Block() as block:
                deco = {"y": block.sync, "g": block.gpsimd, "v": block.vector,
                        "s": block.scalar, "t": block.tensor}
                for e in self.ENGS:
                    if not self.ops[e]:
                        continue

                    def body(eng, e=e):
                        for fn, waits, inc, ch in self.ops[e]:
                            for (wc_ch, wc) in waits:
                                if wc_ch in ("cst", "cstg", "cstg2"):
                                    wc = self.n[wc_ch]
                                eng.wait_ge(sems[wc_ch], wc)
                            fn(eng).then_inc(sems[ch], inc)
                        for c in chans:
                            if self.owner[c] == e:
                                eng.wait_ge(sems[c], self.n[c])
                    deco[e](body)


def alloc(nc, es, name, shape, dt=F32):
    return es.enter_context(nc.sbuf_tensor(name, shape, dt))


def emit_flag_g(S, one_sb, g_sb, ev_one):
    ev = S.op("v", lambda e: e.tensor_scalar(
        g_sb[:], one_sb[:], EPS_FLAG, 1.0, ALU.add, ALU.subtract),
        after=[ev_one])
    return S.op("v", lambda e: e.tensor_scalar(
        g_sb[:], g_sb[:], -(2.0**40), 1.0, ALU.mult, ALU.add), after=[ev])


def emit_round(S, out, tin, g_sb, tmpa, tmpb, after=()):
    eva = S.op("v", lambda e: e.tensor_scalar(
        tmpa, tin, C23, C23, ALU.add, ALU.subtract), after=after)
    evb = S.op("v", lambda e: e.tensor_scalar(
        tmpb, tin, C52, C52, ALU.add, ALU.subtract), after=after)
    return S.op("v", lambda e: e.scalar_tensor_tensor(
        out, tmpa, g_sb[:], tmpb, ALU.mult, ALU.add), after=[eva, evb])


def ap3(handle, pitch, part, d1, d2, off=0):
    (d1s, d1n), (d2s, d2n) = d1, d2
    return bass.AP(handle if not hasattr(handle, "tensor") else handle.tensor,
                   off, [[pitch, part], [d1s, d1n], [d2s, d2n]])



def build_A():
    nc = bass.Bass("TRN2", target_bir_lowering=False)
    xt = nc.dram_tensor("xt", [D, W], F32, kind="ExternalInput")
    winv = nc.dram_tensor("winv", [D, D], F32, kind="ExternalInput")
    winx2 = nc.dram_tensor("winx2", [D, D], F32, kind="ExternalInput")
    bv = nc.dram_tensor("bv", [128, 8], F32, kind="ExternalInput")
    bx2 = nc.dram_tensor("bx2", [128, 8], F32, kind="ExternalInput")
    swc = nc.dram_tensor("swc", [128, 24], F32, kind="ExternalInput")
    sbc = nc.dram_tensor("sbc", [128, 8], F32, kind="ExternalInput")
    edges = nc.dram_tensor("edges", [128, 16], F32, kind="ExternalInput")
    one128 = nc.dram_tensor("one128", [128, 1], F32, kind="ExternalInput")
    wg_s = nc.dram_tensor("wg_s", [IH // NC, D], F32, kind="ExternalInput")
    wu_s = nc.dram_tensor("wu_s", [IH // NC, D], F32, kind="ExternalInput")
    wd_s = nc.dram_tensor("wd_s", [D // NC, IH], F32, kind="ExternalInput")

    v_out = nc.dram_tensor("v_out", [D, TPC], BF16, kind="ExternalOutput")
    x2_out = nc.dram_tensor("x2_out", [D, W], BF16, kind="ExternalOutput")
    wgq = nc.dram_tensor("wgq", [IH // NC, D], BF16, kind="ExternalOutput")
    wuq = nc.dram_tensor("wuq", [IH // NC, D], BF16, kind="ExternalOutput")
    wdq = nc.dram_tensor("wdq", [D // NC, IH], BF16, kind="ExternalOutput")
    vraw_h = nc.dram_tensor("vraw_h", [D, W], BF16)

    es = contextlib.ExitStack()
    with es:
        S = Sched(nc)
        wv_sb = alloc(nc, es, "wv", [128, 8 * 8 * 128], BF16)
        wx_sb = alloc(nc, es, "wx", [128, 8 * 8 * 128], BF16)
        ones_sb = alloc(nc, es, "ones", [128, 128], BF16)
        bv_sb = alloc(nc, es, "bvs", [128, 8], F32)
        bx_sb = alloc(nc, es, "bxs", [128, 8], F32)
        swc_sb = alloc(nc, es, "swcs", [128, 8 * 3], F32)
        sbc_sb = alloc(nc, es, "sbcs", [128, 8], F32)
        edges_sb = alloc(nc, es, "edg", [128, 8 * 2], F32)
        one_sb = alloc(nc, es, "one", [128, 1], F32)
        g_sb = alloc(nc, es, "gflag", [128, 1], F32)

        xf = [alloc(nc, es, f"xf{i}", [128, 8 * 512], F32) for i in range(2)]
        xbf = [alloc(nc, es, f"xbf{i}", [128, 8 * 512], BF16) for i in range(2)]
        xsq_t = alloc(nc, es, "xsq", [128, 8 * 512], BF16)
        mu = [alloc(nc, es, f"mu{i}", [128, 512], F32) for i in range(2)]
        rb = [alloc(nc, es, f"rb{i}", [128, 512], F32) for i in range(2)]
        tstat = alloc(nc, es, "tstat", [128, 512], F32)
        tstat2 = alloc(nc, es, "tstat2", [128, 512], F32)
        xln = [alloc(nc, es, f"xln{i}", [128, 8 * 512], BF16) for i in range(2)]
        x2sb = [alloc(nc, es, f"x2sb{i}", [128, 8 * 512], BF16) for i in range(2)]
        vsb = [alloc(nc, es, f"vsb{i}", [128, 8 * 512], BF16) for i in range(2)]
        vin = [alloc(nc, es, f"vin{i}", [128, 8 * 514], BF16) for i in range(2)]
        cv1 = alloc(nc, es, "cv1", [128, 512], F32)
        cv2 = alloc(nc, es, "cv2", [128, 512], F32)
        vch = alloc(nc, es, "vch", [128, 8 * 512], BF16)
        sred = alloc(nc, es, "sred", [128, 32], F32)
        rs_ = alloc(nc, es, "rs_", [128, 32], F32)
        wqf = lambda: xf[0][:, 0:1024]
        qa = lambda: xf[0][:, 1024:2048]
        qb = lambda: xf[0][:, 2048:3072]
        qt = lambda: xf[1][:, 0:1024]
        qo = lambda: xln[0][:, 0:1024]

        with (nc.psum_tensor("ps_sum", [128, 512], F32) as ps_sum,
              nc.psum_tensor("ps_sq", [128, 512], F32) as ps_sq,
              nc.psum_tensor("pm0", [128, 512], F32) as pm0,
              nc.psum_tensor("pm1", [128, 512], F32) as pm1):

            ev_wv = S.op("g", lambda e: e.dma_start(
                out=ap3(wv_sb, 8192, 128, (1024, 8), (1, 1024)),
                in_=bass.AP(winv, 0, [[D, 128], [128 * D, 8], [1, 1024]])),
                dma=True, ch="cstg")
            ev_wx = S.op("g", lambda e: e.dma_start(
                out=ap3(wx_sb, 8192, 128, (1024, 8), (1, 1024)),
                in_=bass.AP(winx2, 0, [[D, 128], [128 * D, 8], [1, 1024]])),
                dma=True, ch="cstg")
            ev_on = S.op("g", lambda e: e.memset(ones_sb[:], 1.0))
            ev_bv = S.op("y", lambda e: e.dma_start(
                out=bv_sb[:], in_=bv.ap()), dma=True, ch="cst")
            ev_bx = S.op("y", lambda e: e.dma_start(
                out=bx_sb[:], in_=bx2.ap()), dma=True, ch="cst")
            ev_sw = S.op("y", lambda e: e.dma_start(
                out=swc_sb[:], in_=swc.ap()), dma=True, ch="cst")
            ev_sb = S.op("y", lambda e: e.dma_start(
                out=sbc_sb[:], in_=sbc.ap()), dma=True, ch="cst")
            ev_ed = S.op("y", lambda e: e.dma_start(
                out=edges_sb[:], in_=edges.ap()), dma=True, ch="cst")
            ev_1 = S.op("y", lambda e: e.dma_start(out=one_sb[:],
                                                   in_=one128.ap()), dma=True, ch="cst")
            ev_cst = ("cst", S.n["cst"])
            ev_cstg = ("cstg", S.n["cstg"])
            ev_wv = ev_wx = ev_cstg
            ev_bv = ev_bx = ev_sw = ev_sb = ev_ed = ev_cst
            ev_g = emit_flag_g(S, one_sb, g_sb, ev_cst)

            ev_inp = {0: None, 1: None}
            ev_evac = {0: None, 1: None}
            ev_x2st = {0: None, 1: None}
            ev_vst = {0: None, 1: None}
            ev_stats_rd = None
            ev_stats_sqrt = None
            ev_sqmm_prev = None
            col = 0
            for ti, cw in enumerate(TILES):
                bi = ti % 2
                c0 = col
                col += cw
                ev_ld = S.op("y", lambda e, c0=c0, cw=cw, bi=bi: e.dma_start(
                    out=ap3(xf[bi], 8 * 512, 128, (cw, 8), (1, cw)),
                    in_=bass.AP(xt, c0, [[W, 128], [128 * W, 8], [1, cw]])),
                    after=[ev_inp[bi]], dma=True, ch=f"ld{bi}")
                ev_cast = S.op("v", lambda e, cw=cw, bi=bi: e.tensor_copy(
                    xbf[bi][:, 0:8 * cw], xf[bi][:, 0:8 * cw]), after=[ev_ld])
                ev_sq = S.op("s", lambda e, cw=cw, bi=bi: e.activation(
                    xsq_t[:, 0:8 * cw], xbf[bi][:, 0:8 * cw], AF.Square),
                    after=[ev_cast, ev_sqmm_prev])
                ev_mm = None
                for k in range(8):
                    aft = [ev_cast, ev_on, ev_stats_rd] if k == 0 else []
                    ev_mm = S.op("t", lambda e, k=k, cw=cw, bi=bi: e.matmul(
                        ps_sum[:, 0:cw], ones_sb[:],
                        xbf[bi][:, k * cw:(k + 1) * cw],
                        start=(k == 0), stop=(k == 7)), after=aft)
                for k in range(8):
                    aft = [ev_sq] if k == 0 else []
                    ev_mm = S.op("t", lambda e, k=k, cw=cw: e.matmul(
                        ps_sq[:, 0:cw], ones_sb[:],
                        xsq_t[:, k * cw:(k + 1) * cw],
                        start=(k == 0), stop=(k == 7)), after=aft)
                ev_sqmm_prev = ev_mm
                ev1 = S.op("v", lambda e, cw=cw, bi=bi: e.tensor_scalar(
                    mu[bi][:, 0:cw], ps_sum[:, 0:cw], 1.0 / D, None, ALU.mult),
                    after=[ev_mm])
                ev2 = S.op("v", lambda e, cw=cw: e.tensor_scalar(
                    tstat[:, 0:cw], ps_sq[:, 0:cw], 1.0 / D, 1e-5,
                    ALU.mult, ALU.add), after=[ev_stats_sqrt])
                ev3 = S.op("v", lambda e, cw=cw, bi=bi: e.tensor_tensor(
                    tstat2[:, 0:cw], mu[bi][:, 0:cw], mu[bi][:, 0:cw],
                    ALU.mult))
                ev4 = S.op("v", lambda e, cw=cw: e.tensor_tensor(
                    tstat[:, 0:cw], tstat[:, 0:cw], tstat2[:, 0:cw],
                    ALU.subtract))
                ev_stats_rd = ev2
                ev_stats_sqrt = S.op("s", lambda e, cw=cw: e.activation(
                    tstat2[:, 0:cw], tstat[:, 0:cw], AF.Sqrt), after=[ev4])
                ev6 = S.op("v", lambda e, cw=cw, bi=bi: e.reciprocal(
                    rb[bi][:, 0:cw], tstat2[:, 0:cw]), after=[ev_stats_sqrt])
                ev7 = S.op("v", lambda e, cw=cw, bi=bi: e.tensor_tensor(
                    xf[bi][:, 0:8 * cw], xf[bi][:, 0:8 * cw],
                    bass.AP(mu[bi], 0, [[512, 128], [0, 8], [1, cw]]),
                    ALU.subtract))
                ev8 = S.op("v", lambda e, cw=cw, bi=bi: e.tensor_tensor(
                    xln[bi][:, 0:8 * cw], xf[bi][:, 0:8 * cw],
                    bass.AP(rb[bi], 0, [[512, 128], [0, 8], [1, cw]]),
                    ALU.mult))
                last_inp = None
                ev_x2 = []
                ev_v = []
                for m in range(16):
                    isv = m < 8
                    wsb = wv_sb if isv else wx_sb
                    mm_ = m if isv else m - 8
                    pbank = pm0 if (m % 2 == 0) else pm1
                    evk = None
                    for k in range(8):
                        aft = ([ev8, ev_wv if isv else ev_wx, ev_evac[m % 2]]
                               if k == 0 else [])
                        evk = S.op("t", lambda e, k=k, mm_=mm_, cw=cw, bi=bi,
                                   wsb=wsb, pbank=pbank: e.matmul(
                            pbank[:, 0:cw],
                            wsb[:, (k * 8 + mm_) * 128:(k * 8 + mm_ + 1) * 128],
                            xln[bi][:, k * cw:(k + 1) * cw],
                            start=(k == 0), stop=(k == 7)), after=aft)
                    if isv:
                        def dst(mm_=mm_, cw=cw, bi=bi):
                            return vsb[bi][:, mm_ * cw:(mm_ + 1) * cw]
                        bias_sb, ev_b, ev_st = bv_sb, ev_bv, ev_vst[bi]
                    else:
                        def dst(mm_=mm_, cw=cw, bi=bi):
                            return x2sb[bi][:, mm_ * cw:(mm_ + 1) * cw]
                        bias_sb, ev_b, ev_st = bx_sb, ev_bx, ev_x2st[bi]
                    ev_evac[m % 2] = S.op("s", lambda e, dst=dst, mm_=mm_,
                                          cw=cw, pbank=pbank,
                                          bias_sb=bias_sb: e.activation(
                        dst(), pbank[:, 0:cw], AF.Identity,
                        bias=bias_sb[:, mm_:mm_ + 1]), after=[evk, ev_b, ev_st])
                    last_inp = ev_evac[m % 2]
                    (ev_v if isv else ev_x2).append(ev_evac[m % 2])
                ev_inp[bi] = last_inp
                ev_x2st[bi] = S.op("g", lambda e, c0=c0, cw=cw, bi=bi:
                                   e.dma_start(
                    out=bass.AP(x2_out, c0, [[W, 128], [128 * W, 8], [1, cw]]),
                    in_=ap3(x2sb[bi], 8 * 512, 128, (cw, 8), (1, cw))),
                    after=ev_x2, dma=True, ch=f"x2{bi}")
                ev_vst[bi] = S.op("g", lambda e, c0=c0, cw=cw, bi=bi:
                                  e.dma_start(
                    out=bass.AP(vraw_h, c0, [[W, 128], [128 * W, 8], [1, cw]]),
                    in_=ap3(vsb[bi], 8 * 512, 128, (cw, 8), (1, cw))),
                    after=ev_v, dma=True, ch=f"vs{bi}")
            ev_pass1 = [ev_vst[0], ev_vst[1], ev_x2st[0], ev_x2st[1],
                        ev_inp[0], ev_inp[1]]

            ev_vld = {0: None, 1: None}
            ev_conv_rd = {0: None, 1: None}
            ev_cv_rd = None
            ev_vch_st = None
            for ti in range(8):
                bi = ti % 2
                o = HALO + ti * 512
                ev_vld[bi] = S.op("y", lambda e, o=o, bi=bi: e.dma_start(
                    out=ap3(vin[bi], 8 * 514, 128, (514, 8), (1, 514)),
                    in_=bass.AP(vraw_h, o - 1, [[W, 128], [128 * W, 8],
                                                [1, 514]])),
                    after=ev_pass1 + [ev_conv_rd[bi]], dma=True, ch=f"cl{bi}")
                evm = [ev_vld[bi], ev_ed, ev_sw]
                if ti == 0:
                    evm0 = S.op("v", lambda e: e.tensor_scalar(
                        bass.AP(vin[0], 0, [[8 * 514, 128], [514, 8], [1, 1]]),
                        bass.AP(vin[0], 0, [[8 * 514, 128], [514, 8], [1, 1]]),
                        edges_sb[:, 0:1], None, ALU.mult), after=evm)
                    evm = [evm0, ev_sw]
                if ti == 7:
                    evm0 = S.op("v", lambda e: e.tensor_scalar(
                        bass.AP(vin[1], 513, [[8 * 514, 128], [514, 8],
                                              [1, 1]]),
                        bass.AP(vin[1], 513, [[8 * 514, 128], [514, 8],
                                              [1, 1]]),
                        edges_sb[:, 1:2], None, ALU.mult), after=evm)
                    evm = [evm0, ev_sw]
                last_e4 = None
                for ch in range(8):
                    co = ch * 514
                    e1 = S.op("v", lambda e, co=co, bi=bi, ch=ch: e.tensor_scalar(
                        cv1[:], vin[bi][:, co + 1:co + 513],
                        swc_sb[:, ch * 3 + 1:ch * 3 + 2], None, ALU.mult),
                        after=evm + [ev_cv_rd])
                    e2 = S.op("v", lambda e, co=co, bi=bi, ch=ch:
                              e.scalar_tensor_tensor(
                        cv2[:], vin[bi][:, co:co + 512],
                        swc_sb[:, ch * 3 + 0:ch * 3 + 1], cv1[:],
                        ALU.mult, ALU.add))
                    e3 = S.op("v", lambda e, co=co, bi=bi, ch=ch:
                              e.scalar_tensor_tensor(
                        cv1[:], vin[bi][:, co + 2:co + 514],
                        swc_sb[:, ch * 3 + 2:ch * 3 + 3], cv2[:],
                        ALU.mult, ALU.add))
                    e4 = S.op("s", lambda e, ch=ch: e.activation(
                        vch[:, ch * 512:(ch + 1) * 512], cv1[:], AF.Identity,
                        bias=sbc_sb[:, ch:ch + 1]),
                        after=[e3, ev_sb, ev_vch_st])
                    ev_cv_rd = e4
                    last_e4 = e4
                ev_conv_rd[bi] = last_e4
                ev_vch_st = S.op("g", lambda e, ti=ti: e.dma_start(
                    out=bass.AP(v_out, ti * 512, [[TPC, 128], [128 * TPC, 8],
                                                  [1, 512]]),
                    in_=ap3(vch, 8 * 512, 128, (512, 8), (1, 512))),
                    after=[last_e4], dma=True, ch="vch")

            SEG = 1024

            def lpbq(src_t, dst_t, nrow, ncol, after):
                evs = list(after)
                for chunk in range(nrow // 128):
                    for s0 in range(0, ncol, SEG):
                        off = chunk * 128 * ncol + s0
                        nb = SEG // 32
                        evl = S.op("g", lambda e, off=off, ncol=ncol:
                                   e.dma_start(
                            out=wqf(),
                            in_=bass.AP(src_t, off, [[ncol, 128], [1, SEG]])),
                            after=evs, dma=True, ch="lp")
                        ev = S.op("v", lambda e: e.tensor_reduce(
                            sred[:], bass.AP(xf[0], 0, [[8 * 512, 128],
                                                        [32, 32], [1, 32]]),
                            AX.X, ALU.max, apply_absolute_value=True),
                            after=[evl])
                        ev = S.op("v", lambda e: e.tensor_scalar(
                            sred[:], sred[:], 1.0 / 7.0, 1e-12,
                            ALU.mult, ALU.max))
                        ev = S.op("v", lambda e: e.reciprocal(rs_[:], sred[:]))
                        ev = S.op("v", lambda e: e.tensor_tensor(
                            qt(), wqf(),
                            bass.AP(rs_, 0, [[32, 128], [1, 32], [0, 32]]),
                            ALU.mult))
                        ev = emit_round(S, qt(), qt(), g_sb, qa(), qb(),
                                        after=[ev, ev_g])
                        ev = S.op("v", lambda e: e.tensor_tensor(
                            qo(), qt(),
                            bass.AP(sred, 0, [[32, 128], [1, 32], [0, 32]]),
                            ALU.mult), after=[ev])
                        evst = S.op("g", lambda e, off=off, ncol=ncol:
                                    e.dma_start(
                            out=bass.AP(dst_t, off, [[ncol, 128], [1, SEG]]),
                            in_=qo()), after=[ev], dma=True, ch="lp")
                        evs = [evst]
                return evs[0]

            ev_all = ev_pass1 + [ev_vch_st, ev_cv_rd, ev_g]
            e_g = lpbq(wg_s, wgq, IH // NC, D, ev_all)
            e_u = lpbq(wu_s, wuq, IH // NC, D, [e_g])
            e_d = lpbq(wd_s, wdq, D // NC, IH, [e_u])

            S.build()
    return nc



def build_B():
    nc = bass.Bass("TRN2", target_bir_lowering=False)
    v_in = nc.dram_tensor("v_in", [128, 2 * L], BF16, kind="ExternalInput")
    fw1 = nc.dram_tensor("fw1", [1, 64], F32, kind="ExternalInput")
    fb1 = nc.dram_tensor("fb1", [64, 1], F32, kind="ExternalInput")
    fw2 = nc.dram_tensor("fw2", [64, 128], F32, kind="ExternalInput")
    fb2 = nc.dram_tensor("fb2", [128, 1], F32, kind="ExternalInput")
    fw3c = nc.dram_tensor("fw3c", [128, 128], F32, kind="ExternalInput")
    fb3c = nc.dram_tensor("fb3c", [128, 1], F32, kind="ExternalInput")
    posn = nc.dram_tensor("posn", [1, L], F32, kind="ExternalInput")
    E = {}
    for nm, cols in [("ec256", 256), ("es256", 256), ("ec128", 128),
                     ("es128", 128), ("ectw", 256), ("estw", 256),
                     ("ecitw", 256), ("esitw", 256), ("ecc256", 256),
                     ("esc256", 256)]:
        E[nm] = nc.dram_tensor(nm, [128, cols], F32, kind="ExternalInput")
    conv_out = nc.dram_tensor("conv_out", [128, 2 * L], BF16,
                              kind="ExternalOutput")
    h_hbm = nc.dram_tensor("h_hbm", [128, L], BF16)

    TWO_PI = 2.0 * math.pi
    es = contextlib.ExitStack()
    with es:
        S = Sched(nc)
        T = {nm: alloc(nc, es, "t_" + nm, [128, cols], F32)
             for nm, cols in [("cA512", 512), ("cB512", 512),
                              ("f128re", 128), ("f128im", 128),
                              ("nf128im", 128),
                              ("cA256", 256), ("cB256", 256),
                              ("twre", 256), ("twim", 256),
                              ("itwre", 256), ("itwim", 256),
                              ("c256re", 256), ("c256im", 256),
                              ("nc256im", 256)]}
        tgen = alloc(nc, es, "tgen", [128, 256], F32)
        tgen2 = alloc(nc, es, "tgen2", [128, 256], F32)
        fw1_sb = alloc(nc, es, "fw1s", [1, 64], F32)
        fb1_sb = alloc(nc, es, "fb1s", [64, 1], F32)
        fw2_sb = alloc(nc, es, "fw2s", [64, 128], F32)
        fb2_sb = alloc(nc, es, "fb2s", [128, 1], F32)
        fw3_sb = alloc(nc, es, "fw3s", [128, 128], F32)
        fb3_sb = alloc(nc, es, "fb3s", [128, 1], F32)
        pk = [alloc(nc, es, f"pk{i}", [1, 512], F32) for i in range(2)]
        pb = [alloc(nc, es, f"pb{i}", [1, 512], F32) for i in range(2)]
        h1t = alloc(nc, es, "h1t", [64, 512], F32)
        h2t = alloc(nc, es, "h2t", [128, 512], F32)
        hot = [alloc(nc, es, f"hot{i}", [128, 512], BF16) for i in range(2)]
        hz = [alloc(nc, es, f"hz{i}", [128, 128], F32) for i in range(4)]
        zre = [alloc(nc, es, f"zre{i}", [128, 128], F32) for i in range(4)]
        zim = [alloc(nc, es, f"zim{i}", [128, 128], F32) for i in range(4)]
        s2re = alloc(nc, es, "s2re", [128, 1024], F32)
        s2im = alloc(nc, es, "s2im", [128, 1024], F32)
        hre = alloc(nc, es, "hre", [128, 1024], F32)
        him = alloc(nc, es, "him", [128, 1024], F32)
        xre = alloc(nc, es, "xre", [128, 1024], F32)
        xim = alloc(nc, es, "xim", [128, 1024], F32)
        pre = alloc(nc, es, "pre", [128, 1024], F32)
        pim = alloc(nc, es, "pim", [128, 1024], F32)
        t2 = {(pl, r): alloc(nc, es, f"t2{pl}{r}", [128, 512], F32)
              for pl in "ri" for r in range(2)}
        t2p = {(pl, r): alloc(nc, es, f"t2p{pl}{r}", [128, 512], F32)
               for pl in "ri" for r in range(2)}
        zo = [alloc(nc, es, f"zo{i}", [128, 512], BF16) for i in range(2)]
        tmp1 = alloc(nc, es, "tmp1", [128, 1024], F32)
        tmp2 = alloc(nc, es, "tmp2", [128, 1024], F32)
        tmp3 = alloc(nc, es, "tmp3", [128, 1024], F32)
        tmp4 = alloc(nc, es, "tmp4", [128, 1024], F32)

        with (nc.psum_tensor("pA0", [128, 512], F32) as pA0,
              nc.psum_tensor("pA1", [128, 512], F32) as pA1,
              nc.psum_tensor("pA2", [128, 512], F32) as pA2,
              nc.psum_tensor("pA3", [128, 512], F32) as pA3,
              nc.psum_tensor("pB0", [128, 512], F32) as pB0,
              nc.psum_tensor("pB1", [128, 512], F32) as pB1,
              nc.psum_tensor("pB2", [128, 512], F32) as pB2,
              nc.psum_tensor("pB3", [128, 512], F32) as pB3):
            pA = [pA0, pA1, pA2, pA3]
            pB = [pB0, pB1, pB2, pB3]

            ev_e = {}
            for nm in E:
                ev_e[nm] = S.op("y", lambda e, nm=nm: e.dma_start(
                    out=tgen[:, 0:E[nm].shape[1]] if False else None,
                    in_=None), dma=True, ch="cst") if False else None
            def gen_table(enm, dst, off, period_scale, extra_scale, after):
                cols = E[enm].shape[1]
                chi, clo = split_const(TWO_PI * period_scale)
                ev = S.op("y", lambda e: e.dma_start(
                    out=tgen[:, 0:cols], in_=E[enm].ap()),
                    dma=True, ch="tg", after=after)
                ev = S.op("v", lambda e: e.tensor_scalar(
                    tgen2[:, 0:cols], tgen[:, 0:cols], chi, None, ALU.mult),
                    after=[ev])
                ev = S.op("v", lambda e: e.scalar_tensor_tensor(
                    tgen2[:, 0:cols], tgen[:, 0:cols], clo, tgen2[:, 0:cols],
                    ALU.mult, ALU.add))
                ev = S.op("s", lambda e: e.activation(
                    tgen2[:, 0:cols], tgen2[:, 0:cols], AF.Sin), after=[ev])
                ev = S.op("v", lambda e: e.tensor_scalar(
                    dst[:, off:off + cols], tgen2[:, 0:cols], extra_scale,
                    None, ALU.mult), after=[ev])
                return ev

            ev = None
            for enm, dnm, off, psc, xsc in [
                    ("ec256", "cA512", 0, 1.0 / 256, 1.0),
                    ("es256", "cA512", 256, 1.0 / 256, 1.0),
                    ("es256", "cB512", 0, 1.0 / 256, -1.0),
                    ("ec256", "cB512", 256, 1.0 / 256, 1.0),
                    ("ec128", "f128re", 0, 1.0 / 128, 1.0),
                    ("es128", "f128im", 0, 1.0 / 128, 1.0),
                    ("es128", "nf128im", 0, 1.0 / 128, -1.0),
                    ("ec128", "cA256", 0, 1.0 / 128, 1.0),
                    ("es128", "cA256", 128, 1.0 / 128, -1.0),
                    ("es128", "cB256", 0, 1.0 / 128, 1.0),
                    ("ec128", "cB256", 128, 1.0 / 128, 1.0),
                    ("ectw", "twre", 0, 1.0 / 32768, 1.0),
                    ("estw", "twim", 0, 1.0 / 32768, 1.0),
                    ("ecitw", "itwre", 0, 1.0 / 32768, 2.0**-15),
                    ("esitw", "itwim", 0, 1.0 / 32768, -(2.0**-15)),
                    ("ecc256", "c256re", 0, 1.0 / 256, 1.0),
                    ("esc256", "c256im", 0, 1.0 / 256, -1.0),
                    ("esc256", "nc256im", 0, 1.0 / 256, 1.0)]:
                ev = gen_table(enm, T[dnm], off, psc, xsc, [ev])
            ev_tables = ev

            chi, clo = split_const(1.0 / (L - 1))
            ev_w = []
            for src_t, dst_t in [(fw1, fw1_sb), (fw2, fw2_sb), (fw3c, fw3_sb)]:
                ev_w.append(S.op("g", lambda e, s=src_t, d=dst_t: e.dma_start(
                    out=d[:], in_=s.ap()), dma=True, ch="cstg"))
            for src_t, dst_t in [(fb1, fb1_sb), (fb2, fb2_sb), (fb3c, fb3_sb)]:
                ev_w.append(S.op("y", lambda e, s=src_t, d=dst_t: e.dma_start(
                    out=d[:], in_=s.ap()), dma=True, ch="cst"))
            ev_cst = ("cst", S.n["cst"])
            ev_cstg = ("cstg", S.n["cstg"])
            ev_hst = {0: None, 1: None}
            ev_pbu = {0: None, 1: None}
            ev_pkrd = {0: None, 1: None}
            for ti in range(32):
                c0 = ti * 512
                bi = ti % 2
                evk = S.op("y", lambda e, c0=c0, bi=bi: e.dma_start(
                    out=pk[bi][:], in_=bass.AP(posn, c0, [[L, 1], [1, 512]])),
                    dma=True, ch=f"pk{bi}", after=[ev_pkrd[bi]])
                eva = S.op("v", lambda e, bi=bi: e.tensor_scalar(
                    pb[bi][:], pk[bi][:], chi, None, ALU.mult),
                    after=[evk, ev_pbu[bi]])
                evb = S.op("v", lambda e, bi=bi: e.scalar_tensor_tensor(
                    pb[bi][:], pk[bi][:], clo, pb[bi][:], ALU.mult, ALU.add))
                ev_pkrd[bi] = evb
                evm = S.op("t", lambda e, bi=bi: e.matmul(
                    pA[0][:, 0:512][0:64, :], fw1_sb[:],
                    pb[bi][:], start=True, stop=True),
                    after=[evb, ev_cstg, ev_cst, ev_h2rd, ev_hmmrd]
                    if ti > 0 else [evb, ev_cstg, ev_cst])
                ev_pbu[bi] = evm
                evg = S.op("s", lambda e: e.activation(
                    h1t[:], pA[0][0:64, 0:512], AF.Gelu, bias=fb1_sb[:]),
                    after=[evm, ev_cst])
                evm2 = S.op("t", lambda e: e.matmul(
                    pA[1][:, 0:512], fw2_sb[:], h1t[:], start=True,
                    stop=True), after=[evg])
                evg2 = S.op("s", lambda e: e.activation(
                    h2t[:], pA[1][:, 0:512], AF.Gelu, bias=fb2_sb[:]),
                    after=[evm2])
                ev_h2rd = evg2
                evm3 = S.op("t", lambda e: e.matmul(
                    pA[2][:, 0:512], fw3_sb[:], h2t[:], start=True,
                    stop=True), after=[evg2])
                ev_hmmrd = evm3
                evg3 = S.op("s", lambda e, bi=bi: e.activation(
                    hot[bi][:], pA[2][:, 0:512], AF.Identity,
                    bias=fb3_sb[:]), after=[evm3, ev_hst[bi]])
                ev_hst[bi] = S.op("g", lambda e, c0=c0, bi=bi: e.dma_start(
                    out=bass.AP(h_hbm, c0, [[L, 128], [1, 512]]),
                    in_=hot[bi][:]), after=[evg3], dma=True, ch=f"hs{bi}")
            ev_hdone = [ev_hst[0], ev_hst[1]]
            ev_mlp_last_read = evg3

            ev_cplx_prev = [None, None]

            def cplx6(outr, outi, ar, ai, br_ap, bi_ap, after, n=1024):
                t1 = tmp1[:, 0:n]
                t2_ = tmp2[:, 0:n]
                t3 = tmp3[:, 0:n]
                t4 = tmp4[:, 0:n]
                pv, pg = ev_cplx_prev
                e1 = S.op("v", lambda e: e.tensor_tensor(
                    t1, ar, br_ap, ALU.mult), after=after)
                e2 = S.op("g", lambda e: e.tensor_tensor(
                    t2_, ai, bi_ap, ALU.mult), after=after + [pv])
                e3 = S.op("v", lambda e: e.tensor_tensor(
                    t3, ar, bi_ap, ALU.mult), after=[pg])
                e4 = S.op("g", lambda e: e.tensor_tensor(
                    t4, ai, br_ap, ALU.mult), after=after + [pg])
                e5 = S.op("v", lambda e: e.tensor_tensor(
                    outr, t1, t2_, ALU.subtract), after=[e2])
                e6 = S.op("g", lambda e: e.tensor_tensor(
                    outi, t3, t4, ALU.add), after=[e2, e3, e4])
                ev_cplx_prev[0] = e5
                ev_cplx_prev[1] = e6
                return [e5, e6]

            def bc(tile, cols):
                return bass.AP(tile, 0, [[tile.shape[1], 128], [0, 4],
                                         [1, cols]])

            ev_zl = {}
            ev_zuse = {}
            ev_evA = ev_mlp_last_read
            ev_evB = None
            ev_prod = []
            ev_t2rd = []
            ev_t2prd = None
            ev_twrd = []
            ev_s2rd = None
            ev_pird = None
            ev_zodma = {0: None, 1: None}
            for g in range(32):
                cbase = g * 4
                for c in range(4):
                    ev_zl[("h", c)] = S.op("g", lambda e, c=c, cbase=cbase: e.dma_start(
                        out=hz[c][:],
                        in_=bass.AP(h_hbm, (cbase + c) * L,
                                    [[128, 128], [1, 128]])),
                        after=ev_hdone + [ev_zuse.get(("h", c))],
                        dma=True, ch=f"zh{c}")
                    ev_zl[("re", c)] = S.op("g", lambda e, c=c, cbase=cbase: e.dma_start(
                        out=zre[c][:],
                        in_=bass.AP(v_in, (cbase + c) * 2 * L,
                                    [[128, 128], [1, 128]])),
                        after=[ev_zuse.get(("re", c))], dma=True,
                        ch=f"zr{c}")
                    ev_zl[("im", c)] = S.op("g", lambda e, c=c, cbase=cbase: e.dma_start(
                        out=zim[c][:],
                        in_=bass.AP(v_in, (cbase + c) * 2 * L + L,
                                    [[128, 128], [1, 128]])),
                        after=[ev_zuse.get(("im", c))], dma=True,
                        ch=f"zi{c}")
                ev_s1 = []
                for c in range(4):
                    ev2 = S.op("t", lambda e, c=c: e.matmul(
                        pA[c][:, 0:512], hz[c][:], T["cA512"][:],
                        start=True, stop=True),
                        after=[ev_zl[("h", c)], ev_tables, ev_evA])
                    ev_zuse[("h", c)] = ev2
                    ev_s1.append(ev2)
                ev_ev = []
                for c in range(4):
                    e1 = S.op("s", lambda e, c=c: e.copy(
                        s2re[:, c * 256:(c + 1) * 256], pA[c][:, 0:256]),
                        after=[ev_s1[c], ev_s2rd] + ev_twrd)
                    e2 = S.op("s", lambda e, c=c: e.copy(
                        s2im[:, c * 256:(c + 1) * 256], pA[c][:, 256:512]))
                    ev_ev += [e1, e2]
                ev_evA = ev_ev[-1]
                ev_tw = cplx6(s2re[:], s2im[:], s2re[:], s2im[:],
                              bc(T["twre"], 256), bc(T["twim"], 256),
                              ev_ev + [ev_s2rd])
                ev_twrd = ev_tw
                ev_hs2 = []
                for half in range(2):
                    sl = slice(half * 512, (half + 1) * 512)
                    e1 = S.op("t", lambda e, sl=sl, half=half: e.matmul(
                        pB[half][:, 0:512], T["f128re"][:], s2re[:, sl],
                        start=True, stop=False), after=ev_tw + [ev_evB])
                    e1 = S.op("t", lambda e, sl=sl, half=half: e.matmul(
                        pB[half][:, 0:512], T["nf128im"][:], s2im[:, sl],
                        start=False, stop=True))
                    e2 = S.op("t", lambda e, sl=sl, half=half: e.matmul(
                        pB[2 + half][:, 0:512], T["f128re"][:], s2im[:, sl],
                        start=True, stop=False))
                    e2 = S.op("t", lambda e, sl=sl, half=half: e.matmul(
                        pB[2 + half][:, 0:512], T["f128im"][:], s2re[:, sl],
                        start=False, stop=True))
                    ev_hs2 += [e1, e2]
                ev_s2rd = ev_hs2[-1]
                ev_ev = []
                for half in range(2):
                    sl = slice(half * 512, (half + 1) * 512)
                    ev_ev.append(S.op("s", lambda e, sl=sl, half=half:
                                      e.copy(hre[:, sl],
                                                    pB[half][:, 0:512]),
                                      after=[ev_hs2[2 * half + 1]] + ev_prod))
                    ev_ev.append(S.op("s", lambda e, sl=sl, half=half:
                                      e.copy(him[:, sl],
                                                    pB[2 + half][:, 0:512])))
                ev_evB = ev_ev[-1]
                ev_hev = list(ev_ev)
                ev_s1 = []
                for c in range(4):
                    e2 = S.op("t", lambda e, c=c: e.matmul(
                        pA[c][:, 0:512], zre[c][:], T["cA512"][:],
                        start=True, stop=False),
                        after=[ev_zl[("re", c)], ev_zl[("im", c)],
                               ev_evA])
                    e2 = S.op("t", lambda e, c=c: e.matmul(
                        pA[c][:, 0:512], zim[c][:], T["cB512"][:],
                        start=False, stop=True))
                    ev_zuse[("re", c)] = e2
                    ev_zuse[("im", c)] = e2
                    ev_s1.append(e2)
                ev_ev = []
                for c in range(4):
                    e1 = S.op("s", lambda e, c=c: e.copy(
                        s2re[:, c * 256:(c + 1) * 256], pA[c][:, 0:256]),
                        after=[ev_s1[c], ev_s2rd] + ev_twrd)
                    e2 = S.op("s", lambda e, c=c: e.copy(
                        s2im[:, c * 256:(c + 1) * 256], pA[c][:, 256:512]))
                    ev_ev += [e1, e2]
                ev_evA = ev_ev[-1]
                ev_tw = cplx6(s2re[:], s2im[:], s2re[:], s2im[:],
                              bc(T["twre"], 256), bc(T["twim"], 256),
                              ev_ev + [ev_s2rd])
                ev_twrd = ev_tw
                ev_vs2 = []
                for half in range(2):
                    sl = slice(half * 512, (half + 1) * 512)
                    e1 = S.op("t", lambda e, sl=sl, half=half: e.matmul(
                        pB[half][:, 0:512], T["f128re"][:], s2re[:, sl],
                        start=True, stop=False),
                        after=ev_tw + [ev_evB])
                    e1 = S.op("t", lambda e, sl=sl, half=half: e.matmul(
                        pB[half][:, 0:512], T["nf128im"][:], s2im[:, sl],
                        start=False, stop=True))
                    e2 = S.op("t", lambda e, sl=sl, half=half: e.matmul(
                        pB[2 + half][:, 0:512], T["f128re"][:], s2im[:, sl],
                        start=True, stop=False))
                    e2 = S.op("t", lambda e, sl=sl, half=half: e.matmul(
                        pB[2 + half][:, 0:512], T["f128im"][:], s2re[:, sl],
                        start=False, stop=True))
                    ev_vs2 += [e1, e2]
                ev_s2rd = ev_vs2[-1]
                ev_ev = []
                for half in range(2):
                    sl = slice(half * 512, (half + 1) * 512)
                    ev_ev.append(S.op("s", lambda e, sl=sl, half=half:
                                      e.copy(xre[:, sl],
                                                    pB[half][:, 0:512]),
                                      after=[ev_vs2[2 * half + 1]]
                                      + ev_prod))
                    ev_ev.append(S.op("s", lambda e, sl=sl, half=half:
                                      e.copy(xim[:, sl],
                                                    pB[2 + half][:, 0:512])))
                ev_evB = ev_ev[-1]
                ev_prod = cplx6(pre[:], pim[:], xre[:], xim[:], hre[:],
                                him[:], ev_ev + ev_hev + [ev_pird])
                ev_t1 = []
                for c in range(4):
                    evs = []
                    for r in range(2):
                        lre = pre[:, c * 256 + r * 128:c * 256 + (r + 1) * 128]
                        lim = pim[:, c * 256 + r * 128:c * 256 + (r + 1) * 128]
                        e2 = S.op("t", lambda e, c=c, r=r, lre=lre: e.matmul(
                            pA[c][:, r * 256:r * 256 + 256], lre,
                            T["cA256"][:], start=True, stop=False),
                            after=ev_prod + [ev_evA])
                        e2 = S.op("t", lambda e, c=c, r=r, lim=lim: e.matmul(
                            pA[c][:, r * 256:r * 256 + 256], lim,
                            T["cB256"][:], start=False, stop=True))
                        evs.append(e2)
                    ev_t1.append(evs)
                ev_pird = ev_t1[-1][-1]
                ev_ev = []
                for c in range(4):
                    for r in range(2):
                        ev_ev.append(S.op("s", lambda e, c=c, r=r:
                                          e.copy(
                            t2[("r", r)][:, c * 128:(c + 1) * 128],
                            pA[c][:, r * 256:r * 256 + 128]),
                            after=[ev_t1[c][1]] + ev_t2rd))
                        ev_ev.append(S.op("s", lambda e, c=c, r=r:
                                          e.copy(
                            t2[("i", r)][:, c * 128:(c + 1) * 128],
                            pA[c][:, r * 256 + 128:(r + 1) * 256])))
                ev_evA = ev_ev[-1]
                ev_itw = []
                for r in range(2):
                    def bcr(tile, r=r):
                        return bass.AP(tile, r * 128,
                                       [[256, 128], [0, 4], [1, 128]])
                    e6 = cplx6(t2p[("r", r)][:], t2p[("i", r)][:],
                               t2[("r", r)][:], t2[("i", r)][:],
                               bcr(T["itwre"]), bcr(T["itwim"]),
                               ev_ev + [ev_t2prd], n=512)
                    ev_itw.append(e6)
                ev_t2rd = ev_itw[-1]
                el = []
                for (bank, pl1, tb1, pl2, tb2) in [
                        (0, "r", "c256re", "i", "nc256im"),
                        (1, "r", "c256im", "i", "c256re")]:
                    e1 = None
                    for r in range(2):
                        e1 = S.op("t", lambda e, bank=bank, pl1=pl1,
                                  tb1=tb1, r=r: e.matmul(
                            pB[bank][:, 0:512],
                            bass.AP(T[tb1], r * 128, [[256, 128], [1, 128]]),
                            t2p[(pl1, r)][:],
                            start=(r == 0), stop=False),
                            after=ev_itw[r] + [ev_evB] if r == 0
                            else ev_itw[r])
                    for r in range(2):
                        e1 = S.op("t", lambda e, bank=bank, pl2=pl2,
                                  tb2=tb2, r=r: e.matmul(
                            pB[bank][:, 0:512],
                            bass.AP(T[tb2], r * 128, [[256, 128], [1, 128]]),
                            t2p[(pl2, r)][:],
                            start=False, stop=(r == 1)))
                    el.append(e1)
                ev_t2prd = el[-1]
                ev_o = []
                for pl in range(2):
                    eo = S.op("s", lambda e, pl=pl: e.copy(
                        zo[pl][:], pB[pl][:, 0:512]),
                        after=[el[pl], ev_zodma[pl]])
                    ev_o.append(eo)
                ev_evB = ev_o[-1]
                for pl in range(2):
                    last = None
                    for cc in range(4):
                        last = S.op("g", lambda e, pl=pl, cbase=cbase, cc=cc:
                                    e.dma_start(
                            out=bass.AP(conv_out,
                                        (cbase + cc) * 2 * L + pl * L,
                                        [[128, 128], [1, 128]]),
                            in_=zo[pl][:, cc * 128:(cc + 1) * 128]),
                            after=[ev_o[pl]], dma=True, ch=f"co{pl}")
                    ev_zodma[pl] = last

            S.build()
    return nc


def in_maps_B(inputs, v_relay):
    import ml_dtypes
    maps = []
    fw1 = np.asarray(inputs["L0_attn_fw1"], np.float32)
    fb1 = np.asarray(inputs["L0_attn_fb1"], np.float32).reshape(64, 1)
    fw2 = np.asarray(inputs["L0_attn_fw2"], np.float32)
    fb2 = np.asarray(inputs["L0_attn_fb2"], np.float32).reshape(128, 1)
    fw3 = np.asarray(inputs["L0_attn_fw3"], np.float32)
    fb3 = np.asarray(inputs["L0_attn_fb3"], np.float32)
    posn = np.arange(L, dtype=np.float32).reshape(1, L)
    j = np.arange(128)
    def wrap(x, n):
        return ((x + n // 2) % n) - n // 2
    k2g = np.arange(256)
    ec256 = wrap(64 - (j[:, None] * k2g[None, :]) % 256, 256).astype(np.float32)
    es256 = wrap(-(j[:, None] * k2g[None, :]) % 256, 256).astype(np.float32)
    k1g = np.arange(128)
    ec128 = wrap(32 - (j[:, None] * k1g[None, :]) % 128, 128).astype(np.float32)
    es128 = wrap(-(j[:, None] * k1g[None, :]) % 128, 128).astype(np.float32)
    n_ = 32768
    ectw = wrap(8192 - (j[:, None] * k2g[None, :]) % n_, n_).astype(np.float32)
    estw = wrap(-(j[:, None] * k2g[None, :]) % n_, n_).astype(np.float32)
    k2f = np.arange(256)
    itw_c = wrap(8192 - (k2f[:, None] * j[None, :]) % n_, n_)
    itw_s = wrap(+(k2f[:, None] * j[None, :]) % n_, n_)
    ecitw = np.concatenate([itw_c[0:128], itw_c[128:256]], axis=1).astype(np.float32)
    esitw = np.concatenate([itw_s[0:128] * 0 + wrap(-(k2f[0:128, None] * j[None, :]) % n_, n_),
                            wrap(-(k2f[128:256, None] * j[None, :]) % n_, n_)],
                           axis=1).astype(np.float32)
    j2g = np.arange(128)
    c_c = wrap(64 - (k2f[:, None] * j2g[None, :]) % 256, 256)
    c_s = wrap(-(k2f[:, None] * j2g[None, :]) % 256, 256)
    ecc256 = np.concatenate([c_c[0:128], c_c[128:256]], axis=1).astype(np.float32)
    esc256 = np.concatenate([c_s[0:128], c_s[128:256]], axis=1).astype(np.float32)
    for c in range(NC):
        vloc = np.empty((128, 2 * L), dtype=ml_dtypes.bfloat16)
        for s in range(NC):
            b, t0 = s // 4, (s % 4) * TPC
            vloc[:, b * L + t0:b * L + t0 + TPC] =                 v_relay[s][c * 128:(c + 1) * 128]
        ci = np.arange(c * 128, (c + 1) * 128)
        maps.append({
            "v_in": vloc, "fw1": fw1, "fb1": fb1, "fw2": fw2, "fb2": fb2,
            "fw3c": np.ascontiguousarray(fw3[:, 2 * ci + 1]),
            "fb3c": np.ascontiguousarray(fb3[2 * ci + 1]).reshape(128, 1),
            "posn": posn,
            "ec256": ec256, "es256": es256, "ec128": ec128, "es128": es128,
            "ectw": ectw, "estw": estw, "ecitw": ecitw, "esitw": esitw,
            "ecc256": ecc256, "esc256": esc256,
        })
    return maps



def build_C():
    nc = bass.Bass("TRN2", target_bir_lowering=False)
    xt = nc.dram_tensor("xt", [D, W], F32, kind="ExternalInput")
    x2t = nc.dram_tensor("x2t", [D, W], BF16, kind="ExternalInput")
    cvt = nc.dram_tensor("cvt", [D, W], BF16, kind="ExternalInput")
    wout = nc.dram_tensor("wout", [D, D], F32, kind="ExternalInput")
    bout = nc.dram_tensor("bout", [128, 8], F32, kind="ExternalInput")
    id_h = nc.dram_tensor("id_h", [128, 128], F32, kind="ExternalInput")
    x1_out = nc.dram_tensor("x1_out", [D, W], F32, kind="ExternalOutput")
    xln2_out = nc.dram_tensor("xln2_out", [D, W], BF16, kind="ExternalOutput")
    m1_out = nc.dram_tensor("m1_out", [1, 1], F32, kind="ExternalOutput")

    es = contextlib.ExitStack()
    with es:
        S = Sched(nc)
        wo_sb = alloc(nc, es, "wo", [128, 8 * 8 * 128], BF16)
        ones_sb = alloc(nc, es, "ones", [128, 128], BF16)
        bo_sb = alloc(nc, es, "bo", [128, 8], F32)
        id_sb = alloc(nc, es, "idm", [128, 128], BF16)
        xf = [alloc(nc, es, f"xf{i}", [128, 8 * 512], F32) for i in range(2)]
        x2b = [alloc(nc, es, f"x2b{i}", [128, 8 * 512], BF16) for i in range(2)]
        cvb = [alloc(nc, es, f"cvb{i}", [128, 8 * 512], BF16) for i in range(2)]
        ub = alloc(nc, es, "ub", [128, 8 * 512], BF16)
        yb = alloc(nc, es, "yb", [128, 8 * 512], F32)
        ybf = alloc(nc, es, "ybf", [128, 8 * 512], BF16)
        ysq = alloc(nc, es, "ysq", [128, 8 * 512], BF16)
        mu_ = alloc(nc, es, "mu_", [128, 512], F32)
        rb_ = alloc(nc, es, "rb_", [128, 512], F32)
        ts1 = alloc(nc, es, "ts1", [128, 512], F32)
        ts2 = alloc(nc, es, "ts2", [128, 512], F32)
        xl2 = [alloc(nc, es, f"xl2{i}", [128, 8 * 512], BF16) for i in range(2)]
        mxc = alloc(nc, es, "mxc", [128, 16], F32)
        mxt = alloc(nc, es, "mxt", [128, 128], BF16)
        m1sb = alloc(nc, es, "m1sb", [1, 1], F32)

        with (nc.psum_tensor("psA", [128, 512], F32) as psA,
              nc.psum_tensor("psB", [128, 512], F32) as psB,
              nc.psum_tensor("pss1", [128, 512], F32) as pss1,
              nc.psum_tensor("pss2", [128, 512], F32) as pss2,
              nc.psum_tensor("ptp", [128, 128], BF16) as ptp):

            ev_w = S.op("g", lambda e: e.dma_start(
                out=ap3(wo_sb, 8192, 128, (1024, 8), (1, 1024)),
                in_=bass.AP(wout, 0, [[D, 128], [128 * D, 8], [1, 1024]])),
                dma=True, ch="cstg")
            ev_id = S.op("g", lambda e: e.dma_start(
                out=id_sb[:], in_=id_h.ap()), dma=True, ch="cstg")
            S.op("g", lambda e: e.memset(ones_sb[:], 1.0), ch="cstg2")
            S.op("g", lambda e: e.memset(mxc[:], 0.0), ch="cstg2")
            ev_on = S.op("g", lambda e: e.memset(mxt[:], 0.0), ch="cstg2")
            ev_bo = S.op("y", lambda e: e.dma_start(
                out=bo_sb[:], in_=bout.ap()), dma=True, ch="cst")
            ev_cst = ("cst", S.n["cst"])
            ev_cstg = ("cstg", S.n["cstg"])
            ev_cstg2 = ("cstg2", S.n["cstg2"])

            ev_use = {0: None, 1: None}
            ev_evac = {0: None, 1: None}
            ev_x1st = {0: None, 1: None}
            ev_xl2st = {0: None, 1: None}
            ev_srd = None
            ev_ssq = None
            ev_mmu = None
            ev_mxw = ev_cstg2
            col = 0
            for ti, cw in enumerate(TILES):
                bi = ti % 2
                c0 = col
                col += cw
                el1 = S.op("y", lambda e, c0=c0, cw=cw, bi=bi: e.dma_start(
                    out=ap3(xf[bi], 8 * 512, 128, (cw, 8), (1, cw)),
                    in_=bass.AP(xt, c0, [[W, 128], [128 * W, 8], [1, cw]])),
                    after=[ev_use[bi]], dma=True, ch=f"lx{bi}")
                el2 = S.op("y", lambda e, c0=c0, cw=cw, bi=bi: e.dma_start(
                    out=ap3(x2b[bi], 8 * 512, 128, (cw, 8), (1, cw)),
                    in_=bass.AP(x2t, c0, [[W, 128], [128 * W, 8], [1, cw]])),
                    after=[ev_use[bi]], dma=True, ch=f"l2{bi}")
                el3 = S.op("y", lambda e, c0=c0, cw=cw, bi=bi: e.dma_start(
                    out=ap3(cvb[bi], 8 * 512, 128, (cw, 8), (1, cw)),
                    in_=bass.AP(cvt, c0, [[W, 128], [128 * W, 8], [1, cw]])),
                    after=[ev_use[bi]], dma=True, ch=f"lc{bi}")
                evu = S.op("v", lambda e, cw=cw, bi=bi: e.tensor_tensor(
                    ub[:, 0:8 * cw], x2b[bi][:, 0:8 * cw],
                    cvb[bi][:, 0:8 * cw], ALU.mult), after=[el2, el3, ev_mmu])
                ev_ev = []
                for m in range(8):
                    pbank = psA if m % 2 == 0 else psB
                    evk = None
                    for k in range(8):
                        aft = ([evu, ev_w, ev_evac[m % 2]] if k == 0 else [])
                        evk = S.op("t", lambda e, k=k, m=m, cw=cw,
                                   pbank=pbank: e.matmul(
                            pbank[:, 0:cw],
                            wo_sb[:, (k * 8 + m) * 128:(k * 8 + m + 1) * 128],
                            ub[:, k * cw:(k + 1) * cw],
                            start=(k == 0), stop=(k == 7)), after=aft)
                    ev_evac[m % 2] = S.op("s", lambda e, m=m, cw=cw,
                                          pbank=pbank: e.activation(
                        yb[:, m * cw:(m + 1) * cw], pbank[:, 0:cw],
                        AF.Identity, bias=bo_sb[:, m:m + 1]),
                        after=[evk, ev_cst])
                    ev_ev.append(ev_evac[m % 2])
                ev_mmu = evk
                evc = S.op("v", lambda e, cw=cw: e.tensor_copy(
                    ybf[:, 0:8 * cw], yb[:, 0:8 * cw]), after=ev_ev)
                evq = S.op("s", lambda e, cw=cw: e.activation(
                    ysq[:, 0:8 * cw], ybf[:, 0:8 * cw], AF.Square),
                    after=[evc, ev_ssq])
                evm = None
                for k in range(8):
                    aft = [evc, ev_cstg2, ev_srd] if k == 0 else []
                    evm = S.op("t", lambda e, k=k, cw=cw: e.matmul(
                        pss1[:, 0:cw], ones_sb[:], ybf[:, k * cw:(k + 1) * cw],
                        start=(k == 0), stop=(k == 7)), after=aft)
                for k in range(8):
                    aft = [evq] if k == 0 else []
                    evm = S.op("t", lambda e, k=k, cw=cw: e.matmul(
                        pss2[:, 0:cw], ones_sb[:], ysq[:, k * cw:(k + 1) * cw],
                        start=(k == 0), stop=(k == 7)), after=aft)
                ev_ssq = evm

                def stat_chain(evm, cw):
                    e1 = S.op("v", lambda e: e.tensor_scalar(
                        mu_[:, 0:cw], pss1[:, 0:cw], 1.0 / D, None, ALU.mult),
                        after=[evm])
                    e2 = S.op("v", lambda e: e.tensor_scalar(
                        ts1[:, 0:cw], pss2[:, 0:cw], 1.0 / D, 1e-5,
                        ALU.mult, ALU.add))
                    e3 = S.op("v", lambda e: e.tensor_tensor(
                        ts2[:, 0:cw], mu_[:, 0:cw], mu_[:, 0:cw], ALU.mult))
                    e4 = S.op("v", lambda e: e.tensor_tensor(
                        ts1[:, 0:cw], ts1[:, 0:cw], ts2[:, 0:cw],
                        ALU.subtract))
                    e5 = S.op("s", lambda e: e.activation(
                        ts2[:, 0:cw], ts1[:, 0:cw], AF.Sqrt), after=[e4])
                    e6 = S.op("v", lambda e: e.reciprocal(
                        rb_[:, 0:cw], ts2[:, 0:cw]), after=[e5])
                    return e2, e6

                ev_srd, e6 = stat_chain(evm, cw)
                e7 = S.op("v", lambda e, cw=cw: e.tensor_tensor(
                    yb[:, 0:8 * cw], yb[:, 0:8 * cw],
                    bass.AP(mu_, 0, [[512, 128], [0, 8], [1, cw]]),
                    ALU.subtract), after=[e6])
                e8 = S.op("v", lambda e, cw=cw: e.tensor_tensor(
                    yb[:, 0:8 * cw], yb[:, 0:8 * cw],
                    bass.AP(rb_, 0, [[512, 128], [0, 8], [1, cw]]), ALU.mult))
                e9 = S.op("v", lambda e, cw=cw, bi=bi: e.tensor_tensor(
                    xf[bi][:, 0:8 * cw], xf[bi][:, 0:8 * cw],
                    yb[:, 0:8 * cw], ALU.add), after=[e8, el1])
                ev_x1st[bi] = S.op("g", lambda e, c0=c0, cw=cw, bi=bi:
                                   e.dma_start(
                    out=bass.AP(x1_out, c0, [[W, 128], [128 * W, 8], [1, cw]]),
                    in_=ap3(xf[bi], 8 * 512, 128, (cw, 8), (1, cw))),
                    after=[e9], dma=True, ch=f"sx{bi}")
                evc2 = S.op("v", lambda e, cw=cw, bi=bi: e.tensor_copy(
                    ybf[:, 0:8 * cw], xf[bi][:, 0:8 * cw]), after=[e9])
                evq2 = S.op("s", lambda e, cw=cw: e.activation(
                    ysq[:, 0:8 * cw], ybf[:, 0:8 * cw], AF.Square),
                    after=[evc2])
                evm2 = None
                for k in range(8):
                    aft = [evc2, ev_srd] if k == 0 else []
                    evm2 = S.op("t", lambda e, k=k, cw=cw: e.matmul(
                        pss1[:, 0:cw], ones_sb[:], ybf[:, k * cw:(k + 1) * cw],
                        start=(k == 0), stop=(k == 7)), after=aft)
                for k in range(8):
                    aft = [evq2] if k == 0 else []
                    evm2 = S.op("t", lambda e, k=k, cw=cw: e.matmul(
                        pss2[:, 0:cw], ones_sb[:], ysq[:, k * cw:(k + 1) * cw],
                        start=(k == 0), stop=(k == 7)), after=aft)
                ev_ssq = evm2
                ev_srd, e6b = stat_chain(evm2, cw)
                ea = S.op("v", lambda e, cw=cw, bi=bi: e.tensor_tensor(
                    xf[bi][:, 0:8 * cw], xf[bi][:, 0:8 * cw],
                    bass.AP(mu_, 0, [[512, 128], [0, 8], [1, cw]]),
                    ALU.subtract), after=[e6b, ev_x1st[bi]])
                eb = S.op("v", lambda e, cw=cw, bi=bi: e.tensor_tensor(
                    xl2[bi][:, 0:8 * cw], xf[bi][:, 0:8 * cw],
                    bass.AP(rb_, 0, [[512, 128], [0, 8], [1, cw]]), ALU.mult),
                    after=[ev_xl2st[bi]])
                ev_use[bi] = eb
                ev_xl2st[bi] = S.op("g", lambda e, c0=c0, cw=cw, bi=bi:
                                    e.dma_start(
                    out=bass.AP(xln2_out, c0,
                                [[W, 128], [128 * W, 8], [1, cw]]),
                    in_=ap3(xl2[bi], 8 * 512, 128, (cw, 8), (1, cw))),
                    after=[eb], dma=True, ch=f"s2{bi}")
                lo = HALO - c0 if c0 < HALO else 0
                hi = cw if c0 + cw <= W - HALO else max(0, W - HALO - c0)
                if hi > lo:
                    ev_mxw = S.op("v", lambda e, cw=cw, bi=bi, ti=ti, lo=lo,
                                  hi=hi: e.tensor_reduce(
                        mxc[:, ti:ti + 1],
                        bass.AP(xl2[bi], lo, [[8 * 512, 128], [cw, 8],
                                              [1, hi - lo]]),
                        AX.XY, ALU.max, apply_absolute_value=True),
                        after=[eb, ev_mxw])

            ev1 = S.op("v", lambda e: e.tensor_reduce(
                bass.AP(mxt, 0, [[128, 128], [1, 1]]), mxc[:], AX.X, ALU.max,
                apply_absolute_value=True), after=[ev_mxw, ev_on])
            ev2 = S.op("t", lambda e: e.transpose(
                ptp[:], mxt[:], id_sb[:]), after=[ev1, ev_id])
            ev3 = S.op("v", lambda e: e.tensor_reduce(
                m1sb[:], ptp[0:1, :], AX.X, ALU.max,
                apply_absolute_value=True), after=[ev2])
            ev4 = S.op("g", lambda e: e.dma_start(
                out=m1_out.ap(), in_=m1sb[:]), after=[ev3], dma=True, ch="m1")
            S.build()
    return nc



def build_D():
    nc = bass.Bass("TRN2", target_bir_lowering=False)
    xlt = nc.dram_tensor("xlt", [D, W], BF16, kind="ExternalInput")
    m1r = nc.dram_tensor("m1r", [128, 8], F32, kind="ExternalInput")
    wgT = nc.dram_tensor("wgT", [D, IH], BF16, kind="ExternalInput")
    wuT = nc.dram_tensor("wuT", [D, IH], BF16, kind="ExternalInput")
    one128 = nc.dram_tensor("one128", [128, 1], F32, kind="ExternalInput")
    id_h = nc.dram_tensor("id_h", [128, 128], F32, kind="ExternalInput")
    g_out = nc.dram_tensor("g_out", [IH, W], BF16, kind="ExternalOutput")
    u_out = nc.dram_tensor("u_out", [IH, W], BF16, kind="ExternalOutput")
    mg_out = nc.dram_tensor("mg_out", [1, 1], F32, kind="ExternalOutput")
    mu_out = nc.dram_tensor("mu_out", [1, 1], F32, kind="ExternalOutput")

    es = contextlib.ExitStack()
    with es:
        S = Sched(nc)
        wg_sb = alloc(nc, es, "wg", [128, 24 * 8 * 128], BF16)
        wu_sb = alloc(nc, es, "wu", [128, 24 * 8 * 128], BF16)
        id_sb = alloc(nc, es, "idm", [128, 128], BF16)
        one_sb = alloc(nc, es, "one", [128, 1], F32)
        g_sb = alloc(nc, es, "gflag", [128, 1], F32)
        m1_sb = alloc(nc, es, "m1s", [128, 8], F32)
        s1_sb = alloc(nc, es, "s1s", [128, 1], F32)
        r1_sb = alloc(nc, es, "r1s", [128, 1], F32)
        xlb = [alloc(nc, es, f"xlb{i}", [128, 8 * 512], BF16) for i in range(2)]
        tq = alloc(nc, es, "tq", [128, 8 * 512], F32)
        ta = alloc(nc, es, "ta", [128, 8 * 512], F32)
        tb = alloc(nc, es, "tb", [128, 8 * 512], F32)
        xq = [alloc(nc, es, f"xq{i}", [128, 8 * 512], BF16) for i in range(2)]
        gch = [alloc(nc, es, f"gch{i}", [128, 512], BF16) for i in range(4)]
        mxg = alloc(nc, es, "mxg", [128, 1], F32)
        mxu = alloc(nc, es, "mxu", [128, 1], F32)
        mtmp = alloc(nc, es, "mtmp", [128, 1], F32)
        mxt = alloc(nc, es, "mxt", [128, 128], BF16)
        msc = alloc(nc, es, "msc", [1, 1], F32)

        with (nc.psum_tensor("pg0", [128, 512], F32) as pg0,
              nc.psum_tensor("pg1", [128, 512], F32) as pg1,
              nc.psum_tensor("pu0", [128, 512], F32) as pu0,
              nc.psum_tensor("pu1", [128, 512], F32) as pu1,
              nc.psum_tensor("ptp", [128, 128], BF16) as ptp):
            ev_wg = S.op("g", lambda e: e.dma_start(
                out=ap3(wg_sb, 24 * 1024, 128, (3072, 8), (1, 3072)),
                in_=bass.AP(wgT, 0, [[IH, 128], [128 * IH, 8], [1, 3072]])),
                dma=True, ch="cstg")
            ev_wu = S.op("g", lambda e: e.dma_start(
                out=ap3(wu_sb, 24 * 1024, 128, (3072, 8), (1, 3072)),
                in_=bass.AP(wuT, 0, [[IH, 128], [128 * IH, 8], [1, 3072]])),
                dma=True, ch="cstg")
            ev_id = S.op("g", lambda e: e.dma_start(
                out=id_sb[:], in_=id_h.ap()), dma=True, ch="cstg")
            ev_m = S.op("y", lambda e: e.dma_start(
                out=m1_sb[:], in_=m1r.ap()), dma=True, ch="cst")
            ev_1 = S.op("y", lambda e: e.dma_start(
                out=one_sb[:], in_=one128.ap()), dma=True, ch="cst")
            ev_mz = S.op("g", lambda e: e.memset(mxg[:], 0.0), ch="cstg2")
            ev_mz = S.op("g", lambda e: e.memset(mxu[:], 0.0), ch="cstg2")
            ev_mz2 = S.op("g", lambda e: e.memset(mxt[:], 0.0), ch="cstg2")
            ev_cst = ("cst", S.n["cst"])
            ev_cstg = ("cstg", S.n["cstg"])
            ev_cstg2 = ("cstg2", S.n["cstg2"])
            ev_g = emit_flag_g(S, one_sb, g_sb, ev_cst)
            ev = S.op("v", lambda e: e.tensor_reduce(
                s1_sb[:], m1_sb[:], AX.X, ALU.max), after=[ev_m])
            ev = S.op("v", lambda e: e.tensor_scalar(
                s1_sb[:], s1_sb[:], 1.0 / 32767.0, 1e-12, ALU.mult, ALU.max))
            ev_s1 = S.op("v", lambda e: e.reciprocal(r1_sb[:], s1_sb[:]))

            ev_use = {0: None, 1: None}
            ev_gst = {i: None for i in range(4)}
            ev_evac = {i: None for i in range(4)}
            ev_red = {i: None for i in range(4)}
            ev_mx = ev_cstg2
            col = 0
            for ti, cw in enumerate(TILES):
                bi = ti % 2
                c0 = col
                col += cw
                el = S.op("y", lambda e, c0=c0, cw=cw, bi=bi: e.dma_start(
                    out=ap3(xlb[bi], 8 * 512, 128, (cw, 8), (1, cw)),
                    in_=bass.AP(xlt, c0, [[W, 128], [128 * W, 8], [1, cw]])),
                    after=[ev_use[bi]], dma=True, ch=f"lx{bi}")
                e1 = S.op("v", lambda e, cw=cw, bi=bi: e.tensor_scalar(
                    tq[:, 0:8 * cw], xlb[bi][:, 0:8 * cw], r1_sb[:], None,
                    ALU.mult), after=[el, ev_s1])
                e2 = emit_round(S, tq[:, 0:8 * cw], tq[:, 0:8 * cw], g_sb,
                                ta[:, 0:8 * cw], tb[:, 0:8 * cw],
                                after=[e1, ev_g])
                e3 = S.op("v", lambda e, cw=cw, bi=bi: e.tensor_scalar(
                    xq[bi][:, 0:8 * cw], tq[:, 0:8 * cw], s1_sb[:], None,
                    ALU.mult), after=[e2])
                ev_use[bi] = e3
                last_mm = None
                for which, wsb, evw, outt, mxacc in [
                        (0, wg_sb, ev_wg, g_out, mxg),
                        (1, wu_sb, ev_wu, u_out, mxu)]:
                    for m in range(24):
                        slot = which * 2 + (m % 2)
                        pbank = [pg0, pg1, pu0, pu1][slot]
                        evk = None
                        for k in range(8):
                            aft = ([e3, evw, ev_evac[slot]] if k == 0 else [])
                            evk = S.op("t", lambda e, k=k, m=m, cw=cw, bi=bi,
                                       wsb=wsb, pbank=pbank: e.matmul(
                                pbank[:, 0:cw],
                                wsb[:, (k * 24 + m) * 128:
                                       (k * 24 + m + 1) * 128],
                                xq[bi][:, k * cw:(k + 1) * cw],
                                start=(k == 0), stop=(k == 7)), after=aft)
                        ecp = S.op("s", lambda e, slot=slot, cw=cw,
                                   pbank=pbank: e.copy(
                            gch[slot][:, 0:cw], pbank[:, 0:cw]),
                            after=[evk, ev_gst[slot], ev_red[slot]])
                        ev_evac[slot] = ecp
                        lo = HALO - c0 if c0 < HALO else 0
                        hi = cw if c0 + cw <= W - HALO else max(0, W - HALO - c0)
                        if hi > lo:
                            ev_mx = S.op("v", lambda e, slot=slot, lo=lo,
                                         hi=hi: e.tensor_reduce(
                                mtmp[:], gch[slot][:, lo:hi], AX.X, ALU.max,
                                apply_absolute_value=True),
                                after=[ecp, ev_mx])
                            ev_red[slot] = ev_mx
                            ev_mx = S.op("v", lambda e, mxacc=mxacc:
                                         e.tensor_tensor(
                                mxacc[:], mxacc[:], mtmp[:], ALU.max),
                                after=[ev_mx])
                        ev_gst[slot] = S.op("g", lambda e, m=m, c0=c0, cw=cw,
                                            slot=slot, outt=outt: e.dma_start(
                            out=bass.AP(outt, m * 128 * W + c0,
                                        [[W, 128], [1, cw]]),
                            in_=gch[slot][:, 0:cw]), after=[ecp], dma=True,
                            ch=f"gs{slot}")
                        last_mm = evk
            for mxacc, outm in [(mxg, mg_out), (mxu, mu_out)]:
                ev1 = S.op("v", lambda e, mxacc=mxacc: e.tensor_copy(
                    bass.AP(mxt, 0, [[128, 128], [1, 1]]), mxacc[:]),
                    after=[ev_mx, ev_mz2])
                ev2 = S.op("t", lambda e: e.transpose(
                    ptp[:], mxt[:], id_sb[:]), after=[ev1, ev_id])
                ev3 = S.op("v", lambda e: e.tensor_reduce(
                    msc[:], ptp[0:1, :], AX.X, ALU.max,
                    apply_absolute_value=True), after=[ev2])
                ev4 = S.op("g", lambda e, outm=outm: e.dma_start(
                    out=outm.ap(), in_=msc[:]), after=[ev3], dma=True,
                    ch="mo")
                ev_mz2 = ev4
                ev_id = ev4
            S.build()
    return nc



def build_E():
    nc = bass.Bass("TRN2", target_bir_lowering=False)
    g_t = nc.dram_tensor("g_t", [IH, W], BF16, kind="ExternalInput")
    u_t = nc.dram_tensor("u_t", [IH, W], BF16, kind="ExternalInput")
    mgr = nc.dram_tensor("mgr", [128, 8], F32, kind="ExternalInput")
    mur = nc.dram_tensor("mur", [128, 8], F32, kind="ExternalInput")
    one128 = nc.dram_tensor("one128", [128, 1], F32, kind="ExternalInput")
    id_h = nc.dram_tensor("id_h", [128, 128], F32, kind="ExternalInput")
    t0_out = nc.dram_tensor("t0_out", [IH, W], BF16, kind="ExternalOutput")
    uq_out = nc.dram_tensor("uq_out", [IH, W], BF16, kind="ExternalOutput")
    m4_out = nc.dram_tensor("m4_out", [1, 1], F32, kind="ExternalOutput")

    es = contextlib.ExitStack()
    with es:
        S = Sched(nc)
        id_sb = alloc(nc, es, "idm", [128, 128], BF16)
        one_sb = alloc(nc, es, "one", [128, 1], F32)
        g_sb = alloc(nc, es, "gflag", [128, 1], F32)
        mg_sb = alloc(nc, es, "mgs", [128, 8], F32)
        mu_sb = alloc(nc, es, "mus", [128, 8], F32)
        s3_sb = alloc(nc, es, "s3s", [128, 1], F32)
        r3_sb = alloc(nc, es, "r3s", [128, 1], F32)
        s2_sb = alloc(nc, es, "s2s", [128, 1], F32)
        r2_sb = alloc(nc, es, "r2s", [128, 1], F32)
        gb = [alloc(nc, es, f"gb{i}", [128, W], BF16) for i in range(2)]
        ub = [alloc(nc, es, f"ub{i}", [128, W], BF16) for i in range(2)]
        tg = [alloc(nc, es, f"tg{i}", [128, W], F32) for i in range(2)]
        sg = [alloc(nc, es, f"sg{i}", [128, W], F32) for i in range(2)]
        ta = [alloc(nc, es, f"ta{i}", [128, W], F32) for i in range(2)]
        tb = [alloc(nc, es, f"tb{i}", [128, W], F32) for i in range(2)]
        t0b = [alloc(nc, es, f"t0b{i}", [128, W], BF16) for i in range(2)]
        uqb = [alloc(nc, es, f"uqb{i}", [128, W], BF16) for i in range(2)]
        mx4 = alloc(nc, es, "mx4", [128, 1], F32)
        mtmp = alloc(nc, es, "mtmp", [128, 1], F32)
        mxt = alloc(nc, es, "mxt", [128, 128], BF16)
        msc = alloc(nc, es, "msc", [1, 1], F32)
        with nc.psum_tensor("ptp", [128, 128], BF16) as ptp:
            ev_id = S.op("g", lambda e: e.dma_start(
                out=id_sb[:], in_=id_h.ap()), dma=True, ch="cstg")
            ev_mg = S.op("y", lambda e: e.dma_start(
                out=mg_sb[:], in_=mgr.ap()), dma=True, ch="cst")
            ev_mu = S.op("y", lambda e: e.dma_start(
                out=mu_sb[:], in_=mur.ap()), dma=True, ch="cst")
            ev_1 = S.op("y", lambda e: e.dma_start(
                out=one_sb[:], in_=one128.ap()), dma=True, ch="cst")
            S.op("g", lambda e: e.memset(mx4[:], 0.0), ch="cstg2")
            ev_mz = S.op("g", lambda e: e.memset(mxt[:], 0.0), ch="cstg2")
            ev_cst = ("cst", S.n["cst"])
            ev_cstg2 = ("cstg2", S.n["cstg2"])
            ev_g = emit_flag_g(S, one_sb, g_sb, ev_cst)
            for src_sb, ssb, rsb in [(mg_sb, s3_sb, r3_sb),
                                     (mu_sb, s2_sb, r2_sb)]:
                ev = S.op("v", lambda e, a=src_sb, b=ssb: e.tensor_reduce(
                    b[:], a[:], AX.X, ALU.max), after=[ev_cst])
                ev = S.op("v", lambda e, b=ssb: e.tensor_scalar(
                    b[:], b[:], 1.0 / 32767.0, 1e-12, ALU.mult, ALU.max))
                ev_sc = S.op("v", lambda e, b=ssb, r=rsb: e.reciprocal(
                    r[:], b[:]))

            ev_use = {0: None, 1: None}
            ev_st = {0: None, 1: None}
            ev_ust = {0: None, 1: None}
            ev_mx = ev_cstg2
            for hc in range(24):
                bi = hc % 2
                el = S.op("y", lambda e, hc=hc, bi=bi: e.dma_start(
                    out=gb[bi][:], in_=bass.AP(g_t, hc * 128 * W,
                                               [[W, 128], [1, W]])),
                    after=[ev_use[bi]], dma=True, ch=f"lg{bi}")
                elu = S.op("y", lambda e, hc=hc, bi=bi: e.dma_start(
                    out=ub[bi][:], in_=bass.AP(u_t, hc * 128 * W,
                                               [[W, 128], [1, W]])),
                    after=[ev_use[bi]], dma=True, ch=f"lu{bi}")
                e1 = S.op("v", lambda e, bi=bi: e.tensor_scalar(
                    tg[bi][:], gb[bi][:], r3_sb[:], None, ALU.mult),
                    after=[el, ev_sc, ev_g])
                e2 = emit_round(S, tg[bi][:], tg[bi][:], g_sb, ta[bi][:],
                                tb[bi][:], after=[e1])
                e3 = S.op("s", lambda e, bi=bi: e.activation(
                    sg[bi][:], tg[bi][:], AF.Sigmoid, scale=s3_sb[:]),
                    after=[e2])
                e4 = S.op("v", lambda e, bi=bi: e.tensor_scalar(
                    sg[bi][:], sg[bi][:], 65536.0, None, ALU.mult),
                    after=[e3])
                e5 = emit_round(S, sg[bi][:], sg[bi][:], g_sb, ta[bi][:],
                                tb[bi][:], after=[e4])
                e6 = S.op("v", lambda e, bi=bi: e.tensor_scalar(
                    sg[bi][:], sg[bi][:], 65535.0, None, ALU.min))
                e7 = S.op("v", lambda e, bi=bi: e.tensor_tensor(
                    t0b[bi][:], tg[bi][:], sg[bi][:], ALU.mult),
                    after=[e6, ev_st[bi]])
                e8 = S.op("v", lambda e, bi=bi: e.tensor_reduce(
                    mtmp[:], t0b[bi][:, HALO:W - HALO], AX.X, ALU.max,
                    apply_absolute_value=True), after=[e7, ev_mx])
                ev_mx = S.op("v", lambda e: e.tensor_tensor(
                    mx4[:], mx4[:], mtmp[:], ALU.max))
                ev_st[bi] = S.op("g", lambda e, hc=hc, bi=bi: e.dma_start(
                    out=bass.AP(t0_out, hc * 128 * W, [[W, 128], [1, W]]),
                    in_=t0b[bi][:]), after=[e7], dma=True, ch=f"st{bi}")
                f1 = S.op("v", lambda e, bi=bi: e.tensor_scalar(
                    tg[bi][:], ub[bi][:], r2_sb[:], None, ALU.mult),
                    after=[elu, e7])
                f2 = emit_round(S, tg[bi][:], tg[bi][:], g_sb, ta[bi][:],
                                tb[bi][:], after=[f1])
                f3 = S.op("v", lambda e, bi=bi: e.tensor_scalar(
                    uqb[bi][:], tg[bi][:], s2_sb[:], None, ALU.mult),
                    after=[f2, ev_ust[bi]])
                ev_use[bi] = f3
                ev_ust[bi] = S.op("g", lambda e, hc=hc, bi=bi: e.dma_start(
                    out=bass.AP(uq_out, hc * 128 * W, [[W, 128], [1, W]]),
                    in_=uqb[bi][:]), after=[f3], dma=True, ch=f"su{bi}")
            ev1 = S.op("v", lambda e: e.tensor_copy(
                bass.AP(mxt, 0, [[128, 128], [1, 1]]), mx4[:]),
                after=[ev_mx, ev_mz])
            ev2 = S.op("t", lambda e: e.transpose(
                ptp[:], mxt[:], id_sb[:]), after=[ev1, ev_id])
            ev3 = S.op("v", lambda e: e.tensor_reduce(
                msc[:], ptp[0:1, :], AX.X, ALU.max,
                apply_absolute_value=True), after=[ev2])
            S.op("g", lambda e: e.dma_start(
                out=m4_out.ap(), in_=msc[:]), after=[ev3], dma=True, ch="mo")
            S.build()
    return nc



def build_F():
    nc = bass.Bass("TRN2", target_bir_lowering=False)
    t0_t = nc.dram_tensor("t0_t", [IH, W], BF16, kind="ExternalInput")
    uq_t = nc.dram_tensor("uq_t", [IH, W], BF16, kind="ExternalInput")
    m4r = nc.dram_tensor("m4r", [128, 8], F32, kind="ExternalInput")
    mgr = nc.dram_tensor("mgr", [128, 8], F32, kind="ExternalInput")
    one128 = nc.dram_tensor("one128", [128, 1], F32, kind="ExternalInput")
    id_h = nc.dram_tensor("id_h", [128, 128], F32, kind="ExternalInput")
    u0_out = nc.dram_tensor("u0_out", [IH, W], BF16, kind="ExternalOutput")
    m5_out = nc.dram_tensor("m5_out", [1, 1], F32, kind="ExternalOutput")

    es = contextlib.ExitStack()
    with es:
        S = Sched(nc)
        id_sb = alloc(nc, es, "idm", [128, 128], BF16)
        one_sb = alloc(nc, es, "one", [128, 1], F32)
        g_sb = alloc(nc, es, "gflag", [128, 1], F32)
        m4_sb = alloc(nc, es, "m4s", [128, 8], F32)
        mg_sb = alloc(nc, es, "mgs", [128, 8], F32)
        s3_sb = alloc(nc, es, "s3s", [128, 1], F32)
        s4_sb = alloc(nc, es, "s4s", [128, 1], F32)
        rf_sb = alloc(nc, es, "rfs", [128, 1], F32)
        t0b = [alloc(nc, es, f"t0b{i}", [128, W], BF16) for i in range(2)]
        uqb = [alloc(nc, es, f"uqb{i}", [128, W], BF16) for i in range(2)]
        tg = [alloc(nc, es, f"tg{i}", [128, W], F32) for i in range(2)]
        ta = [alloc(nc, es, f"ta{i}", [128, W], F32) for i in range(2)]
        tb = [alloc(nc, es, f"tb{i}", [128, W], F32) for i in range(2)]
        u0b = [alloc(nc, es, f"u0b{i}", [128, W], BF16) for i in range(2)]
        mx5 = alloc(nc, es, "mx5", [128, 1], F32)
        mtmp = alloc(nc, es, "mtmp", [128, 1], F32)
        mxt = alloc(nc, es, "mxt", [128, 128], BF16)
        msc = alloc(nc, es, "msc", [1, 1], F32)
        with nc.psum_tensor("ptp", [128, 128], BF16) as ptp:
            ev_id = S.op("g", lambda e: e.dma_start(
                out=id_sb[:], in_=id_h.ap()), dma=True, ch="cstg")
            ev_m4 = S.op("y", lambda e: e.dma_start(
                out=m4_sb[:], in_=m4r.ap()), dma=True, ch="cst")
            ev_mg = S.op("y", lambda e: e.dma_start(
                out=mg_sb[:], in_=mgr.ap()), dma=True, ch="cst")
            ev_1 = S.op("y", lambda e: e.dma_start(
                out=one_sb[:], in_=one128.ap()), dma=True, ch="cst")
            S.op("g", lambda e: e.memset(mx5[:], 0.0), ch="cstg2")
            ev_mz = S.op("g", lambda e: e.memset(mxt[:], 0.0), ch="cstg2")
            ev_cst = ("cst", S.n["cst"])
            ev_cstg2 = ("cstg2", S.n["cstg2"])
            ev_g = emit_flag_g(S, one_sb, g_sb, ev_cst)
            ev = S.op("v", lambda e: e.tensor_reduce(
                s3_sb[:], mg_sb[:], AX.X, ALU.max), after=[ev_cst])
            ev = S.op("v", lambda e: e.tensor_scalar(
                s3_sb[:], s3_sb[:], 1.0 / 32767.0, 1e-12, ALU.mult, ALU.max))
            ev = S.op("v", lambda e: e.tensor_reduce(
                s4_sb[:], m4_sb[:], AX.X, ALU.max))
            ev = S.op("v", lambda e: e.tensor_tensor(
                s4_sb[:], s4_sb[:], s3_sb[:], ALU.mult))
            ev = S.op("v", lambda e: e.tensor_scalar(
                s4_sb[:], s4_sb[:], 2.0**-16 / 32767.0, 1e-12,
                ALU.mult, ALU.max))
            ev = S.op("v", lambda e: e.reciprocal(rf_sb[:], s4_sb[:]))
            ev = S.op("v", lambda e: e.tensor_tensor(
                rf_sb[:], rf_sb[:], s3_sb[:], ALU.mult))
            ev_sc = S.op("v", lambda e: e.tensor_scalar(
                rf_sb[:], rf_sb[:], 2.0**-16, None, ALU.mult))

            ev_use = {0: None, 1: None}
            ev_st = {0: None, 1: None}
            ev_mx = ev_cstg2
            for hc in range(24):
                bi = hc % 2
                el = S.op("y", lambda e, hc=hc, bi=bi: e.dma_start(
                    out=t0b[bi][:], in_=bass.AP(t0_t, hc * 128 * W,
                                                [[W, 128], [1, W]])),
                    after=[ev_use[bi]], dma=True, ch=f"lt{bi}")
                elu = S.op("y", lambda e, hc=hc, bi=bi: e.dma_start(
                    out=uqb[bi][:], in_=bass.AP(uq_t, hc * 128 * W,
                                                [[W, 128], [1, W]])),
                    after=[ev_use[bi]], dma=True, ch=f"lu{bi}")
                e1 = S.op("v", lambda e, bi=bi: e.tensor_scalar(
                    tg[bi][:], t0b[bi][:], rf_sb[:], None, ALU.mult),
                    after=[el, ev_sc, ev_g])
                e2 = emit_round(S, tg[bi][:], tg[bi][:], g_sb, ta[bi][:],
                                tb[bi][:], after=[e1])
                e3 = S.op("v", lambda e, bi=bi: e.tensor_scalar(
                    tg[bi][:], tg[bi][:], s4_sb[:], None, ALU.mult))
                e4 = S.op("v", lambda e, bi=bi: e.tensor_tensor(
                    u0b[bi][:], tg[bi][:], uqb[bi][:], ALU.mult),
                    after=[elu, ev_st[bi]])
                ev_use[bi] = e4
                e5 = S.op("v", lambda e, bi=bi: e.tensor_reduce(
                    mtmp[:], u0b[bi][:, HALO:W - HALO], AX.X, ALU.max,
                    apply_absolute_value=True), after=[e4, ev_mx])
                ev_mx = S.op("v", lambda e: e.tensor_tensor(
                    mx5[:], mx5[:], mtmp[:], ALU.max))
                ev_st[bi] = S.op("g", lambda e, hc=hc, bi=bi: e.dma_start(
                    out=bass.AP(u0_out, hc * 128 * W, [[W, 128], [1, W]]),
                    in_=u0b[bi][:]), after=[e4], dma=True, ch=f"st{bi}")
            ev1 = S.op("v", lambda e: e.tensor_copy(
                bass.AP(mxt, 0, [[128, 128], [1, 1]]), mx5[:]),
                after=[ev_mx, ev_mz])
            ev2 = S.op("t", lambda e: e.transpose(
                ptp[:], mxt[:], id_sb[:]), after=[ev1, ev_id])
            ev3 = S.op("v", lambda e: e.tensor_reduce(
                msc[:], ptp[0:1, :], AX.X, ALU.max,
                apply_absolute_value=True), after=[ev2])
            S.op("g", lambda e: e.dma_start(
                out=m5_out.ap(), in_=msc[:]), after=[ev3], dma=True, ch="mo")
            S.build()
    return nc



def build_G():
    nc = bass.Bass("TRN2", target_bir_lowering=False)
    u0_t = nc.dram_tensor("u0_t", [IH, W], BF16, kind="ExternalInput")
    x1t = nc.dram_tensor("x1t", [D, W], F32, kind="ExternalInput")
    m5r = nc.dram_tensor("m5r", [128, 8], F32, kind="ExternalInput")
    wdT = nc.dram_tensor("wdT", [IH, D], BF16, kind="ExternalInput")
    wqkv = nc.dram_tensor("wqkv", [D, 3 * D], F32, kind="ExternalInput")
    wo2 = nc.dram_tensor("wo2", [D, D], F32, kind="ExternalInput")
    bo2 = nc.dram_tensor("bo2", [128, 8], F32, kind="ExternalInput")
    selh = nc.dram_tensor("selh", [128, 128], F32, kind="ExternalInput")
    bselh = nc.dram_tensor("bselh", [16, 1024], F32, kind="ExternalInput")
    maskA = nc.dram_tensor("maskA", [16, W], BF16, kind="ExternalInput")
    maskB = nc.dram_tensor("maskB", [16, W], BF16, kind="ExternalInput")
    edgea = nc.dram_tensor("edgea", [128, 2], F32, kind="ExternalInput")
    one128 = nc.dram_tensor("one128", [128, 1], F32, kind="ExternalInput")
    id_h = nc.dram_tensor("id_h", [128, 128], F32, kind="ExternalInput")
    x3_out = nc.dram_tensor("x3_out", [D, TPC], F32, kind="ExternalOutput")
    xln5_out = nc.dram_tensor("xln5_out", [D, TPC], BF16,
                              kind="ExternalOutput")
    mi_out = nc.dram_tensor("mi_out", [1, 1], F32, kind="ExternalOutput")
    x2r_h = nc.dram_tensor("x2r_h", [D, W], F32)
    xln4_h = nc.dram_tensor("xln4_h", [D, W], BF16)

    es = contextlib.ExitStack()
    with es:
        S = Sched(nc)
        wpool = alloc(nc, es, "wpool", [128, 24 * 8 * 128], BF16)
        wo_sb = alloc(nc, es, "wo2s", [128, 8 * 8 * 128], BF16)
        ones_sb = alloc(nc, es, "ones", [128, 128], BF16)
        id_sb = alloc(nc, es, "idm", [128, 128], BF16)
        sel_sb = alloc(nc, es, "sel", [128, 128], BF16)
        bsel_sb = alloc(nc, es, "bsel", [16, 1024], BF16)
        mA_sb = alloc(nc, es, "mA", [16, 512], BF16)
        mB_sb = alloc(nc, es, "mB", [16, 512], BF16)
        bo_sb = alloc(nc, es, "bo2s", [128, 8], F32)
        bo2x = alloc(nc, es, "bo2x", [128, 8], F32)
        bL_sb = alloc(nc, es, "bL", [128, 8], F32)
        bR_sb = alloc(nc, es, "bR", [128, 8], F32)
        edg_sb = alloc(nc, es, "edga", [128, 2], F32)
        one_sb = alloc(nc, es, "one", [128, 1], F32)
        g_sb = alloc(nc, es, "gflag", [128, 1], F32)
        m5_sb = alloc(nc, es, "m5s", [128, 8], F32)
        s5_sb = alloc(nc, es, "s5s", [128, 1], F32)
        r5_sb = alloc(nc, es, "r5s", [128, 1], F32)
        u0pool = alloc(nc, es, "u0pool", [128, 24 * 512], BF16)
        tqpool = alloc(nc, es, "tqpool", [128, 3584], F32)
        ta = alloc(nc, es, "ta", [128, 1536], F32)
        tb = alloc(nc, es, "tb", [128, 1536], F32)
        xf = alloc(nc, es, "xf", [128, 8 * 512], F32)
        yb = alloc(nc, es, "yb", [128, 8 * 512], F32)
        ybf = alloc(nc, es, "ybf", [128, 8 * 512], BF16)
        ysq = alloc(nc, es, "ysq", [128, 8 * 512], BF16)
        mu_ = alloc(nc, es, "mu_", [128, 512], F32)
        rb_ = alloc(nc, es, "rb_", [128, 512], F32)
        ts1 = alloc(nc, es, "ts1", [128, 512], F32)
        ts2 = alloc(nc, es, "ts2", [128, 512], F32)
        mxc = alloc(nc, es, "mxi", [128, 1], F32)
        mtmp = alloc(nc, es, "mtmp", [128, 1], F32)
        mxt = alloc(nc, es, "mxt", [128, 128], BF16)
        msc = alloc(nc, es, "msc", [1, 1], F32)
        qkv_sb = alloc(nc, es, "qkv", [128, 24 * 518], BF16)
        cf_ = alloc(nc, es, "cf_", [16, 7 * 512], BF16)
        tmpv = ts2
        u0b = [u0pool]
        oq = u0pool
        xl4 = ybf
        acc = yb
        accb = u0pool[:, 0:4096]
        xl4b = u0pool[:, 4096:4096 + 8 * 518]
        prod = ysq
        with (nc.psum_tensor("pA", [128, 512], F32) as pA,
              nc.psum_tensor("pB", [128, 512], F32) as pB,
              nc.psum_tensor("pS1", [128, 512], F32) as pS1,
              nc.psum_tensor("pS2", [128, 512], F32) as pS2,
              nc.psum_tensor("pD0", [128, 512], F32) as pD0,
              nc.psum_tensor("pDT", [128, 128], BF16) as pDT,
              nc.psum_tensor("pC0", [128, 512], F32) as pC0,
              nc.psum_tensor("pC1", [128, 512], F32) as pC1):

            ev_wd = S.op("g", lambda e: e.dma_start(
                out=ap3(wpool, 24 * 1024, 128, (1024, 24), (1, 1024)),
                in_=bass.AP(wdT, 0, [[D, 128], [128 * D, 24], [1, 1024]])),
                dma=True, ch="wp")
            ev_wo = S.op("g", lambda e: e.dma_start(
                out=ap3(wo_sb, 8192, 128, (1024, 8), (1, 1024)),
                in_=bass.AP(wo2, 0, [[D, 128], [128 * D, 8], [1, 1024]])),
                dma=True, ch="cstg")
            ev_id = S.op("g", lambda e: e.dma_start(
                out=id_sb[:], in_=id_h.ap()), dma=True, ch="cstg")
            ev_sel = S.op("g", lambda e: e.dma_start(
                out=sel_sb[:], in_=selh.ap()), dma=True, ch="cstg")
            ev_bsel = S.op("g", lambda e: e.dma_start(
                out=bsel_sb[:], in_=bselh.ap()), dma=True, ch="cstg")
            ev_bo = S.op("y", lambda e: e.dma_start(
                out=bo_sb[:], in_=bo2.ap()), dma=True, ch="cst")
            ev_ea = S.op("y", lambda e: e.dma_start(
                out=edg_sb[:], in_=edgea.ap()), dma=True, ch="cst")
            ev_m5 = S.op("y", lambda e: e.dma_start(
                out=m5_sb[:], in_=m5r.ap()), dma=True, ch="cst")
            ev_1 = S.op("y", lambda e: e.dma_start(
                out=one_sb[:], in_=one128.ap()), dma=True, ch="cst")
            S.op("g", lambda e: e.memset(ones_sb[:], 1.0), ch="cstg2")
            S.op("g", lambda e: e.memset(mxc[:], 0.0), ch="cstg2")
            ev_mz = S.op("g", lambda e: e.memset(mxt[:], 0.0), ch="cstg2")
            ev_cst = ("cst", S.n["cst"])
            ev_cstg = ("cstg", S.n["cstg"])
            ev_cstg2 = ("cstg2", S.n["cstg2"])
            ev_wo = ev_id = ev_sel = ev_bsel = ev_cstg
            ev_g = emit_flag_g(S, one_sb, g_sb, ev_cst)
            ev = S.op("v", lambda e: e.tensor_reduce(
                s5_sb[:], m5_sb[:], AX.X, ALU.max), after=[ev_cst])
            ev = S.op("v", lambda e: e.tensor_scalar(
                s5_sb[:], s5_sb[:], 1.0 / 32767.0, 1e-12, ALU.mult, ALU.max))
            ev_s5 = S.op("v", lambda e: e.reciprocal(r5_sb[:], s5_sb[:]))
            ev_b2 = S.op("v", lambda e: e.tensor_scalar(
                bo2x[:], bo_sb[:], 2.0, None, ALU.mult), after=[ev_cst])
            ev_bL = S.op("v", lambda e: e.tensor_scalar(
                bL_sb[:], bo_sb[:], edg_sb[:, 0:1], None, ALU.mult))
            ev_bR = S.op("v", lambda e: e.tensor_scalar(
                bR_sb[:], bo_sb[:], edg_sb[:, 1:2], None, ALU.mult))

            def stat_chain(evm, cw):
                e1 = S.op("v", lambda e: e.tensor_scalar(
                    mu_[:, 0:cw], pS1[:, 0:cw], 1.0 / D, None, ALU.mult),
                    after=[evm])
                e2 = S.op("v", lambda e: e.tensor_scalar(
                    ts1[:, 0:cw], pS2[:, 0:cw], 1.0 / D, 1e-5,
                    ALU.mult, ALU.add))
                e3 = S.op("v", lambda e: e.tensor_tensor(
                    ts2[:, 0:cw], mu_[:, 0:cw], mu_[:, 0:cw], ALU.mult))
                e4 = S.op("v", lambda e: e.tensor_tensor(
                    ts1[:, 0:cw], ts1[:, 0:cw], ts2[:, 0:cw], ALU.subtract))
                e5 = S.op("s", lambda e: e.activation(
                    ts2[:, 0:cw], ts1[:, 0:cw], AF.Sqrt), after=[e4])
                e6 = S.op("v", lambda e: e.reciprocal(
                    rb_[:, 0:cw], ts2[:, 0:cw]), after=[e5])
                return e2, e6

            def ln_stats(evc_src, ybf_t, ysq_t, cw, aft_sq, aft_mm):
                evq = S.op("s", lambda e: e.activation(
                    ysq_t[:, 0:8 * cw], ybf_t[:, 0:8 * cw], AF.Square),
                    after=aft_sq)
                evm = None
                for k in range(8):
                    aft = aft_mm if k == 0 else []
                    evm = S.op("t", lambda e, k=k: e.matmul(
                        pS1[:, 0:cw], ones_sb[:],
                        ybf_t[:, k * cw:(k + 1) * cw],
                        start=(k == 0), stop=(k == 7)), after=aft)
                for k in range(8):
                    aft = [evq] if k == 0 else []
                    evm = S.op("t", lambda e, k=k: e.matmul(
                        pS2[:, 0:cw], ones_sb[:],
                        ysq_t[:, k * cw:(k + 1) * cw],
                        start=(k == 0), stop=(k == 7)), after=aft)
                return evm

            ev_ssq = None
            ev_srd = None
            ev_evac = {0: None, 1: None}
            ev_oq = None
            ev_x2st = None
            ev_x4st = None
            ev_x4rd = None
            ev_dn_last = None
            ev_prev_ld = None
            col = 0
            for ti, cw in enumerate(TILES):
                c0 = col
                col += cw
                el = S.op("y", lambda e, c0=c0, cw=cw: e.dma_start(
                    out=ap3(u0pool, 24 * 512, 128, (cw, 24), (1, cw)),
                    in_=bass.AP(u0_t, c0, [[W, 128], [128 * W, 24], [1, cw]])),
                    after=[ev_oq, ev_dn_last], dma=True, ch="lu")
                elx = S.op("y", lambda e, c0=c0, cw=cw: e.dma_start(
                    out=ap3(xf, 8 * 512, 128, (cw, 8), (1, cw)),
                    in_=bass.AP(x1t, c0, [[W, 128], [128 * W, 8], [1, cw]])),
                    after=[ev_prev_ld], dma=True, ch="lx")
                nseg = (24 * cw + 2047) // 2048
                ev_oq = None
                for sgi in range(nseg):
                    s0 = sgi * 2048
                    s1 = min(24 * cw, s0 + 2048)
                    n = s1 - s0
                    e1 = S.op("v", lambda e, s0=s0, n=n: e.tensor_scalar(
                        tqpool[:, 0:n], u0pool[:, s0:s0 + n], r5_sb[:], None,
                        ALU.mult), after=[el, ev_s5, ev_g, ev_oq])
                    h = (n + 1) // 2
                    for qq in range(2):
                        q0, q1 = qq * h, min(n, (qq + 1) * h)
                        if q1 <= q0:
                            continue
                        m = q1 - q0
                        eva = S.op("v", lambda e, q0=q0, m=m: e.tensor_scalar(
                            ta[:, 0:m], tqpool[:, q0:q0 + m], C23, C23,
                            ALU.add, ALU.subtract))
                        evb = S.op("v", lambda e, q0=q0, m=m: e.tensor_scalar(
                            tb[:, 0:m], tqpool[:, q0:q0 + m], C52, C52,
                            ALU.add, ALU.subtract))
                        evc_ = S.op("v", lambda e, q0=q0, m=m:
                                    e.scalar_tensor_tensor(
                            tqpool[:, q0:q0 + m], ta[:, 0:m], g_sb[:],
                            tb[:, 0:m], ALU.mult, ALU.add))
                    ev_oq = S.op("v", lambda e, s0=s0, n=n: e.tensor_scalar(
                        u0pool[:, s0:s0 + n], tqpool[:, 0:n], s5_sb[:], None,
                        ALU.mult))
                ev_ev = []
                for m in range(8):
                    pbank = pA if m % 2 == 0 else pB
                    evk = None
                    for k in range(24):
                        aft = ([ev_oq, ev_wd, ev_evac[m % 2]] if k == 0
                               else [])
                        evk = S.op("t", lambda e, k=k, m=m, cw=cw,
                                   pbank=pbank: e.matmul(
                            pbank[:, 0:cw],
                            wpool[:, (k * 8 + m) * 128:(k * 8 + m + 1) * 128],
                            u0pool[:, k * cw:(k + 1) * cw],
                            start=(k == 0), stop=(k == 23)), after=aft)
                    ev_evac[m % 2] = S.op("s", lambda e, m=m, cw=cw,
                                          pbank=pbank: e.copy(
                        yb[:, m * cw:(m + 1) * cw], pbank[:, 0:cw]),
                        after=[evk])
                    ev_ev.append(ev_evac[m % 2])
                    ev_dn_last = evk
                evc = S.op("v", lambda e, cw=cw: e.tensor_copy(
                    ybf[:, 0:8 * cw], yb[:, 0:8 * cw]),
                    after=ev_ev + [ev_x4st])
                evm = ln_stats(evc, ybf, ysq, cw, [evc, ev_ssq],
                               [evc, ev_cstg2, ev_srd])
                ev_ssq = evm
                ev_srd, e6 = stat_chain(evm, cw)
                e7 = S.op("v", lambda e, cw=cw: e.tensor_tensor(
                    yb[:, 0:8 * cw], yb[:, 0:8 * cw],
                    bass.AP(mu_, 0, [[512, 128], [0, 8], [1, cw]]),
                    ALU.subtract), after=[e6])
                e8 = S.op("v", lambda e, cw=cw: e.tensor_tensor(
                    yb[:, 0:8 * cw], yb[:, 0:8 * cw],
                    bass.AP(rb_, 0, [[512, 128], [0, 8], [1, cw]]), ALU.mult))
                e9 = S.op("v", lambda e, cw=cw: e.tensor_tensor(
                    xf[:, 0:8 * cw], xf[:, 0:8 * cw], yb[:, 0:8 * cw],
                    ALU.add), after=[e8, elx])
                ev_x2st = S.op("g", lambda e, c0=c0, cw=cw: e.dma_start(
                    out=bass.AP(x2r_h, c0, [[W, 128], [128 * W, 8], [1, cw]]),
                    in_=ap3(xf, 8 * 512, 128, (cw, 8), (1, cw))),
                    after=[e9], dma=True, ch="sx2")
                evc2 = S.op("v", lambda e, cw=cw: e.tensor_copy(
                    ybf[:, 0:8 * cw], xf[:, 0:8 * cw]), after=[e9])
                evm2 = ln_stats(evc2, ybf, ysq, cw, [evc2, ev_ssq],
                                [evc2, ev_srd])
                ev_ssq = evm2
                ev_srd, e6b = stat_chain(evm2, cw)
                ea = S.op("v", lambda e, cw=cw: e.tensor_tensor(
                    xf[:, 0:8 * cw], xf[:, 0:8 * cw],
                    bass.AP(mu_, 0, [[512, 128], [0, 8], [1, cw]]),
                    ALU.subtract), after=[e6b, ev_x2st])
                eb = S.op("v", lambda e, cw=cw: e.tensor_tensor(
                    xl4[:, 0:8 * cw], xf[:, 0:8 * cw],
                    bass.AP(rb_, 0, [[512, 128], [0, 8], [1, cw]]), ALU.mult),
                    after=[ev_x4st])
                ev_prev_ld = eb
                ev_x4st = S.op("g", lambda e, c0=c0, cw=cw: e.dma_start(
                    out=bass.AP(xln4_h, c0, [[W, 128], [128 * W, 8],
                                             [1, cw]]),
                    in_=ap3(xl4, 8 * 512, 128, (cw, 8), (1, cw))),
                    after=[eb], dma=True, ch="sx4")
            ev_phase1 = [ev_x2st, ev_x4st, ev_oq, ev_prev_ld, ev_ssq, ev_srd]

            ev_wq = S.op("g", lambda e: e.dma_start(
                out=ap3(wpool, 24 * 1024, 128, (3072, 8), (1, 3072)),
                in_=bass.AP(wqkv, 0, [[3 * D, 128], [128 * 3 * D, 8],
                                      [1, 3072]])),
                after=ev_phase1 + [ev_evac[0], ev_evac[1]], dma=True,
                ch="wp")

            PIT = 3584
            PITA = 1536

            def dotsv(off, dims):
                return bass.AP(tqpool, off, [[PIT, 16]] + dims)

            ev_qkv_rd = None
            ev_accb_rd = None
            ev_x3st = None
            ev_x5st = None
            ev_mx = ev_cstg2
            ev_dots_rd = None
            ev_cf_rd = None
            ev_prod_rd = None
            ev_dlast = None
            ev_xf_rd = None
            ev_att_evac = {0: None, 1: None}
            ev_mload = {0: None, 1: None}
            for ti in range(8):
                c0 = HALO + ti * 512 - 3
                elq = S.op("y", lambda e, c0=c0: e.dma_start(
                    out=ap3(xl4b, 24 * 512, 128, (518, 8), (1, 518), off=4096),
                    in_=bass.AP(xln4_h, c0, [[W, 128], [128 * W, 8],
                                             [1, 518]])),
                    after=[ev_qkv_rd, ev_x4st, ev_oq], dma=True, ch="lq")
                evmA = S.op("y", lambda e, ti=ti: e.dma_start(
                    out=mA_sb[:], in_=bass.AP(maskA, HALO + ti * 512,
                                              [[W, 16], [1, 512]])),
                    after=[ev_mload[0]], dma=True, ch="lmA")
                evmB = S.op("y", lambda e, ti=ti: e.dma_start(
                    out=mB_sb[:], in_=bass.AP(maskB, HALO + ti * 512,
                                              [[W, 16], [1, 512]])),
                    after=[ev_mload[1]], dma=True, ch="lmB")
                ev_qmm = []
                for m in range(24):
                    pbank = pA if m % 2 == 0 else pB
                    evk = None
                    for k in range(8):
                        aft = ([elq, ev_wq, ev_att_evac[m % 2]] if k == 0
                               else [])
                        evk = S.op("t", lambda e, k=k, m=m, pbank=pbank:
                                   e.matmul(
                            pbank[:, 0:512],
                            wpool[:, (k * 24 + m) * 128:
                                     (k * 24 + m + 1) * 128],
                            bass.AP(u0pool, 4096 + k * 518,
                                    [[24 * 512, 128], [1, 512]]),
                            start=(k == 0), stop=(k == 7)), after=aft)
                    evk2 = None
                    for k in range(8):
                        aft = [evk] if k == 0 else []
                        evk2 = S.op("t", lambda e, k=k, m=m: e.matmul(
                            (pC0 if m % 2 == 0 else pC1)[:, 0:6],
                            wpool[:, (k * 24 + m) * 128:
                                     (k * 24 + m + 1) * 128],
                            bass.AP(u0pool, 4096 + k * 518 + 512,
                                    [[24 * 512, 128], [1, 6]]),
                            start=(k == 0), stop=(k == 7)), after=aft)
                    e_a = S.op("s", lambda e, m=m, pbank=pbank: e.copy(
                        qkv_sb[:, m * 518:m * 518 + 512], pbank[:, 0:512]),
                        after=[evk, ev_qkv_rd])
                    e_b = S.op("s", lambda e, m=m: e.copy(
                        qkv_sb[:, m * 518 + 512:(m + 1) * 518],
                        (pC0 if m % 2 == 0 else pC1)[:, 0:6]),
                        after=[evk2])
                    ev_att_evac[m % 2] = e_b
                    ev_qmm.append(e_b)
                ev_d7 = []
                for dlt in range(-3, 4):
                    di = dlt + 3
                    dbank = pD0
                    aftp = ev_qmm + ([ev_prod_rd] if ev_prod_rd else [])
                    ep = S.op("v", lambda e, dlt=dlt: e.tensor_tensor(
                        prod[:, 0:4096],
                        bass.AP(qkv_sb, 3, [[24 * 518, 128], [518, 8],
                                            [1, 512]]),
                        bass.AP(qkv_sb, 8 * 518 + 3 + dlt,
                                [[24 * 518, 128], [518, 8], [1, 512]]),
                        ALU.mult), after=aftp)
                    evr = None
                    for ci in range(8):
                        aft = ([ep, ev_sel, ev_dots_rd, ev_dlast]
                               if ci == 0 else [])
                        evr = S.op("t", lambda e, ci=ci, dbank=dbank:
                                   e.matmul(
                            dbank[0:16, 0:512],
                            sel_sb[:, 16 * ci:16 * (ci + 1)],
                            prod[:, ci * 512:(ci + 1) * 512],
                            start=(ci == 0), stop=(ci == 7)), after=aft)
                    ev_prod_rd = evr
                    edv = S.op("s", lambda e, di=di, dbank=dbank:
                               e.activation(
                        dotsv(di * 512, [[1, 512]]), dbank[0:16, 0:512],
                        AF.Identity, scale=0.03125), after=[evr, ev_cf_rd])
                    ev_dlast = edv
                    ev_d7.append(edv)
                ev_dots_rd = ev_d7[-1]
                ez = S.op("v", lambda e: e.memset(cf_[:], 0.0),
                          after=ev_d7 + ([ev_cf_rd] if ev_cf_rd else []))
                ev_cfw = ez
                for par in range(2):
                    for slot in range(2):
                        start = (3 - par) if slot == 0 else (1 - par)
                        msk = mA_sb if slot == 0 else mB_sb
                        evmsk = evmA if slot == 0 else evmB
                        doff = start * 512 + par
                        e1 = S.op("v", lambda e, doff=doff: e.tensor_reduce(
                            bass.AP(ta, 1024, [[PITA, 16], [1, 256]]),
                            dotsv(doff, [[2, 256], [512, 4]]),
                            AX.X, ALU.max), after=[ev_cfw, ev_dots_rd])
                        e2 = S.op("v", lambda e, doff=doff: e.tensor_tensor(
                            bass.AP(ta, 0, [[PITA, 16], [1, 1024]]),
                            dotsv(doff, [[2, 256], [512, 4]]),
                            bass.AP(ta, 1024, [[PITA, 16], [1, 256], [0, 4]]),
                            ALU.subtract))
                        e3 = S.op("s", lambda e: e.activation(
                            bass.AP(ta, 0, [[PITA, 16], [1, 1024]]),
                            bass.AP(ta, 0, [[PITA, 16], [1, 1024]]),
                            AF.Exp), after=[e2])
                        e4 = S.op("v", lambda e: e.tensor_reduce(
                            bass.AP(tb, 0, [[PITA, 16], [1, 256]]),
                            bass.AP(ta, 0, [[PITA, 16], [4, 256], [1, 4]]),
                            AX.X, ALU.add), after=[e3])
                        e5 = S.op("v", lambda e: e.reciprocal(
                            bass.AP(tb, 256, [[PITA, 16], [1, 256]]),
                            bass.AP(tb, 0, [[PITA, 16], [1, 256]])))
                        e6 = S.op("v", lambda e, msk=msk, par=par:
                                  e.tensor_tensor(
                            bass.AP(tb, 256, [[PITA, 16], [1, 256]]),
                            bass.AP(tb, 256, [[PITA, 16], [1, 256]]),
                            bass.AP(msk, par, [[512, 16], [2, 256]]),
                            ALU.mult), after=[e5, evmsk])
                        e7 = S.op("v", lambda e: e.tensor_tensor(
                            bass.AP(ta, 0, [[PITA, 16], [4, 256], [1, 4]]),
                            bass.AP(ta, 0, [[PITA, 16], [4, 256], [1, 4]]),
                            bass.AP(tb, 256, [[PITA, 16], [1, 256], [0, 4]]),
                            ALU.mult), after=[e6])
                        ev_cfw = S.op("v", lambda e, doff=doff:
                                      e.tensor_tensor(
                            bass.AP(cf_, doff, [[7 * 512, 16], [2, 256],
                                                [512, 4]]),
                            bass.AP(cf_, doff, [[7 * 512, 16], [2, 256],
                                                [512, 4]]),
                            bass.AP(ta, 0, [[PITA, 16], [4, 256], [1, 4]]),
                            ALU.add), after=[e7])
                ev_cf_rd = ev_cfw
                ezz = S.op("v", lambda e: e.memset(acc[:], 0.0),
                           after=[ev_accb_rd, ev_cfw])
                ev_ap = ezz
                for di in range(7):
                    for ci in range(8):
                        cbank = pC0 if (di * 8 + ci) % 2 == 0 else pC1
                        eb1 = S.op("t", lambda e, di=di, ci=ci, cbank=cbank:
                                   e.matmul(
                            cbank[:, 0:512],
                            bsel_sb[:, 128 * ci:128 * (ci + 1)],
                            cf_[:, di * 512:(di + 1) * 512],
                            start=True, stop=True),
                            after=[ev_cfw, ev_bsel, ev_ap])
                        eb2 = S.op("v", lambda e, di=di, ci=ci, cbank=cbank:
                                   e.tensor_tensor(
                            tmpv[:], cbank[:, 0:512],
                            bass.AP(qkv_sb, 16 * 518 + ci * 518 + di,
                                    [[24 * 518, 128], [1, 512]]),
                            ALU.mult), after=[eb1])
                        ev_ap = S.op("v", lambda e, ci=ci: e.tensor_tensor(
                            acc[:, ci * 512:(ci + 1) * 512],
                            acc[:, ci * 512:(ci + 1) * 512], tmpv[:],
                            ALU.add))
                ev_qkv_rd = ev_ap
                ecb = S.op("v", lambda e: e.tensor_copy(accb[:], acc[:]),
                           after=[ev_ap, ev_oq])
                ev_oq = ecb
                ev_accb_rd = None
                ev_ev2 = []
                for m in range(8):
                    pbank = pA if m % 2 == 0 else pB
                    evk = None
                    for k in range(8):
                        aft = ([ecb, ev_wo, ev_att_evac[m % 2]] if k == 0
                               else [])
                        evk = S.op("t", lambda e, k=k, m=m, pbank=pbank:
                                   e.matmul(
                            pbank[:, 0:512],
                            wo_sb[:, (k * 8 + m) * 128:(k * 8 + m + 1) * 128],
                            accb[:, k * 512:(k + 1) * 512],
                            start=(k == 0), stop=(k == 7)), after=aft)
                    ev_att_evac[m % 2] = S.op("s", lambda e, m=m,
                                              pbank=pbank: e.activation(
                        yb[:, m * 512:(m + 1) * 512], pbank[:, 0:512],
                        AF.Identity, bias=bo2x[:, m:m + 1]),
                        after=[evk, ev_b2, ev_qkv_rd])
                    ev_ev2.append(ev_att_evac[m % 2])
                ev_accb_rd = evk
                ev_fix = list(ev_ev2)
                if ti == 0:
                    f1 = S.op("v", lambda e: e.tensor_tensor(
                        bass.AP(yb, 0, [[4096, 128], [512, 8], [1, 1]]),
                        bass.AP(yb, 0, [[4096, 128], [512, 8], [1, 1]]),
                        bass.AP(bL_sb, 0, [[8, 128], [1, 8], [0, 1]]),
                        ALU.subtract), after=ev_ev2 + [ev_bL])
                    f2 = S.op("v", lambda e: e.tensor_tensor(
                        bass.AP(yb, 1, [[4096, 128], [512, 8], [1, 1]]),
                        bass.AP(yb, 1, [[4096, 128], [512, 8], [1, 1]]),
                        bass.AP(bL_sb, 0, [[8, 128], [1, 8], [0, 1]]),
                        ALU.subtract))
                    ev_fix = [f2]
                if ti == 7:
                    f1 = S.op("v", lambda e: e.tensor_tensor(
                        bass.AP(yb, 510, [[4096, 128], [512, 8], [1, 1]]),
                        bass.AP(yb, 510, [[4096, 128], [512, 8], [1, 1]]),
                        bass.AP(bR_sb, 0, [[8, 128], [1, 8], [0, 1]]),
                        ALU.subtract), after=ev_ev2 + [ev_bR])
                    f2 = S.op("v", lambda e: e.tensor_tensor(
                        bass.AP(yb, 511, [[4096, 128], [512, 8], [1, 1]]),
                        bass.AP(yb, 511, [[4096, 128], [512, 8], [1, 1]]),
                        bass.AP(bR_sb, 0, [[8, 128], [1, 8], [0, 1]]),
                        ALU.subtract))
                    ev_fix = [f2]
                elx2 = S.op("y", lambda e, ti=ti: e.dma_start(
                    out=ap3(xf, 8 * 512, 128, (512, 8), (1, 512)),
                    in_=bass.AP(x2r_h, HALO + ti * 512,
                                [[W, 128], [128 * W, 8], [1, 512]])),
                    after=[ev_x3st, ev_x2st, ev_prev_ld, ev_xf_rd],
                    dma=True, ch="lx")
                evc = S.op("v", lambda e: e.tensor_copy(
                    ybf[:], yb[:]), after=ev_fix + [ev_x4st, ev_x5st])
                evm = ln_stats(evc, ybf, ysq, 512, [evc, ev_ssq],
                               [evc, ev_srd])
                ev_ssq = evm
                ev_srd, e6 = stat_chain(evm, 512)
                e7 = S.op("v", lambda e: e.tensor_tensor(
                    yb[:], yb[:], bass.AP(mu_, 0, [[512, 128], [0, 8],
                                                   [1, 512]]),
                    ALU.subtract), after=[e6])
                e8 = S.op("v", lambda e: e.tensor_tensor(
                    yb[:], yb[:], bass.AP(rb_, 0, [[512, 128], [0, 8],
                                                   [1, 512]]), ALU.mult))
                e9 = S.op("v", lambda e: e.tensor_tensor(
                    xf[:], xf[:], yb[:], ALU.add), after=[e8, elx2])
                ev_x3st = S.op("g", lambda e, ti=ti: e.dma_start(
                    out=bass.AP(x3_out, ti * 512, [[TPC, 128], [128 * TPC, 8],
                                                   [1, 512]]),
                    in_=ap3(xf, 8 * 512, 128, (512, 8), (1, 512))),
                    after=[e9], dma=True, ch="sx3")
                evc2 = S.op("v", lambda e: e.tensor_copy(ybf[:], xf[:]),
                            after=[e9])
                evm2 = ln_stats(evc2, ybf, ysq, 512, [evc2, ev_ssq],
                                [evc2, ev_srd])
                ev_ssq = evm2
                ev_srd, e6b = stat_chain(evm2, 512)
                ea2 = S.op("v", lambda e: e.tensor_tensor(
                    xf[:], xf[:], bass.AP(mu_, 0, [[512, 128], [0, 8],
                                                   [1, 512]]),
                    ALU.subtract), after=[e6b, ev_x3st])
                eb2 = S.op("v", lambda e: e.tensor_tensor(
                    xl4[:], xf[:], bass.AP(rb_, 0, [[512, 128], [0, 8],
                                                    [1, 512]]), ALU.mult),
                    after=[ev_x5st])
                ev_xf_rd = eb2
                em1 = S.op("v", lambda e: e.tensor_reduce(
                    mtmp[:], bass.AP(xl4, 0, [[8 * 512, 128], [1, 8 * 512]]),
                    AX.X, ALU.max, apply_absolute_value=True),
                    after=[eb2, ev_mx])
                ev_mx = S.op("v", lambda e: e.tensor_tensor(
                    mxc[:], mxc[:], mtmp[:], ALU.max))
                ev_x5st = S.op("g", lambda e, ti=ti: e.dma_start(
                    out=bass.AP(xln5_out, ti * 512,
                                [[TPC, 128], [128 * TPC, 8], [1, 512]]),
                    in_=ap3(xl4, 8 * 512, 128, (512, 8), (1, 512))),
                    after=[eb2], dma=True, ch="sx5")
            ev1 = S.op("v", lambda e: e.tensor_copy(
                bass.AP(mxt, 0, [[128, 128], [1, 1]]), mxc[:]),
                after=[ev_mx, ev_mz])
            ev2 = S.op("t", lambda e: e.transpose(
                pDT[:], mxt[:], id_sb[:]), after=[ev1, ev_id, ev_dots_rd,
                                                  ev_prod_rd])
            ev3 = S.op("v", lambda e: e.tensor_reduce(
                msc[:], pDT[0:1, :], AX.X, ALU.max,
                apply_absolute_value=True), after=[ev2])
            S.op("g", lambda e: e.dma_start(
                out=mi_out.ap(), in_=msc[:]), after=[ev3], dma=True, ch="mo")
            S.build()
    return nc



def build_H():
    nc = bass.Bass("TRN2", target_bir_lowering=False)
    xlt = nc.dram_tensor("xlt", [D, TPC], BF16, kind="ExternalInput")
    x3t = nc.dram_tensor("x3t", [D, TPC], F32, kind="ExternalInput")
    mir = nc.dram_tensor("mir", [128, 8], F32, kind="ExternalInput")
    l1w = nc.dram_tensor("l1w", [D, D], F32, kind="ExternalInput")
    l1b = nc.dram_tensor("l1b", [128, 8], F32, kind="ExternalInput")
    one128 = nc.dram_tensor("one128", [128, 1], F32, kind="ExternalInput")
    out_t = nc.dram_tensor("out_t", [D, TPC], F32, kind="ExternalOutput")
    scr = nc.dram_tensor("scr", [128, 1], F32)
    scr2 = nc.dram_tensor("scr2", [1, 1], F32)

    es = contextlib.ExitStack()
    with es:
        S = Sched(nc)
        wf = alloc(nc, es, "wf", [128, 8 * 8 * 128], F32)
        wt = alloc(nc, es, "wt", [128, 8 * 8 * 128], BF16)
        ones_sb = alloc(nc, es, "ones", [128, 128], BF16)
        one_sb = alloc(nc, es, "one", [128, 1], F32)
        g_sb = alloc(nc, es, "gflag", [128, 1], F32)
        mi_sb = alloc(nc, es, "mis", [128, 8], F32)
        si_sb = alloc(nc, es, "sis", [128, 1], F32)
        ri_sb = alloc(nc, es, "ris", [128, 1], F32)
        bias_sb = alloc(nc, es, "bias", [128, 8], F32)
        wsum = alloc(nc, es, "wsum", [128, 1], F32)
        wrow = alloc(nc, es, "wrow", [1, 128], F32)
        wsc1 = alloc(nc, es, "wsc1", [1, 1], F32)
        wscb = alloc(nc, es, "wscb", [128, 1], F32)
        thr = alloc(nc, es, "thr", [128, 1], F32)
        sc_ = alloc(nc, es, "sc_", [128, 1], F32)
        absw = alloc(nc, es, "absw", [128, 2048], F32)
        sgnw = alloc(nc, es, "sgnw", [128, 2048], F32)
        xlb = [alloc(nc, es, f"xlb{i}", [128, 8 * 512], BF16) for i in range(2)]
        xf = [alloc(nc, es, f"xf{i}", [128, 8 * 512], F32) for i in range(2)]
        tq = alloc(nc, es, "tq", [128, 8 * 512], F32)
        ta = alloc(nc, es, "ta", [128, 2048], F32)
        tb = alloc(nc, es, "tb", [128, 2048], F32)
        xq = alloc(nc, es, "xq", [128, 8 * 512], BF16)
        yb = alloc(nc, es, "yb", [128, 8 * 512], F32)
        ybf = alloc(nc, es, "ybf", [128, 8 * 512], BF16)
        ysq = alloc(nc, es, "ysq", [128, 8 * 512], BF16)
        mu_ = alloc(nc, es, "mu_", [128, 512], F32)
        rb_ = alloc(nc, es, "rb_", [128, 512], F32)
        ts1 = alloc(nc, es, "ts1", [128, 512], F32)
        ts2 = alloc(nc, es, "ts2", [128, 512], F32)

        with (nc.psum_tensor("pA", [128, 512], F32) as pA,
              nc.psum_tensor("pB", [128, 512], F32) as pB,
              nc.psum_tensor("pS1", [128, 512], F32) as pS1,
              nc.psum_tensor("pS2", [128, 512], F32) as pS2):
            ev_w = S.op("g", lambda e: e.dma_start(
                out=ap3(wf, 8192, 128, (1024, 8), (1, 1024)),
                in_=bass.AP(l1w, 0, [[D, 128], [128 * D, 8], [1, 1024]])),
                dma=True, ch="cstg")
            ev_on = S.op("g", lambda e: e.memset(ones_sb[:], 1.0),
                         ch="cstg2")
            ev_mi = S.op("y", lambda e: e.dma_start(
                out=mi_sb[:], in_=mir.ap()), dma=True, ch="cst")
            ev_b = S.op("y", lambda e: e.dma_start(
                out=bias_sb[:], in_=l1b.ap()), dma=True, ch="cst")
            ev_1 = S.op("y", lambda e: e.dma_start(
                out=one_sb[:], in_=one128.ap()), dma=True, ch="cst")
            ev_cst = ("cst", S.n["cst"])
            ev_cstg = ("cstg", S.n["cstg"])
            ev_cstg2 = ("cstg2", S.n["cstg2"])
            ev_g = emit_flag_g(S, one_sb, g_sb, ev_cst)
            ev = S.op("v", lambda e: e.tensor_reduce(
                wsum[:], wf[:], AX.X, ALU.add, apply_absolute_value=True),
                after=[ev_w])
            ev = S.op("g", lambda e: e.dma_start(
                out=scr.ap(), in_=wsum[:]), after=[ev], dma=True, ch="ws")
            ev = S.op("y", lambda e: e.dma_start(
                out=wrow[:], in_=bass.AP(scr, 0, [[128, 1], [1, 128]])),
                after=[ev], dma=True, ch="ws2")
            ev = S.op("v", lambda e: e.tensor_reduce(
                wsc1[:], wrow[:], AX.X, ALU.add), after=[ev])
            ev = S.op("v", lambda e: e.tensor_scalar(
                wsc1[:], wsc1[:], 1.0 / (1024.0 * 1024.0), None, ALU.mult))
            ev = S.op("g", lambda e: e.dma_start(
                out=scr2.ap(), in_=wsc1[:]), after=[ev], dma=True, ch="ws")
            ev_wsc = S.op("y", lambda e: e.dma_start(
                out=wscb[:], in_=bass.AP(scr2, 0, [[0, 128], [1, 1]])),
                after=[ev], dma=True, ch="ws2")
            ev_thr = S.op("v", lambda e: e.tensor_scalar(
                thr[:], wscb[:], 0.5, None, ALU.mult), after=[ev_wsc])
            ev_tq = None
            for hh in range(4):
                sl = slice(hh * 2048, (hh + 1) * 2048)
                e1 = S.op("s", lambda e, sl=sl: e.activation(
                    absw[:], wf[:, sl], AF.Abs), after=[ev_w, ev_tq])
                e2 = S.op("s", lambda e, sl=sl: e.activation(
                    sgnw[:], wf[:, sl], AF.Sign), after=[e1])
                e3 = S.op("v", lambda e: e.tensor_scalar(
                    absw[:], absw[:], thr[:], None, ALU.is_gt),
                    after=[e2, ev_thr])
                ev_tq = S.op("v", lambda e, sl=sl: e.tensor_tensor(
                    wt[:, sl], absw[:], sgnw[:], ALU.mult))
            ev = S.op("v", lambda e: e.tensor_reduce(
                si_sb[:], mi_sb[:], AX.X, ALU.max), after=[ev_cst])
            ev = S.op("v", lambda e: e.tensor_scalar(
                si_sb[:], si_sb[:], 1.0 / 127.0, None, ALU.mult))
            ev_si = S.op("v", lambda e: e.reciprocal(ri_sb[:], si_sb[:]))
            ev_sc = S.op("v", lambda e: e.tensor_tensor(
                sc_[:], wscb[:], si_sb[:], ALU.mult), after=[ev_wsc, ev_si])

            ev_use = {0: None, 1: None}
            ev_evac = {0: None, 1: None}
            ev_ost = {0: None, 1: None}
            ev_ssq = None
            ev_srd = None
            for ti in range(8):
                bi = ti % 2
                c0 = ti * 512
                el = S.op("y", lambda e, c0=c0, bi=bi: e.dma_start(
                    out=ap3(xlb[bi], 8 * 512, 128, (512, 8), (1, 512)),
                    in_=bass.AP(xlt, c0, [[TPC, 128], [128 * TPC, 8],
                                          [1, 512]])),
                    after=[ev_use[bi]], dma=True, ch=f"lx{bi}")
                elx = S.op("y", lambda e, c0=c0, bi=bi: e.dma_start(
                    out=ap3(xf[bi], 8 * 512, 128, (512, 8), (1, 512)),
                    in_=bass.AP(x3t, c0, [[TPC, 128], [128 * TPC, 8],
                                          [1, 512]])),
                    after=[ev_ost[bi]], dma=True, ch=f"l3{bi}")
                e1 = S.op("v", lambda e, bi=bi: e.tensor_scalar(
                    tq[:], xlb[bi][:], ri_sb[:], None, ALU.mult),
                    after=[el, ev_si, ev_g])
                e2 = e1
                for hh in range(2):
                    sl = slice(hh * 2048, (hh + 1) * 2048)
                    e2 = emit_round(S, tq[:, sl], tq[:, sl], g_sb, ta[:],
                                    tb[:], after=[e2])
                e3 = S.op("v", lambda e: e.tensor_scalar(
                    xq[:], tq[:], 127.0, -128.0, ALU.min, ALU.max),
                    after=[e2])
                ev_use[bi] = e3
                ev_ev = []
                for m in range(8):
                    pbank = pA if m % 2 == 0 else pB
                    evk = None
                    for k in range(8):
                        aft = ([e3, ev_tq, ev_evac[m % 2]] if k == 0 else [])
                        evk = S.op("t", lambda e, k=k, m=m, pbank=pbank:
                                   e.matmul(
                            pbank[:, 0:512],
                            wt[:, (k * 8 + m) * 128:(k * 8 + m + 1) * 128],
                            xq[:, k * 512:(k + 1) * 512],
                            start=(k == 0), stop=(k == 7)), after=aft)
                    ev_evac[m % 2] = S.op("s", lambda e, m=m, pbank=pbank:
                                          e.activation(
                        yb[:, m * 512:(m + 1) * 512], pbank[:, 0:512],
                        AF.Identity, bias=bias_sb[:, m:m + 1],
                        scale=sc_[:]), after=[evk, ev_sc, ev_b])
                    ev_ev.append(ev_evac[m % 2])
                evc = S.op("v", lambda e: e.tensor_copy(ybf[:], yb[:]),
                           after=ev_ev)
                evq = S.op("s", lambda e: e.activation(
                    ysq[:], ybf[:], AF.Square), after=[evc, ev_ssq])
                evm = None
                for k in range(8):
                    aft = [evc, ev_cstg2, ev_srd] if k == 0 else []
                    evm = S.op("t", lambda e, k=k: e.matmul(
                        pS1[:, 0:512], ones_sb[:],
                        ybf[:, k * 512:(k + 1) * 512],
                        start=(k == 0), stop=(k == 7)), after=aft)
                for k in range(8):
                    aft = [evq] if k == 0 else []
                    evm = S.op("t", lambda e, k=k: e.matmul(
                        pS2[:, 0:512], ones_sb[:],
                        ysq[:, k * 512:(k + 1) * 512],
                        start=(k == 0), stop=(k == 7)), after=aft)
                ev_ssq = evm
                e1 = S.op("v", lambda e: e.tensor_scalar(
                    mu_[:], pS1[:, 0:512], 1.0 / D, None, ALU.mult),
                    after=[evm])
                e2 = S.op("v", lambda e: e.tensor_scalar(
                    ts1[:], pS2[:, 0:512], 1.0 / D, 1e-5, ALU.mult, ALU.add))
                e3 = S.op("v", lambda e: e.tensor_tensor(
                    ts2[:], mu_[:], mu_[:], ALU.mult))
                e4 = S.op("v", lambda e: e.tensor_tensor(
                    ts1[:], ts1[:], ts2[:], ALU.subtract))
                ev_srd = e2
                e5 = S.op("s", lambda e: e.activation(
                    ts2[:], ts1[:], AF.Sqrt), after=[e4])
                e6 = S.op("v", lambda e: e.reciprocal(rb_[:], ts2[:]),
                          after=[e5])
                e7 = S.op("v", lambda e: e.tensor_tensor(
                    yb[:], yb[:], bass.AP(mu_, 0, [[512, 128], [0, 8],
                                                   [1, 512]]),
                    ALU.subtract), after=[e6])
                e8 = S.op("v", lambda e: e.tensor_tensor(
                    yb[:], yb[:], bass.AP(rb_, 0, [[512, 128], [0, 8],
                                                   [1, 512]]), ALU.mult))
                e9 = S.op("v", lambda e, bi=bi: e.tensor_tensor(
                    xf[bi][:], xf[bi][:], yb[:], ALU.add), after=[e8, elx])
                ev_ost[bi] = S.op("g", lambda e, c0=c0, bi=bi: e.dma_start(
                    out=bass.AP(out_t, c0, [[TPC, 128], [128 * TPC, 8],
                                            [1, 512]]),
                    in_=ap3(xf[bi], 8 * 512, 128, (512, 8), (1, 512))),
                    after=[e9], dma=True, ch=f"so{bi}")
            S.build()
    return nc



def stage_x(x):
    xs = []
    xT = np.ascontiguousarray(np.transpose(x, (0, 2, 1)))
    for c in range(NC):
        b, t0 = c // 4, (c % 4) * TPC
        buf = np.zeros((D, W), np.float32)
        lo, hi = t0 - HALO, t0 + TPC + HALO
        slo, shi = max(lo, 0), min(hi, L)
        buf[:, slo - lo:W - (hi - shi)] = xT[b, :, slo:shi]
        xs.append(buf)
    return xs


def in_maps_A(inputs):
    x = np.asarray(inputs["L0_attn_x"], np.float32)
    win = np.asarray(inputs["L0_attn_in_proj_w"], np.float32)
    binp = np.asarray(inputs["L0_attn_in_proj_b"], np.float32)
    cidx = np.arange(D)
    winv = np.ascontiguousarray(win[:, 3 * cidx])
    winx2 = np.ascontiguousarray(win[:, 3 * cidx + 2])
    def perm8(a):
        a = a.reshape(8, 128, -1)
        return np.ascontiguousarray(a.transpose(1, 0, 2).reshape(128, -1))
    bv = perm8(binp[3 * cidx].reshape(D, 1))
    bx2 = perm8(binp[3 * cidx + 2].reshape(D, 1))
    swc = perm8(np.ascontiguousarray(
        np.asarray(inputs["L0_attn_short_w"], np.float32)[:, 0, :]))
    sbc = perm8(np.asarray(inputs["L0_attn_short_b"], np.float32).reshape(D, 1))
    wg = np.ascontiguousarray(np.asarray(inputs["L0_ffn_w_gate"], np.float32))
    wu = np.ascontiguousarray(np.asarray(inputs["L0_ffn_w_up"], np.float32))
    wd = np.ascontiguousarray(np.asarray(inputs["L0_ffn_w_down"], np.float32))
    one = np.ones((128, 1), np.float32)
    xs = stage_x(x)
    maps = []
    for c in range(NC):
        t0 = (c % 4) * TPC
        edges = np.zeros((128, 16), np.float32)
        edges[:, 0::2] = 1.0 if t0 > 0 else 0.0
        edges[:, 1::2] = 1.0 if t0 + TPC < L else 0.0
        maps.append({
            "xt": xs[c], "winv": winv, "winx2": winx2, "bv": bv, "bx2": bx2,
            "swc": swc, "sbc": sbc, "edges": edges, "one128": one,
            "wg_s": np.ascontiguousarray(wg[c * (IH // NC):(c + 1) * (IH // NC)]),
            "wu_s": np.ascontiguousarray(wu[c * (IH // NC):(c + 1) * (IH // NC)]),
            "wd_s": np.ascontiguousarray(wd[c * (D // NC):(c + 1) * (D // NC)]),
        })
    return maps


def in_maps_C(inputs, a_out, b_out):
    x = np.asarray(inputs["L0_attn_x"], np.float32)
    xs = stage_x(x)
    wo = np.ascontiguousarray(np.asarray(inputs["L0_attn_out_w"], np.float32))
    def perm8(a):
        a = a.reshape(8, 128, -1)
        return np.ascontiguousarray(a.transpose(1, 0, 2).reshape(128, -1))
    bo = perm8(np.asarray(inputs["L0_attn_out_b"], np.float32).reshape(D, 1))
    idm = np.eye(128, dtype=np.float32)
    maps = []
    for c in range(NC):
        b_, t0 = c // 4, (c % 4) * TPC
        cvt = np.zeros((D, W), ml_dtypes.bfloat16)
        lo, hi = t0 - HALO, t0 + TPC + HALO
        slo, shi = max(lo, 0), min(hi, L)
        for s in range(NC):
            cvt[s * 128:(s + 1) * 128, slo - lo:W - (hi - shi)] = \
                b_out[s]["conv_out"][:, b_ * L + slo:b_ * L + shi]
        maps.append({"xt": xs[c], "x2t": a_out[c]["x2_out"], "cvt": cvt,
                     "wout": wo, "bout": bo, "id_h": idm})
    return maps


def _repl(vals):
    v = np.asarray(vals, np.float32).reshape(1, 8)
    return np.ascontiguousarray(np.broadcast_to(v, (128, 8)))


def in_maps_D(inputs, a_out, c_out):
    wgq = np.concatenate([a_out[c]["wgq"] for c in range(NC)], axis=0)
    wuq = np.concatenate([a_out[c]["wuq"] for c in range(NC)], axis=0)
    wgT = np.ascontiguousarray(wgq.T)
    wuT = np.ascontiguousarray(wuq.T)
    m1 = _repl([c_out[c]["m1_out"][0, 0] for c in range(NC)])
    one = np.ones((128, 1), np.float32)
    idm = np.eye(128, dtype=np.float32)
    return [{"xlt": c_out[c]["xln2_out"], "m1r": m1, "wgT": wgT, "wuT": wuT,
             "one128": one, "id_h": idm} for c in range(NC)]


def in_maps_E(inputs, d_out):
    mg = _repl([d_out[c]["mg_out"][0, 0] for c in range(NC)])
    mu = _repl([d_out[c]["mu_out"][0, 0] for c in range(NC)])
    one = np.ones((128, 1), np.float32)
    idm = np.eye(128, dtype=np.float32)
    return [{"g_t": d_out[c]["g_out"], "u_t": d_out[c]["u_out"], "mgr": mg,
             "mur": mu, "one128": one, "id_h": idm} for c in range(NC)]


def in_maps_F(inputs, d_out, e_out):
    m4 = _repl([e_out[c]["m4_out"][0, 0] for c in range(NC)])
    mg = _repl([d_out[c]["mg_out"][0, 0] for c in range(NC)])
    one = np.ones((128, 1), np.float32)
    idm = np.eye(128, dtype=np.float32)
    return [{"t0_t": e_out[c]["t0_out"], "uq_t": e_out[c]["uq_out"],
             "m4r": m4, "mgr": mg, "one128": one, "id_h": idm}
            for c in range(NC)]


def in_maps_G(inputs, a_out, c_out, e_out, f_out):
    wdq = np.concatenate([a_out[c]["wdq"] for c in range(NC)], axis=0)
    wdT = np.ascontiguousarray(wdq.T)
    m5 = _repl([f_out[c]["m5_out"][0, 0] for c in range(NC)])
    wqkv = np.ascontiguousarray(np.asarray(inputs["L1_attn_w_qkv"],
                                           np.float32))
    wo2 = np.ascontiguousarray(np.asarray(inputs["L1_attn_w_out"],
                                          np.float32))
    def perm8(a):
        a = a.reshape(8, 128, -1)
        return np.ascontiguousarray(a.transpose(1, 0, 2).reshape(128, -1))
    bo2 = perm8(np.asarray(inputs["L1_attn_b_out"], np.float32).reshape(D, 1))
    one = np.ones((128, 1), np.float32)
    idm = np.eye(128, dtype=np.float32)
    selh = np.zeros((128, 128), np.float32)
    for ci in range(8):
        for p in range(128):
            selh[p, 16 * ci + 2 * ci + p // 64] = 1.0
    bselh = np.zeros((16, 1024), np.float32)
    for ci in range(8):
        for p in range(128):
            bselh[2 * ci + p // 64, 128 * ci + p] = 1.0
    maps = []
    for c in range(NC):
        b_, t0 = c // 4, (c % 4) * TPC
        mA = np.ones((16, W), np.float32)
        mB = np.ones((16, W), np.float32)
        for j in range(W):
            t = t0 + j - HALO
            if t >= L - 2:
                mA[:, j] = 0.0
            if t < 2:
                mB[:, j] = 0.0
        edgea = np.zeros((128, 2), np.float32)
        edgea[:, 0] = 1.0 if t0 == 0 else 0.0
        edgea[:, 1] = 1.0 if t0 + TPC == L else 0.0
        maps.append({
            "u0_t": f_out[c]["u0_out"], "x1t": c_out[c]["x1_out"],
            "m5r": m5, "wdT": wdT, "wqkv": wqkv, "wo2": wo2, "bo2": bo2,
            "selh": selh, "bselh": bselh,
            "maskA": mA.astype(ml_dtypes.bfloat16),
            "maskB": mB.astype(ml_dtypes.bfloat16),
            "edgea": edgea, "one128": one, "id_h": idm})
    return maps


def in_maps_H(inputs, g_out):
    mi = _repl([g_out[c]["mi_out"][0, 0] for c in range(NC)])
    l1w = np.ascontiguousarray(np.asarray(inputs["L1_ffn_weight"],
                                          np.float32).T)
    def perm8(a):
        a = a.reshape(8, 128, -1)
        return np.ascontiguousarray(a.transpose(1, 0, 2).reshape(128, -1))
    l1b = perm8(np.asarray(inputs["L1_ffn_bias"], np.float32).reshape(D, 1))
    one = np.ones((128, 1), np.float32)
    return [{"xlt": g_out[c]["xln5_out"], "x3t": g_out[c]["x3_out"],
             "mir": mi, "l1w": l1w, "l1b": l1b, "one128": one}
            for c in range(NC)]


_NCS = {}


def _get(name, builder):
    if name not in _NCS:
        _NCS[name] = builder()
    return _NCS[name]


def kernel(**inputs):
    import ml_dtypes
    cores = list(range(NC))
    ncA = _get("A", build_A)
    a_out = run_bass_kernel_spmd(ncA, in_maps_A(inputs), core_ids=cores).results
    ncB = _get("B", build_B)
    v_relay = [a_out[c]["v_out"] for c in range(NC)]
    b_out = run_bass_kernel_spmd(ncB, in_maps_B(inputs, v_relay),
                                 core_ids=cores).results
    ncC = _get("C", build_C)
    c_out = run_bass_kernel_spmd(ncC, in_maps_C(inputs, a_out, b_out),
                                 core_ids=cores).results
    ncD = _get("D", build_D)
    d_out = run_bass_kernel_spmd(ncD, in_maps_D(inputs, a_out, c_out),
                                 core_ids=cores).results
    ncE = _get("E", build_E)
    e_out = run_bass_kernel_spmd(ncE, in_maps_E(inputs, d_out),
                                 core_ids=cores).results
    ncF = _get("F", build_F)
    f_out = run_bass_kernel_spmd(ncF, in_maps_F(inputs, d_out, e_out),
                                 core_ids=cores).results
    ncG = _get("G", build_G)
    g_out = run_bass_kernel_spmd(ncG, in_maps_G(inputs, a_out, c_out, e_out,
                                                f_out),
                                 core_ids=cores).results
    ncH = _get("H", build_H)
    h_out = run_bass_kernel_spmd(ncH, in_maps_H(inputs, g_out),
                                 core_ids=cores).results
    out = np.empty((2, L, D), np.float32)
    for c in range(NC):
        b_, t0 = c // 4, (c % 4) * TPC
        out[b_, t0:t0 + TPC, :] = np.asarray(h_out[c]["out_t"],
                                             np.float32).T
    return out
